# version 24
# baseline (speedup 1.0000x reference)
import numpy as np
import ml_dtypes
import concourse.bass as bass
import concourse.mybir as mybir

F32 = mybir.dt.float32
BF16 = mybir.dt.bfloat16
I32 = mybir.dt.int32
AF = mybir.ActivationFunctionType
ALU = mybir.AluOpType
AX = mybir.AxisListType

NB = 4
L = 2048
D = 1024
H = 16
DH = 64
NF = 1025
HID = 3072
SC = float(1.0 / 127.0)
C1 = float(1.0 / (np.sqrt(DH) + 1e-8))
ENGS = ("t", "v", "s", "q", "g", "sd")
STREAM_OF = {"sd": "s"}


class Sched:

    def __init__(self):
        self.ops = {e: [] for e in ENGS}
        self.cnt = {e: 0 for e in ENGS}
        self.seen = {e: {p: 0 for p in ENGS} for e in ENGS}
        self.snap = {e: {0: {p: 0 for p in ENGS}} for e in ENGS}
        self.resw = {}
        self.resr = {}
        self.pend = {e: [] for e in ENGS}
        self.pend_f = {e: [] for e in ENGS}
        self.streams = {e: [] for e in ENGS if e not in STREAM_OF}
        self.nops = 0

    def _push(self, e, entry):
        self.ops[e].append(entry)
        self.streams[STREAM_OF.get(e, e)].append((e, entry))

    def emit_fence(self, e):
        bnd = self.cnt[e]
        self._push(e, (None, [(e, bnd)], False, 0, "dmafence"))
        self.seen[e][e] = bnd
        clk = dict(self.seen[e])
        clk[e] = bnd
        self.snap[e][bnd] = clk
        for kind, tk in self.pend_f[e]:
            if kind == "w":
                if self.resw.get(tk, (None, 0))[1] is None and self.resw[tk][0] == e:
                    self.resw[tk] = (e, bnd)
            else:
                lst = self.resr.get(tk, [])
                for i, (re_, rv) in enumerate(lst):
                    if re_ == e and rv is None:
                        lst[i] = (e, bnd)
        self.pend_f[e] = []

    def emit(self, e, fn, reads=(), writes=(), inc=True, dma=False, fence=True, tag=""):
        waits = {}

        def need(pe, pv):
            if pv is None:
                assert pe == e, f"cross-engine dep on un-inc'd op, tag={tag}"
                return
            if self.seen[e].get(pe, 0) < pv:
                waits[pe] = max(waits.get(pe, 0), pv)

        for tk in reads:
            w = self.resw.get(tk)
            if w is not None:
                need(w[0], w[1])
        for tk in writes:
            w = self.resw.get(tk)
            if w is not None:
                need(w[0], w[1])
            for (re_, rv) in self.resr.get(tk, ()):
                need(re_, rv)
        for pe in list(waits):
            pv = waits[pe]
            self.seen[e][pe] = max(self.seen[e][pe], pv)
            sn = self.snap[pe].get(pv)
            if sn:
                for k2, v2 in sn.items():
                    if self.seen[e].get(k2, 0) < v2:
                        self.seen[e][k2] = v2
        amt = 16 if dma else 1
        self._push(e, (fn, sorted(waits.items()), inc, amt, tag))
        self.nops += 1
        if inc:
            self.cnt[e] += amt
            nc_ = self.cnt[e]
            for kind, tk in self.pend[e]:
                if kind == "w":
                    if self.resw.get(tk, (None, 0))[1] is None and self.resw[tk][0] == e:
                        self.resw[tk] = (e, nc_)
                else:
                    lst = self.resr.get(tk, [])
                    for i, (re_, rv) in enumerate(lst):
                        if re_ == e and rv is None:
                            lst[i] = (e, nc_)
            self.pend[e] = []
            clk = dict(self.seen[e])
            clk[e] = nc_
            self.snap[e][nc_] = clk
        for tk in writes:
            if inc:
                self.resw[tk] = (e, self.cnt[e])
            else:
                self.resw[tk] = (e, None)
                self.pend[e].append(("w", tk))
            self.resr[tk] = []
        for tk in reads:
            lst = self.resr.setdefault(tk, [])
            if inc:
                lst.append((e, self.cnt[e]))
            else:
                lst.append((e, None))
                self.pend[e].append(("r", tk))
        if dma and inc:
            if fence:
                self._push(e, (None, [(e, self.cnt[e])], False, 0, "dmafence"))
                self.seen[e][e] = self.cnt[e]
                bnd = self.cnt[e]
                for kind, tk in self.pend_f[e]:
                    if kind == "w":
                        if self.resw.get(tk, (None, 0))[1] is None and self.resw[tk][0] == e:
                            self.resw[tk] = (e, bnd)
                    else:
                        lst = self.resr.get(tk, [])
                        for i, (re_, rv) in enumerate(lst):
                            if re_ == e and rv is None:
                                lst[i] = (e, bnd)
                self.pend_f[e] = []
            else:
                for tk in writes:
                    self.resw[tk] = (e, None)
                    self.pend_f[e].append(("w", tk))
                for tk in reads:
                    lst = self.resr.setdefault(tk, [])
                    for i in range(len(lst) - 1, -1, -1):
                        if lst[i] == (e, self.cnt[e]):
                            lst[i] = (e, None)
                            break
                    self.pend_f[e].append(("r", tk))


def gr(reg, lo, hi):
    return [f"{reg}g{i}" for i in range(lo // 1024, (hi + 1023) // 1024)]


def build_consts():
    n = np.arange(L)
    k = np.arange(D)
    ph = 2 * np.pi * np.outer(n, k) / L
    dft_cos = np.cos(ph).astype(ml_dtypes.bfloat16)
    dft_sinn = (-np.sin(ph)).astype(ml_dtypes.bfloat16)
    w = np.full(D, 2.0, np.float64)
    w[0] = 1.0
    phT = 2 * np.pi * np.outer(np.arange(D), n) / L
    crt = ((w[:, None] / L) * np.cos(phT)).astype(ml_dtypes.bfloat16)
    cit = (-(w[:, None] / L) * np.sin(phT)).astype(ml_dtypes.bfloat16)
    crt_nyq = ((1.0 / L) * np.where(n % 2 == 0, 1.0, -1.0)).astype(ml_dtypes.bfloat16)[None, :]
    altcol = np.where(np.arange(128) % 2 == 0, 1.0, -1.0).astype(ml_dtypes.bfloat16)[:, None]
    eexp = np.zeros((H, D), ml_dtypes.bfloat16)
    for h in range(H):
        eexp[h, h * DH:(h + 1) * DH] = 1.0
    identb = np.eye(128, dtype=ml_dtypes.bfloat16)
    identf = np.eye(128, dtype=np.float32)
    return dict(dft_cos=dft_cos, dft_sinn=dft_sinn, crt=crt, cit=cit,
                crt_nyq=crt_nyq, altcol=altcol, eexp=eexp,
                identb=identb, identf=identf)


def build_program(debug=False):
    nc = bass.Bass("TRN2", target_bir_lowering=False, debug=False)
    S = Sched()
    em = S.emit

    x_in = nc.dram_tensor("x", [NB, L, D], F32, kind="ExternalInput").ap()
    W_ins = {nm: nc.dram_tensor(nm, [D, D], F32, kind="ExternalInput").ap()
             for nm in ("Wq", "Wk", "Wv", "Wo")}
    b_ins = {nm: nc.dram_tensor(nm, [1, D], F32, kind="ExternalInput").ap()
             for nm in ("bq", "bk", "bv", "bo")}
    wg_in = nc.dram_tensor("wg", [D, HID], I32, kind="ExternalInput").ap()
    wu_in = nc.dram_tensor("wu", [D, HID], I32, kind="ExternalInput").ap()
    wd_in = nc.dram_tensor("wd", [HID, D], I32, kind="ExternalInput").ap()
    c_cos = nc.dram_tensor("dft_cos", [L, D], BF16, kind="ExternalInput").ap()
    c_sinn = nc.dram_tensor("dft_sinn", [L, D], BF16, kind="ExternalInput").ap()
    c_crt = nc.dram_tensor("crt", [D, L], BF16, kind="ExternalInput").ap()
    c_cit = nc.dram_tensor("cit", [D, L], BF16, kind="ExternalInput").ap()
    c_crtnyq = nc.dram_tensor("crt_nyq", [1, L], BF16, kind="ExternalInput").ap()
    c_altcol = nc.dram_tensor("altcol", [128, 1], BF16, kind="ExternalInput").ap()
    c_eexp = nc.dram_tensor("eexp", [H, D], BF16, kind="ExternalInput").ap()
    c_identb = nc.dram_tensor("identb", [128, 128], BF16, kind="ExternalInput").ap()
    c_identf = nc.dram_tensor("identf", [128, 128], F32, kind="ExternalInput").ap()
    out_d = nc.dram_tensor("out", [NB, L, D], F32, kind="ExternalOutput").ap()

    w_scr = {nm: nc.dram_tensor(nm.lower() + "_s", [D, D], BF16, kind="Internal").ap()
             for nm in ("Wq", "Wk", "Wv", "Wo")}
    wgT_s = nc.dram_tensor("wgT_s", [D, HID], BF16, kind="Internal").ap()
    wuT_s = nc.dram_tensor("wuT_s", [D, HID], BF16, kind="Internal").ap()
    wdT_s = nc.dram_tensor("wdT_s", [HID, D], BF16, kind="Internal").ap()
    attn_scr = nc.dram_tensor("attn_scr", [L, D], BF16, kind="Internal").ap()

    dbg = {}
    if debug:
        for nm, shp, dt in [("d_xnT", [128, 8, L], BF16), ("d_qr", [128, 8, D], BF16),
                            ("d_qi", [128, 8, D], BF16), ("d_kr", [128, 8, D], BF16),
                            ("d_vr", [128, 8, D], BF16), ("d_eT", [H, NF], F32),
                            ("d_scT", [H, NF], F32), ("d_attnT", [H, NF], BF16),
                            ("d_um", [H, NF], BF16), ("d_xre", [128, 8, D], BF16),
                            ("d_xim", [128, 8, D], BF16), ("d_outT", [128, 8, L], BF16),
                            ("d_attn", [L, D], BF16), ("d_qn", [1, D], BF16),
                            ("d_qmr", [1, D], BF16), ("d_qmi", [1, D], BF16),
                            ("d_sig", [128, 16, D], BF16), ("d_h", [128, 8, L], BF16)]:
            dbg[nm] = nc.dram_tensor(nm, shp, dt, kind="ExternalOutput").ap()

    from contextlib import ExitStack
    stack = ExitStack()
    E = stack.enter_context

    r1 = E(nc.sbuf_tensor("r1", [128, 16384], BF16))
    r2 = E(nc.sbuf_tensor("r2", [128, 16384], BF16))
    r3 = E(nc.sbuf_tensor("r3", [128, 8192], BF16))
    r4 = E(nc.sbuf_tensor("r4", [128, 8192], BF16))
    r5 = E(nc.sbuf_tensor("r5", [128, 8192], BF16))
    r6 = E(nc.sbuf_tensor("r6", [128, 8192], BF16))
    strm = E(nc.sbuf_tensor("strm", [128, 8192], BF16))
    spA = E(nc.sbuf_tensor("spA", [128, 1040], F32))
    spB = E(nc.sbuf_tensor("spB", [128, 1040], F32))
    spC = E(nc.sbuf_tensor("spC", [128, 1040], F32))
    spD = E(nc.sbuf_tensor("spD", [128, 1040], F32))
    spE = E(nc.sbuf_tensor("spE", [128, 2048], BF16))
    e_fh = E(nc.sbuf_tensor("e_fh", [128, 8, 16], F32))
    s_fh = E(nc.sbuf_tensor("s_fh", [128, 8, 16], F32))
    max8 = E(nc.sbuf_tensor("max8", [16, 8], F32))
    enrow = E(nc.sbuf_tensor("enrow", [1, 16], F32))
    snrow = E(nc.sbuf_tensor("snrow", [1, 16], F32))
    mxx = E(nc.sbuf_tensor("mxx", [16, 1], F32))
    negmx = E(nc.sbuf_tensor("negmx", [16, 1], F32))
    smr = E(nc.sbuf_tensor("smr", [16, 1], F32))
    rsm = E(nc.sbuf_tensor("rsm", [16, 1], F32))
    msum = [E(nc.sbuf_tensor(f"msum{i}", [128, 1], F32)) for i in range(4)]
    ssq = [E(nc.sbuf_tensor(f"ssq{i}", [128, 1], F32)) for i in range(4)]
    meanv = [E(nc.sbuf_tensor(f"meanv{i}", [128, 1], F32)) for i in range(4)]
    m2v = [E(nc.sbuf_tensor(f"m2v{i}", [128, 1], F32)) for i in range(4)]
    varv = [E(nc.sbuf_tensor(f"varv{i}", [128, 1], F32)) for i in range(4)]
    sdv = [E(nc.sbuf_tensor(f"sdv{i}", [128, 1], F32)) for i in range(4)]
    rstd = [E(nc.sbuf_tensor(f"rstd{i}", [128, 1], F32)) for i in range(4)]
    epsv = E(nc.sbuf_tensor("epsv", [128, 1], F32))
    tmpA = E(nc.sbuf_tensor("tmpA", [128, 512], F32))
    tmpB = E(nc.sbuf_tensor("tmpB", [128, 512], F32))
    tmpC = E(nc.sbuf_tensor("tmpC", [1, 1024], F32))
    qmr = E(nc.sbuf_tensor("qmr", [1, 1024], BF16))
    qmi = E(nc.sbuf_tensor("qmi", [1, 1024], BF16))
    qtr = E(nc.sbuf_tensor("qtr", [16, 1024], BF16))
    qti = E(nc.sbuf_tensor("qti", [16, 1024], BF16))
    umF = E(nc.sbuf_tensor("umF", [128, 8, 16], BF16))
    qtiT = E(nc.sbuf_tensor("qtiT", [128, 8, 16], BF16))
    eexp_t = E(nc.sbuf_tensor("eexp_t", [16, 1024], BF16))
    identb_t = E(nc.sbuf_tensor("identb_t", [128, 128], BF16))
    identf_t = E(nc.sbuf_tensor("identf_t", [128, 128], F32))
    altcol_t = E(nc.sbuf_tensor("altcol_t", [128, 1], BF16))
    onesrow = E(nc.sbuf_tensor("onesrow", [1, 128], BF16))
    onescol = E(nc.sbuf_tensor("onescol", [128, 1], BF16))
    bias_t = {nm: E(nc.sbuf_tensor(nm + "_t", [1, 1024], BF16))
              for nm in ("bq", "bk", "bv", "bo")}
    nyq_qr = E(nc.sbuf_tensor("nyq_qr", [1, 1024], BF16))
    nyq_kr = E(nc.sbuf_tensor("nyq_kr", [1, 1024], BF16))
    nyq_vr = E(nc.sbuf_tensor("nyq_vr", [1, 1024], BF16))
    nyq_xre = E(nc.sbuf_tensor("nyq_xre", [1, 1024], BF16))
    crtn_t = E(nc.sbuf_tensor("crtn_t", [1, 1024], BF16))
    pw = [E(nc.psum_tensor(f"pw{i}", [128, 1024], F32)) for i in range(4)]

    semT = E(nc.semaphore("semT"))
    semV = E(nc.semaphore("semV"))
    semS = E(nc.semaphore("semS"))
    semQ = E(nc.semaphore("semQ"))
    semG = E(nc.semaphore("semG"))
    semSD = E(nc.semaphore("semSD"))
    SEMS = {"t": semT, "v": semV, "s": semS, "q": semQ, "g": semG, "sd": semSD}
    block = E(nc.Block())

    def rv(t, a, b):
        return t[:, 0:a * b].rearrange("p (a b) -> p a b", a=a)

    xnT = rv(r1, 8, 2048)
    sig = rv(r2, 16, 1024)
    crci = rv(r2, 16, 1024)
    qr_v = rv(r3, 8, 1024)
    qi_v = rv(r4, 8, 1024)
    kr_v = rv(r5, 8, 1024)
    vr_v = rv(r6, 8, 1024)
    xre_v = qr_v
    xim_v = qi_v
    cmS = r4[0:16, 0:2048]
    wqim = r4[0:16, 2048:3072]
    outT_lo = rv(r5, 4, 2048)
    outT_hi = rv(r6, 4, 2048)

    def outT(chi):
        return outT_lo[:, chi, :] if chi < 4 else outT_hi[:, chi - 4, :]

    def outT_tok(chi, lo=0, hi=2048):
        reg = "r5" if chi < 4 else "r6"
        base = (chi % 4) * 2048
        return gr(reg, base + lo, base + hi)

    hTv = [rv(r2, 8, 2048), rv(r3, 4, 2048), rv(r4, 4, 2048),
           rv(r5, 4, 2048), rv(r6, 4, 2048)]

    def hT(hc):
        if hc < 8:
            return hTv[0][:, hc, :]
        return hTv[1 + (hc - 8) // 4][:, (hc - 8) % 4, :]

    def hT_tok(hc, lo=0, hi=2048):
        reg = "r2" if hc < 8 else ("r3", "r4", "r5", "r6")[(hc - 8) // 4]
        base = (hc * 2048 if hc < 8 else ((hc - 8) % 4) * 2048)
        return gr(reg, base + lo, base + hi)

    wdres_a = rv(r1, 16, 1024)
    wdres_b = rv(strm, 8, 1024)

    def wdres(hc):
        return wdres_a[:, hc, :] if hc < 16 else wdres_b[:, hc - 16, :]

    def wdres_tok(hc):
        return gr("r1", hc * 1024, (hc + 1) * 1024) if hc < 16 else \
            gr("st", (hc - 16) * 1024, (hc - 15) * 1024)

    wbuf = rv(strm, 8, 1024)
    cosslot = rv(strm, 32, 128)
    sinslot4 = rv(strm, 64, 128)
    wgslot = rv(strm, 8, 1024)
    siluslot = rv(strm, 8, 1024)

    eT = spA[0:16, 0:1025]
    scT = spB[0:16, 0:1025]
    attnT = spC[:].bitcast(BF16)[0:16, 0:1025]
    um = spD[:].bitcast(BF16)[0:16, 0:1025]
    xstage = [spA[:, 0:1024], spB[:, 0:1024]]
    ostage = [spC[:, 0:1024], spD[:, 0:1024]]
    istage = [spA[:].bitcast(I32)[:, 0:1024], spB[:].bitcast(I32)[:, 0:1024]]
    bstage = [spC[:].bitcast(BF16)[:, 0:2048], spD[:].bitcast(BF16)[:, 0:2048]]
    lnx = [r5[:].bitcast(F32)[:, i * 1024:(i + 1) * 1024] for i in range(4)]
    LNX_TK = [gr("r5", i * 2048, (i + 1) * 2048) for i in range(4)]
    lnxn = [r6[:, 0:1024], r6[:, 1024:2048], r6[:, 6144:7168], r6[:, 7168:8192]]
    LNXN_TK = [gr("r6", 0, 1024), gr("r6", 1024, 2048),
               gr("r6", 6144, 7168), gr("r6", 7168, 8192)]
    sqdump = [r6[:].bitcast(F32)[:, 1024:2048], r6[:].bitcast(F32)[:, 2048:3072]]
    SQD_TK = [gr("r6", 2048, 4096), gr("r6", 4096, 6144)]
    pwb = [p[:].bitcast(BF16) for p in pw]

    em("q", lambda e: e.dma_start(out=identb_t[:], in_=c_identb), writes=["identb"], dma=True)
    em("q", lambda e: e.dma_start(out=identf_t[:], in_=c_identf), writes=["identf"], dma=True)
    em("q", lambda e: e.dma_start(out=altcol_t[:], in_=c_altcol), writes=["altcol"], dma=True)
    em("q", lambda e: e.dma_start(out=eexp_t[:], in_=c_eexp), writes=["eexp"], dma=True)
    em("v", lambda e: e.memset(onesrow[:], 1.0), writes=["onesrow"])
    em("v", lambda e: e.memset(epsv[:], 1e-5), writes=["epsv"])
    em("v", lambda e: e.memset(onescol[:], 1.0), writes=["onescol"])
    for nm in ("bq", "bk", "bv", "bo"):
        em("q", lambda e, nm=nm: e.dma_start(out=lnx[0][0:1, :], in_=b_ins[nm]),
           writes=LNX_TK[0], dma=True)
        em("v", lambda e, nm=nm: e.tensor_copy(bias_t[nm][:], lnx[0][0:1, :]),
           reads=LNX_TK[0], writes=[nm])
    sp4 = [spA[:, 0:1024], spB[:, 0:1024], spC[:, 0:1024], spD[:, 0:1024]]
    sp4i = [t.bitcast(I32) for t in sp4]
    SP4 = [["spA"], ["spB"], ["spC"], ["spD"]]
    spEh = [spE[:, 0:1024], spE[:, 1024:2048]]
    prep_n = {"i": 0}
    prep_pipe = []

    def prep_one(src_ap, dst_ap, dtk, int32):
        i = prep_n["i"]
        prep_n["i"] += 1
        par, ph = i % 4, i % 2
        it = sp4i[par] if int32 else sp4[par]
        em("q", lambda e, it=it, src_ap=src_ap:
           e.dma_start(out=it, in_=src_ap), writes=SP4[par], dma=True)

        def cast_store(i=i, it=it, par=par, ph=ph, dst_ap=dst_ap, dtk=dtk):
            em("v" if i % 2 == 0 else "s", lambda e, it=it, ph=ph, i=i:
               (e.tensor_copy if i % 2 == 0 else e.copy)(spEh[ph], it),
               reads=SP4[par], writes=[f"spE{ph}"])
            em("q", lambda e, ph=ph, dst_ap=dst_ap: e.dma_start(out=dst_ap, in_=spEh[ph]),
               reads=[f"spE{ph}"], writes=[dtk], dma=True)

        prep_pipe.append(cast_store)
        if len(prep_pipe) > 3:
            prep_pipe.pop(0)()

    def prep_flush():
        while prep_pipe:
            prep_pipe.pop(0)()

    prep_pending = []

    def prep_service(k):
        for _ in range(k):
            if prep_pending:
                prep_one(*prep_pending.pop(0))

    def prep_attn(nm):
        for dc in range(8):
            prep_one(W_ins[nm][dc * 128:(dc + 1) * 128, :],
                     w_scr[nm][dc * 128:(dc + 1) * 128, :], f"scr{nm}", False)
        prep_flush()

    def prep_attn_pend(nm):
        for dc in range(8):
            prep_pending.append((W_ins[nm][dc * 128:(dc + 1) * 128, :],
                                 w_scr[nm][dc * 128:(dc + 1) * 128, :], f"scr{nm}", False))

    def prep_chunk(idx, src_ap, dst_ap, dtk):
        prep_one(src_ap, dst_ap, dtk, True)

    ffn_chunks = []
    for hc in range(24):
        ffn_chunks.append((wd_in[hc * 128:(hc + 1) * 128, :], wdT_s[hc * 128:(hc + 1) * 128, :], "scrwd"))
    prep_state = {"i": 0}

    def prep_some(n):
        i0 = prep_state["i"]
        for i in range(i0, min(i0 + n, len(ffn_chunks))):
            prep_chunk(i, *ffn_chunks[i])
        prep_state["i"] = min(i0 + n, len(ffn_chunks))
        prep_flush()

    def phase_ln(b):
        for tau in range(16):
            par = tau % 4
            pwi = 2 + (tau % 2)
            em("q", lambda e, b=b, tau=tau, par=par:
               e.dma_start(out=lnx[par], in_=x_in[b, tau * 128:(tau + 1) * 128, :]),
               writes=LNX_TK[par], dma=True, tag=f"lnx{tau}")
            em("v", lambda e, par=par: e.reduce_sum(msum[par][:], lnx[par], axis=AX.X),
               reads=LNX_TK[par], writes=[f"msum{par}"])
            em("s", lambda e, par=par: e.activation(sqdump[par % 2], lnx[par], AF.Square,
                                                    accum_out=ssq[par][:]),
               reads=LNX_TK[par], writes=[f"ssq{par}"] + SQD_TK[par % 2])
            em("v", lambda e, par=par: e.tensor_scalar(out=meanv[par][:], in0=msum[par][:],
                                                       scalar1=1.0 / D, scalar2=None, op0=ALU.mult),
               reads=[f"msum{par}"], writes=[f"meanv{par}"])
            em("v", lambda e, par=par: e.tensor_mul(m2v[par][:], meanv[par][:], meanv[par][:]),
               reads=[f"meanv{par}"], writes=[f"m2v{par}"])
            em("v", lambda e, par=par: e.scalar_tensor_tensor(out=varv[par][:], in0=ssq[par][:],
                                                              scalar=1.0 / D, in1=m2v[par][:],
                                                              op0=ALU.mult, op1=ALU.subtract),
               reads=[f"ssq{par}", f"m2v{par}"], writes=[f"varv{par}"])
            em("s", lambda e, par=par: e.activation(sdv[par][:], varv[par][:], AF.Sqrt, bias=epsv[:]),
               reads=[f"varv{par}", "epsv"], writes=[f"sdv{par}"])
            em("v", lambda e, par=par: e.reciprocal(rstd[par][:], sdv[par][:]),
               reads=[f"sdv{par}"], writes=[f"rstd{par}"])
            em("v", lambda e, par=par: e.tensor_scalar(out=lnxn[par], in0=lnx[par],
                                                       scalar1=meanv[par][:], scalar2=rstd[par][:],
                                                       op0=ALU.subtract, op1=ALU.mult),
               reads=LNX_TK[par] + [f"meanv{par}", f"rstd{par}"], writes=LNXN_TK[par])
            for dc in range(8):
                em("t", lambda e, par=par, dc=dc, pwi=pwi:
                   e.transpose(pwb[pwi][:, dc * 128:(dc + 1) * 128],
                               lnxn[par][:, dc * 128:(dc + 1) * 128], identb_t[:]),
                   reads=LNXN_TK[par] + ["identb"], writes=[f"pw{pwi}"], inc=(dc == 7))
            em("s", lambda e, tau=tau, pwi=pwi:
               e.copy(xnT[:, :, tau * 128:(tau + 1) * 128],
                      pwb[pwi][:, 0:1024].rearrange("p (dc c) -> p dc c", dc=8)),
               reads=[f"pw{pwi}"], writes=[f"r1c{tau}"])
        if debug and b == 0:
            em("g", lambda e: e.dma_start(out=dbg["d_xnT"], in_=xnT),
               reads=[f"r1c{t}" for t in range(16)], dma=True)

    def prefetch_wbuf(b, si):
        nm = ("Wq", "Wk", "Wv")[si]
        for dc in range(8):
            em("q", lambda e, nm=nm, dc=dc:
               e.dma_start(out=wbuf[:, dc, :], in_=w_scr[nm][dc * 128:(dc + 1) * 128, :]),
               reads=[f"scr{nm}"], writes=gr("st", dc * 1024, (dc + 1) * 1024), dma=True)

    def phase_proj(b, si):
        bias = bias_t[("bq", "bk", "bv")[si]]
        btk = ("bq", "bk", "bv")[si]
        reg = "st"
        wv = wbuf
        for tau in range(16):
            pwi = tau % 4
            stok = gr("r2", tau * 1024, (tau + 1) * 1024)
            for dc in range(8):
                for eta in range(2):
                    em("t", lambda e, tau=tau, dc=dc, eta=eta, pwi=pwi, wv=wv:
                       e.matmul(pw[pwi][:, eta * 512:(eta + 1) * 512],
                                xnT[:, dc, tau * 128:(tau + 1) * 128],
                                wv[:, dc, eta * 512:(eta + 1) * 512],
                                start=(dc == 0), stop=False),
                       reads=[f"r1c{tau}"] + gr(reg, dc * 1024, (dc + 1) * 1024),
                       writes=[f"pw{pwi}"], inc=False)
            for eta in range(2):
                em("t", lambda e, eta=eta, pwi=pwi, bias=bias:
                   e.matmul(pw[pwi][:, eta * 512:(eta + 1) * 512], onesrow[0:1, :],
                            bias[0:1, eta * 512:(eta + 1) * 512], start=False, stop=True),
                   reads=["onesrow", btk], writes=[f"pw{pwi}"], inc=(eta == 1))
            if tau % 2 == 0:
                em("s", lambda e, tau=tau, pwi=pwi: e.copy(sig[:, tau, :], pw[pwi][:, :]),
                   reads=[f"pw{pwi}"], writes=stok)
            else:
                em("v", lambda e, tau=tau, pwi=pwi: e.tensor_copy(sig[:, tau, :], pw[pwi][:, :]),
                   reads=[f"pw{pwi}"], writes=stok)
            if b == 0 and si >= 1:
                prep_service(1)
        if debug and b == 0 and si == 0:
            em("g", lambda e: e.dma_start(out=dbg["d_sig"], in_=sig),
               reads=gr("r2", 0, 16384), dma=True)

    def phase_ln_proj(b):
        def lndma(tau):
            par = tau % 4
            em("q", lambda e, b=b, tau=tau, par=par:
               e.dma_start(out=lnx[par], in_=x_in[b, tau * 128:(tau + 1) * 128, :]),
               writes=LNX_TK[par], dma=True, tag=f"lnx{tau}")

        def lnchain(tau):
            par = tau % 4
            em("v", lambda e, par=par: e.reduce_sum(msum[par][:], lnx[par], axis=AX.X),
               reads=LNX_TK[par], writes=[f"msum{par}"])
            em("s", lambda e, par=par: e.activation(sqdump[par % 2], lnx[par], AF.Square,
                                                    accum_out=ssq[par][:]),
               reads=LNX_TK[par], writes=[f"ssq{par}"] + SQD_TK[par % 2])
            em("v", lambda e, par=par: e.tensor_scalar(out=meanv[par][:], in0=msum[par][:],
                                                       scalar1=1.0 / D, scalar2=None, op0=ALU.mult),
               reads=[f"msum{par}"], writes=[f"meanv{par}"])
            em("v", lambda e, par=par: e.tensor_mul(m2v[par][:], meanv[par][:], meanv[par][:]),
               reads=[f"meanv{par}"], writes=[f"m2v{par}"])
            em("v", lambda e, par=par: e.scalar_tensor_tensor(out=varv[par][:], in0=ssq[par][:],
                                                              scalar=1.0 / D, in1=m2v[par][:],
                                                              op0=ALU.mult, op1=ALU.subtract),
               reads=[f"ssq{par}", f"m2v{par}"], writes=[f"varv{par}"])
            em("s", lambda e, par=par: e.activation(sdv[par][:], varv[par][:], AF.Sqrt, bias=epsv[:]),
               reads=[f"varv{par}", "epsv"], writes=[f"sdv{par}"])
            em("v", lambda e, par=par: e.reciprocal(rstd[par][:], sdv[par][:]),
               reads=[f"sdv{par}"], writes=[f"rstd{par}"])
            em("v", lambda e, par=par: e.tensor_scalar(out=lnxn[par], in0=lnx[par],
                                                       scalar1=meanv[par][:], scalar2=rstd[par][:],
                                                       op0=ALU.subtract, op1=ALU.mult),
               reads=LNX_TK[par] + [f"meanv{par}", f"rstd{par}"], writes=LNXN_TK[par])

        def transp(tau):
            par = tau % 4
            pwi = 2 + (tau % 2)
            for dc in range(8):
                em("t", lambda e, par=par, dc=dc, pwi=pwi:
                   e.transpose(pwb[pwi][:, dc * 128:(dc + 1) * 128],
                               lnxn[par][:, dc * 128:(dc + 1) * 128], identb_t[:]),
                   reads=LNXN_TK[par] + ["identb"], writes=[f"pw{pwi}"], inc=(dc == 7))
            em("s", lambda e, tau=tau, pwi=pwi:
               e.copy(xnT[:, :, tau * 128:(tau + 1) * 128],
                      pwb[pwi][:, 0:1024].rearrange("p (dc c) -> p dc c", dc=8)),
               reads=[f"pw{pwi}"], writes=[f"r1c{tau}"])

        def projmms(tau):
            ppw = tau % 2
            stok = gr("r2", tau * 1024, (tau + 1) * 1024)
            for dc in range(8):
                for eta in range(2):
                    em("t", lambda e, tau=tau, dc=dc, eta=eta, ppw=ppw:
                       e.matmul(pw[ppw][:, eta * 512:(eta + 1) * 512],
                                xnT[:, dc, tau * 128:(tau + 1) * 128],
                                wbuf[:, dc, eta * 512:(eta + 1) * 512],
                                start=(dc == 0), stop=False),
                       reads=[f"r1c{tau}"] + gr("st", dc * 1024, (dc + 1) * 1024),
                       writes=[f"pw{ppw}"], inc=False)
            for eta in range(2):
                em("t", lambda e, eta=eta, ppw=ppw:
                   e.matmul(pw[ppw][:, eta * 512:(eta + 1) * 512], onesrow[0:1, :],
                            bias_t["bq"][0:1, eta * 512:(eta + 1) * 512], start=False, stop=True),
                   reads=["onesrow", "bq"], writes=[f"pw{ppw}"], inc=(eta == 1))
            em("v", lambda e, tau=tau, ppw=ppw: e.tensor_copy(sig[:, tau, :], pw[ppw][:, :]),
               reads=[f"pw{ppw}"], writes=stok)

        for tau in range(4):
            lndma(tau)
        prefetch_wbuf(b, 0)
        for tau in range(16):
            lnchain(tau)
            if tau + 4 < 16:
                lndma(tau + 4)
            transp(tau)
            if tau >= 1:
                projmms(tau - 1)
        projmms(15)

    def phase_fft(b, si):
        tlist = {0: [("qr", c_cos, 0, qr_v, "r3", 0), ("qi", c_sinn, 1, qi_v, "r4", 2)],
                 1: [("kr", c_cos, 0, kr_v, "r5", 0)],
                 2: [("vr", c_cos, 0, vr_v, "r6", 0)]}[si]
        for phi in range(8):
            par = phi % 2
            for (nm, mat, kind, dstv, reg, pwbase) in tlist:
                slotv = cosslot[:, par * 16:(par + 1) * 16, :] if kind == 0 else \
                    sinslot4[:, 32 + par * 16:32 + (par + 1) * 16, :]
                slot_tk = gr("st", par * 2048, (par + 1) * 2048) if kind == 0 else \
                    gr("st", 4096 + par * 2048, 4096 + (par + 1) * 2048)
                qq = "q"
                em(qq, lambda e, mat=mat, phi=phi, slotv=slotv:
                   e.dma_start(out=slotv, in_=mat[:, phi * 128:(phi + 1) * 128]
                               .rearrange("(tau p) c -> p tau c", p=128)),
                   writes=slot_tk, dma=True, tag=f"fftdma{nm}{phi}")
                pwi = pwbase + par
                for tau in range(16):
                    for eta in range(2):
                        em("t", lambda e, slotv=slotv, tau=tau, eta=eta, pwi=pwi:
                           e.matmul(pw[pwi][:, eta * 512:(eta + 1) * 512],
                                    slotv[:, tau, :],
                                    sig[:, tau, eta * 512:(eta + 1) * 512],
                                    start=(tau == 0), stop=(tau == 15)),
                           reads=slot_tk + gr("r2", tau * 1024, (tau + 1) * 1024),
                           writes=[f"pw{pwi}"], inc=(tau == 15 and eta == 1))
                eng = "s" if kind == 0 else "v"
                em(eng, lambda e, dstv=dstv, phi=phi, pwi=pwi, eng=eng:
                   (e.copy if eng == "s" else e.tensor_copy)(dstv[:, phi, :], pw[pwi][:, :]),
                   reads=[f"pw{pwi}"], writes=gr(reg, phi * 1024, (phi + 1) * 1024))
            if si == 0:
                for eta in range(2):
                    sl = slice(eta * 512, (eta + 1) * 512)
                    qtk = gr("r3", phi * 1024 + eta * 512, phi * 1024 + (eta + 1) * 512)
                    itk = gr("r4", phi * 1024 + eta * 512, phi * 1024 + (eta + 1) * 512)
                    em("v", lambda e, phi=phi, sl=sl: e.tensor_mul(tmpA[:], qr_v[:, phi, sl], qr_v[:, phi, sl]),
                       reads=qtk, writes=["tmpA"])
                    em("v", lambda e, phi=phi, sl=sl: e.tensor_mul(tmpB[:], qi_v[:, phi, sl], qi_v[:, phi, sl]),
                       reads=itk, writes=["tmpB"])
                    em("v", lambda e: e.tensor_add(tmpA[:], tmpA[:], tmpB[:]),
                       reads=["tmpA", "tmpB"], writes=["tmpA"])
                    em("s", lambda e: e.activation(tmpA[:], tmpA[:], AF.Sqrt),
                       reads=["tmpA"], writes=["tmpA"])
                    em("v", lambda e, phi=phi, eta=eta:
                       e.tensor_reduce(out=e_fh[:, phi, eta * 8:(eta + 1) * 8].unsqueeze(2),
                                       in_=tmpA[:].rearrange("p (g d) -> p g d", g=8),
                                       axis=AX.X, op=ALU.add),
                       reads=["tmpA"], writes=["e_fh"])
            if si == 1:
                for eta in range(2):
                    sl = slice(eta * 512, (eta + 1) * 512)
                    qtk = gr("r3", phi * 1024 + eta * 512, phi * 1024 + (eta + 1) * 512)
                    ktk = gr("r5", phi * 1024 + eta * 512, phi * 1024 + (eta + 1) * 512)
                    em("v", lambda e, phi=phi, sl=sl: e.tensor_mul(tmpB[:], qr_v[:, phi, sl], kr_v[:, phi, sl]),
                       reads=qtk + ktk, writes=["tmpB"])
                    em("v", lambda e, phi=phi, eta=eta:
                       e.tensor_reduce(out=s_fh[:, phi, eta * 8:(eta + 1) * 8].unsqueeze(2),
                                       in_=tmpB[:].rearrange("p (g d) -> p g d", g=8),
                                       axis=AX.X, op=ALU.add),
                       reads=["tmpB"], writes=["s_fh"])
            if b == 0:
                prep_service(2 if si == 0 else 1)
        nyq_dst, nyq_tk = [(nyq_qr, "nyq_qr"), (nyq_kr, "nyq_kr"), (nyq_vr, "nyq_vr")][si]
        for tau in range(16):
            for eta in range(2):
                em("t", lambda e, tau=tau, eta=eta:
                   e.matmul(pw[3][0:1, eta * 512:(eta + 1) * 512], altcol_t[:],
                            sig[:, tau, eta * 512:(eta + 1) * 512],
                            start=(tau == 0), stop=(tau == 15)),
                   reads=["altcol"] + gr("r2", tau * 1024, (tau + 1) * 1024),
                   writes=["pw3"], inc=(tau == 15 and eta == 1))
        em("s", lambda e, nyq_dst=nyq_dst: e.copy(nyq_dst[:], pw[3][0:1, :]),
           reads=["pw3"], writes=[nyq_tk])
        if si == 0:
            em("v", lambda e: e.tensor_mul(tmpC[:], nyq_qr[:], nyq_qr[:]),
               reads=["nyq_qr"], writes=["tmpC"])
            em("s", lambda e: e.activation(tmpC[:], tmpC[:], AF.Sqrt),
               reads=["tmpC"], writes=["tmpC"])
            em("v", lambda e: e.tensor_reduce(out=enrow[:].unsqueeze(2),
                                              in_=tmpC[:].rearrange("p (g d) -> p g d", g=16),
                                              axis=AX.X, op=ALU.add),
               reads=["tmpC"], writes=["enrow"])
        if si == 1:
            em("v", lambda e: e.tensor_mul(tmpC[:], nyq_qr[:], nyq_kr[:]),
               reads=["nyq_qr", "nyq_kr"], writes=["tmpC"])
            em("v", lambda e: e.tensor_reduce(out=snrow[:].unsqueeze(2),
                                              in_=tmpC[:].rearrange("p (g d) -> p g d", g=16),
                                              axis=AX.X, op=ALU.add),
               reads=["tmpC"], writes=["snrow"])
        if debug and b == 0:
            dv, dr = {0: (qr_v, "r3"), 1: (kr_v, "r5"), 2: (vr_v, "r6")}[si]
            dn = {0: "d_qr", 1: "d_kr", 2: "d_vr"}[si]
            em("g", lambda e, dv=dv, dn=dn: e.dma_start(out=dbg[dn], in_=dv),
               reads=gr(dr, 0, 8192), dma=True)
            if si == 0:
                em("g", lambda e: e.dma_start(out=dbg["d_qi"], in_=qi_v), reads=gr("r4", 0, 8192), dma=True)
                em("g", lambda e: e.dma_start(out=dbg["d_qn"], in_=nyq_qr[:]), reads=["nyq_qr"], dma=True)

    def phase_middle_pre(b):
        for (src, srctk, dst, dsttk, nrow, nrtk, pwi) in [
                (e_fh, "e_fh", eT, "spA", enrow, "enrow", 2),
                (s_fh, "s_fh", scT, "spB", snrow, "snrow", 3)]:
            for phi in range(8):
                em("t", lambda e, src=src, phi=phi, pwi=pwi:
                   e.transpose(pw[pwi][0:16, phi * 128:(phi + 1) * 128],
                               src[:, phi, :], identf_t[:]),
                   reads=[srctk, "identf"], writes=[f"pw{pwi}"], inc=(phi == 7))
            em("s", lambda e, dst=dst, pwi=pwi: e.copy(dst[:, 0:1024], pw[pwi][0:16, :]),
               reads=[f"pw{pwi}"], writes=[dsttk])
            em("t", lambda e, nrow=nrow, pwi=pwi:
               e.transpose(pw[pwi][0:16, 0:1], nrow[:], identf_t[0:1, 0:1]),
               reads=[nrtk, "identf", dsttk], writes=[f"pw{pwi}"])
            em("s", lambda e, dst=dst, pwi=pwi: e.copy(dst[:, 1024:1025], pw[pwi][0:16, 0:1]),
               reads=[f"pw{pwi}"], writes=[dsttk])

    def phase_middle_a(b):
        for it in range(8):
            em("v", lambda e: e.max(out=max8[:], in_=eT), reads=["spA"], writes=["max8"])
            em("v", lambda e: e.match_replace(out=eT, in_to_replace=max8[:], in_values=eT,
                                              imm_value=-1.0),
               reads=["max8", "spA"], writes=["spA"])
        if debug and b == 0:
            em("g", lambda e: e.dma_start(out=dbg["d_eT"], in_=eT), reads=["spA"], dma=True)
            em("g", lambda e: e.dma_start(out=dbg["d_scT"], in_=scT), reads=["spB"], dma=True)
        em("v", lambda e: e.tensor_scalar(out=um, in0=eT, scalar1=0.0, scalar2=None,
                                          op0=ALU.is_ge), reads=["spA"], writes=["spD"])
        em("v", lambda e: e.tensor_scalar(out=scT, in0=scT, scalar1=C1, scalar2=None,
                                          op0=ALU.mult), reads=["spB"], writes=["spB"])
        em("v", lambda e: e.scalar_tensor_tensor(out=scT, in0=um, scalar=-1e9, in1=scT,
                                                 op0=ALU.mult, op1=ALU.add),
           reads=["spD", "spB"], writes=["spB"])
        em("v", lambda e: e.reduce_max(mxx[:], scT, axis=AX.X), reads=["spB"], writes=["mxx"])
        em("v", lambda e: e.tensor_scalar(out=negmx[:], in0=mxx[:], scalar1=-1.0, scalar2=None,
                                          op0=ALU.mult), reads=["mxx"], writes=["negmx"])
        em("s", lambda e: e.activation(scT, scT, AF.Exp, bias=negmx[:]),
           reads=["spB", "negmx"], writes=["spB"])
        em("v", lambda e: e.reduce_sum(smr[:], scT, axis=AX.X), reads=["spB"], writes=["smr"])
        em("v", lambda e: e.reciprocal(rsm[:], smr[:]), reads=["smr"], writes=["rsm"])
        em("v", lambda e: e.tensor_scalar(out=attnT, in0=scT, scalar1=rsm[:], scalar2=None,
                                          op0=ALU.mult), reads=["spB", "rsm"], writes=["spC"])

    def phase_middle_b(b):
        for (vsp, reg, nrow, nrtk, dst, dtk, pwi) in [
                (qr_v, "r3", nyq_qr, "nyq_qr", qmr, "qmr", 0),
                (qi_v, "r4", None, None, qmi, "qmi", 1)]:
            for phi in range(8):
                last = (phi == 7 and nrow is None)
                for eta in range(2):
                    em("t", lambda e, vsp=vsp, phi=phi, eta=eta, pwi=pwi, last=last:
                       e.matmul(pw[pwi][0:1, eta * 512:(eta + 1) * 512], onescol[:],
                                vsp[:, phi, eta * 512:(eta + 1) * 512],
                                start=(phi == 0), stop=last),
                       reads=gr(reg, phi * 1024, (phi + 1) * 1024) + ["onescol"],
                       writes=[f"pw{pwi}"], inc=(last and eta == 1))
            if nrow is not None:
                for eta in range(2):
                    em("t", lambda e, nrow=nrow, eta=eta, pwi=pwi:
                       e.matmul(pw[pwi][0:1, eta * 512:(eta + 1) * 512], onesrow[0:1, 0:1],
                                nrow[0:1, eta * 512:(eta + 1) * 512], start=False, stop=True),
                       reads=[nrtk, "onesrow"], writes=[f"pw{pwi}"], inc=(eta == 1))
            em("s", lambda e, dst=dst, pwi=pwi: e.mul(dst[:], pw[pwi][0:1, :], 1.0 / NF),
               reads=[f"pw{pwi}"], writes=[dtk])
        for (qrow, qrtk, qt, qttk, pwi) in [(qmr, "qmr", qtr, "qtr", 0), (qmi, "qmi", qti, "qti", 1)]:
            for eta in range(2):
                em("t", lambda e, qrow=qrow, eta=eta, pwi=pwi:
                   e.matmul(pw[pwi][0:16, eta * 512:(eta + 1) * 512], onesrow[0:1, 0:16],
                            qrow[0:1, eta * 512:(eta + 1) * 512], start=True, stop=True),
                   reads=[qrtk, "onesrow"], writes=[f"pw{pwi}"], inc=(eta == 1))
            em("v", lambda e, qt=qt, pwi=pwi: e.tensor_mul(qt[:], pw[pwi][0:16, 0:1024], eexp_t[:]),
               reads=[f"pw{pwi}", "eexp"], writes=[qttk])
        if debug and b == 0:
            em("g", lambda e: e.dma_start(out=dbg["d_attnT"], in_=attnT), reads=["spC"], dma=True)
            em("g", lambda e: e.dma_start(out=dbg["d_um"], in_=um), reads=["spD"], dma=True)
            em("g", lambda e: e.dma_start(out=dbg["d_qmr"], in_=qmr[:]), reads=["qmr"], dma=True)
            em("g", lambda e: e.dma_start(out=dbg["d_qmi"], in_=qmi[:]), reads=["qmi"], dma=True)
        for phi in range(8):
            for eta in range(2):
                em("t", lambda e, phi=phi, eta=eta:
                   e.matmul(pw[0][:, eta * 512:(eta + 1) * 512],
                            attnT[:, phi * 128:(phi + 1) * 128],
                            eexp_t[:, eta * 512:(eta + 1) * 512], start=True, stop=True),
                   reads=["spC", "eexp"], writes=["pw0"], inc=(eta == 1))
                em("t", lambda e, phi=phi, eta=eta:
                   e.matmul(pw[1][:, eta * 512:(eta + 1) * 512],
                            um[:, phi * 128:(phi + 1) * 128],
                            qtr[:, eta * 512:(eta + 1) * 512], start=True, stop=True),
                   reads=["spD", "qtr"], writes=["pw1"], inc=(eta == 1))
            em("t", lambda e, phi=phi:
               e.transpose(pwb[2][:, phi * 16:(phi + 1) * 16],
                           um[:, phi * 128:(phi + 1) * 128], identb_t[0:16, 0:16]),
               reads=["spD", "identb"], writes=["pw2"], inc=(phi == 7))
            for eta in range(2):
                sl = slice(eta * 512, (eta + 1) * 512)
                vtk = gr("r6", phi * 1024 + eta * 512, phi * 1024 + (eta + 1) * 512)
                xtk = gr("r3", phi * 1024 + eta * 512, phi * 1024 + (eta + 1) * 512)
                em("v", lambda e, phi=phi, sl=sl: e.tensor_mul(tmpA[:], pw[0][:, sl], vr_v[:, phi, sl]),
                   reads=["pw0"] + vtk, writes=["tmpA"])
                em("v", lambda e, phi=phi, sl=sl: e.tensor_add(xre_v[:, phi, sl], tmpA[:], pw[1][:, sl]),
                   reads=["tmpA", "pw1"], writes=xtk)
        em("s", lambda e: e.copy(umF[:], pwb[2][:, 0:128].rearrange("p (a c) -> p a c", a=8)),
           reads=["pw2"], writes=["umF"])
        for c in range(8):
            em("t", lambda e, c=c:
               e.transpose(pwb[3][:, c * 16:(c + 1) * 16],
                           qti[:, c * 128:(c + 1) * 128], identb_t[0:16, 0:16]),
               reads=["qti", "identb"], writes=["pw3"], inc=(c == 7))
        em("s", lambda e: e.copy(qtiT[:], pwb[3][:, 0:128].rearrange("p (a c) -> p a c", a=8)),
           reads=["pw3"], writes=["qtiT"])
        for eta in range(2):
            for c in range(8):
                em("t", lambda e, c=c, eta=eta:
                   e.matmul(pw[2][0:16, eta * 512:(eta + 1) * 512], qtiT[:, c, :],
                            wbuf[:, c, eta * 512:(eta + 1) * 512],
                            start=(c == 0), stop=(c == 7)),
                   reads=["qtiT"] + gr("st", c * 1024, (c + 1) * 1024),
                   writes=["pw2"], inc=(c == 7 and eta == 1))
        em("s", lambda e: e.copy(wqim[:], pw[2][0:16, :]), reads=["pw2"], writes=gr("r4", 2048, 3072))
        for eta in range(2):
            em("t", lambda e, eta=eta:
               e.matmul(pw[0][0:1, eta * 512:(eta + 1) * 512], attnT[:, 1024:1025],
                        eexp_t[:, eta * 512:(eta + 1) * 512], start=True, stop=True),
               reads=["spC", "eexp"], writes=["pw0"], inc=(eta == 1))
            em("t", lambda e, eta=eta:
               e.matmul(pw[1][0:1, eta * 512:(eta + 1) * 512], um[:, 1024:1025],
                        qtr[:, eta * 512:(eta + 1) * 512], start=True, stop=True),
               reads=["spD", "qtr"], writes=["pw1"], inc=(eta == 1))
        em("v", lambda e: e.tensor_mul(tmpC[:], pw[0][0:1, :], nyq_vr[:]),
           reads=["pw0", "nyq_vr"], writes=["tmpC"])
        em("v", lambda e: e.tensor_add(nyq_xre[:], tmpC[:], pw[1][0:1, :]),
           reads=["tmpC", "pw1"], writes=["nyq_xre"])
        if debug and b == 0:
            em("g", lambda e: e.dma_start(out=dbg["d_xre"], in_=xre_v), reads=gr("r3", 0, 8192), dma=True)

    def phase_irfft(b):
        for th in range(2):
            tsl = slice(th * 1024, (th + 1) * 1024)
            em("q", lambda e, tsl=tsl: e.dma_start(
                out=crci[:, 0:8, :],
                in_=c_crt[:, tsl].rearrange("(phi p) t -> p phi t", p=128)),
               writes=gr("r2", 0, 8192), dma=True)
            em("q", lambda e, tsl=tsl: e.dma_start(
                out=crci[:, 8:16, :],
                in_=c_cit[:, tsl].rearrange("(phi p) t -> p phi t", p=128)),
               writes=gr("r2", 8192, 16384), dma=True)
            em("q", lambda e, tsl=tsl: e.dma_start(out=crtn_t[:], in_=c_crtnyq[:, tsl]),
               writes=["crtn"], dma=True)
            for half in range(2):
                hsl = slice(half * 512, (half + 1) * 512)
                for phi in range(8):
                    em("t", lambda e, phi=phi, hsl=hsl, half=half:
                       e.matmul(pw[3][0:16, hsl], umF[:, phi, :], crci[:, 8 + phi, hsl],
                                start=(phi == 0), stop=(phi == 7)),
                       reads=["umF"] + gr("r2", 8192 + phi * 1024, 8192 + (phi + 1) * 1024),
                       writes=["pw3"], inc=(phi == 7 and half == 1))
            em("s", lambda e, th=th: e.copy(cmS[:, th * 1024:(th + 1) * 1024], pw[3][0:16, :]),
               reads=["pw3"], writes=gr("r4", 0, 2048))
            for chi in range(8):
                pwi = chi % 2
                for half in range(2):
                    hsl = slice(half * 512, (half + 1) * 512)
                    for phi in range(8):
                        em("t", lambda e, chi=chi, phi=phi, hsl=hsl, pwi=pwi:
                           e.matmul(pw[pwi][:, hsl], xre_v[:, phi, chi * 128:(chi + 1) * 128],
                                    crci[:, phi, hsl], start=(phi == 0), stop=False),
                           reads=gr("r3", phi * 1024 + chi * 128, phi * 1024 + (chi + 1) * 128) +
                           gr("r2", phi * 1024, (phi + 1) * 1024),
                           writes=[f"pw{pwi}"], inc=False)
                    em("t", lambda e, chi=chi, hsl=hsl, pwi=pwi, half=half:
                       e.matmul(pw[pwi][:, hsl], nyq_xre[0:1, chi * 128:(chi + 1) * 128],
                                crtn_t[0:1, hsl], start=False, stop=True),
                       reads=["nyq_xre", "crtn"], writes=[f"pw{pwi}"], inc=(half == 1))
                eng = "s" if chi % 2 == 0 else "v"
                em(eng, lambda e, chi=chi, th=th, pwi=pwi, eng=eng:
                   (e.copy if eng == "s" else e.tensor_copy)(
                       outT(chi)[:, th * 1024:(th + 1) * 1024], pw[pwi][:, :]),
                   reads=[f"pw{pwi}"], writes=outT_tok(chi, th * 1024, (th + 1) * 1024))
        if debug and b == 0:
            em("g", lambda e: e.dma_start(out=dbg["d_outT"][:, 0:4, :], in_=outT_lo),
               reads=gr("r5", 0, 8192), dma=True)
            em("g", lambda e: e.dma_start(out=dbg["d_outT"][:, 4:8, :], in_=outT_hi),
               reads=gr("r6", 0, 8192), dma=True)

    def prefetch_wo(b):
        for dc in range(8):
            em("q", lambda e, dc=dc:
               e.dma_start(out=wbuf[:, dc, :], in_=w_scr["Wo"][dc * 128:(dc + 1) * 128, :]),
               reads=["scrWo"], writes=gr("st", dc * 1024, (dc + 1) * 1024), dma=True)

    def phase_wo(b):
        for tau in range(16):
            pwi = tau % 4
            par = tau % 2
            for dc in range(8):
                for eta in range(2):
                    em("t", lambda e, tau=tau, dc=dc, eta=eta, pwi=pwi:
                       e.matmul(pw[pwi][:, eta * 512:(eta + 1) * 512],
                                outT(dc)[:, tau * 128:(tau + 1) * 128],
                                wbuf[:, dc, eta * 512:(eta + 1) * 512],
                                start=(dc == 0), stop=False),
                       reads=outT_tok(dc, tau * 128, (tau + 1) * 128) +
                       gr("st", dc * 1024, (dc + 1) * 1024),
                       writes=[f"pw{pwi}"], inc=False)
            for eta in range(2):
                em("t", lambda e, tau=tau, eta=eta, pwi=pwi:
                   e.matmul(pw[pwi][:, eta * 512:(eta + 1) * 512],
                            cmS[:, tau * 128:(tau + 1) * 128],
                            wqim[:, eta * 512:(eta + 1) * 512], start=False, stop=False),
                   reads=gr("r4", 0, 3072), writes=[f"pw{pwi}"], inc=False)
                em("t", lambda e, eta=eta, pwi=pwi:
                   e.matmul(pw[pwi][:, eta * 512:(eta + 1) * 512], onesrow[0:1, :],
                            bias_t["bo"][0:1, eta * 512:(eta + 1) * 512], start=False, stop=True),
                   reads=["onesrow", "bo"], writes=[f"pw{pwi}"], inc=(eta == 1))
            em("s", lambda e, pwi=pwi, par=par:
               e.copy(spE[:, par * 1024:(par + 1) * 1024], pw[pwi][:, :]),
               reads=[f"pw{pwi}"], writes=[f"spE{par}"])
            em("q", lambda e, tau=tau, par=par:
               e.dma_start(out=attn_scr[tau * 128:(tau + 1) * 128, :],
                           in_=spE[:, par * 1024:(par + 1) * 1024]),
               reads=[f"spE{par}"], writes=["attn_scr"], dma=True)
        if debug and b == 0:
            em("g", lambda e: e.dma_start(out=dbg["d_attn"], in_=attn_scr),
               reads=["attn_scr"], dma=True)

    def phase_ffn_gu(b, hclist):
        stg = [spA[:].bitcast(I32)[:, 0:1024], spB[:].bitcast(I32)[:, 0:1024],
               spC[:].bitcast(I32)[:, 0:1024], spD[:].bitcast(I32)[:, 0:1024]]
        stk = [["spA"], ["spB"], ["spC"], ["spD"]]
        for hc in hclist:
            par = hc % 2
            pg, pu = (0, 1) if par == 0 else (2, 3)
            em("q", lambda e, hc=hc, par=par:
               e.dma_start(out=stg[par].rearrange("p (dc c) -> p dc c", dc=8),
                           in_=wg_in[:, hc * 128:(hc + 1) * 128]
                           .rearrange("(dc p) c -> p dc c", p=128)),
               writes=stk[par], dma=True)
            em("s", lambda e, hc=hc, par=par:
               e.copy(wgslot[:, par, :], stg[par]),
               reads=stk[par], writes=gr("st", par * 1024, (par + 1) * 1024))
            em("q", lambda e, hc=hc, par=par:
               e.dma_start(out=stg[2 + par].rearrange("p (dc c) -> p dc c", dc=8),
                           in_=wu_in[:, hc * 128:(hc + 1) * 128]
                           .rearrange("(dc p) c -> p dc c", p=128)),
               writes=stk[2 + par], dma=True)
            em("v", lambda e, hc=hc, par=par:
               e.tensor_copy(wgslot[:, 2 + par, :], stg[2 + par]),
               reads=stk[2 + par], writes=gr("st", (2 + par) * 1024, (3 + par) * 1024))
            for th in range(2):
                wgv = wgslot[:, par, :].rearrange("p (dc c) -> p dc c", dc=8)
                wuv = wgslot[:, 2 + par, :].rearrange("p (dc c) -> p dc c", dc=8)
                for dc in range(8):
                    for tq in range(2):
                        em("t", lambda e, wgv=wgv, dc=dc, th=th, tq=tq, pg=pg:
                           e.matmul(pw[pg][:, tq * 512:(tq + 1) * 512], wgv[:, dc, :],
                                    xnT[:, dc, th * 1024 + tq * 512:th * 1024 + (tq + 1) * 512],
                                    start=(dc == 0), stop=(dc == 7)),
                           reads=gr("st", par * 1024, (par + 1) * 1024) +
                           [f"r1c{th * 8 + tq * 4 + i}" for i in range(4)],
                           writes=[f"pw{pg}"], inc=(dc == 7 and tq == 1))
                for dc in range(8):
                    for tq in range(2):
                        em("t", lambda e, wuv=wuv, dc=dc, th=th, tq=tq, pu=pu:
                           e.matmul(pw[pu][:, tq * 512:(tq + 1) * 512], wuv[:, dc, :],
                                    xnT[:, dc, th * 1024 + tq * 512:th * 1024 + (tq + 1) * 512],
                                    start=(dc == 0), stop=(dc == 7)),
                           reads=gr("st", (2 + par) * 1024, (3 + par) * 1024) +
                           [f"r1c{th * 8 + tq * 4 + i}" for i in range(4)],
                           writes=[f"pw{pu}"], inc=(dc == 7 and tq == 1))
                em("s", lambda e, par=par, pg=pg:
                   e.activation(siluslot[:, 4 + par, :], pw[pg][:, :], AF.Silu, scale=SC),
                   reads=[f"pw{pg}"], writes=gr("st", (4 + par) * 1024, (5 + par) * 1024))
                em("v", lambda e, hc=hc, th=th, par=par, pu=pu:
                   e.tensor_mul(hT(hc)[:, th * 1024:(th + 1) * 1024],
                                siluslot[:, 4 + par, :], pw[pu][:, :]),
                   reads=gr("st", (4 + par) * 1024, (5 + par) * 1024) + [f"pw{pu}"],
                   writes=hT_tok(hc, th * 1024, (th + 1) * 1024))
        if debug and b == 0:
            em("g", lambda e: e.dma_start(out=dbg["d_h"], in_=hTv[0]),
               reads=gr("r2", 0, 16384), dma=True)

    def phase_down_final(b):
        em("q", lambda e: e.dma_start(out=wdres_a,
                                      in_=wdT_s[0:2048, :].rearrange("(hc p) d -> p hc d", p=128)),
           reads=["scrwd"], writes=gr("r1", 0, 16384) + [f"r1c{t}" for t in range(16)], dma=True)
        em("q", lambda e: e.dma_start(out=wdres_b,
                                      in_=wdT_s[2048:3072, :].rearrange("(hc p) d -> p hc d", p=128)),
           reads=["scrwd"], writes=gr("st", 0, 8192), dma=True)
        for tp in range(8):
            taus = (2 * tp, 2 * tp + 1)
            for i, tau in enumerate(taus):
                pwi = 2 * (tp % 2) + i
                for hc in range(24):
                    for eta in range(2):
                        em("t", lambda e, tau=tau, hc=hc, eta=eta, pwi=pwi:
                           e.matmul(pw[pwi][:, eta * 512:(eta + 1) * 512],
                                    hT(hc)[:, tau * 128:(tau + 1) * 128],
                                    wdres(hc)[:, eta * 512:(eta + 1) * 512],
                                    start=(hc == 0), stop=(hc == 23)),
                           reads=hT_tok(hc, tau * 128, (tau + 1) * 128) + wdres_tok(hc),
                           writes=[f"pw{pwi}"], inc=(hc == 23 and eta == 1))
            for i, tau in enumerate(taus):
                pwi = 2 * (tp % 2) + i
                par = tau % 2
                sp = ("spA", "spB")[par]
                op_ = ("spC", "spD")[par]
                em("q", lambda e, b=b, tau=tau, par=par:
                   e.dma_start(out=xstage[par], in_=x_in[b, tau * 128:(tau + 1) * 128, :]),
                   writes=[sp], dma=True)
                em("q", lambda e, tau=tau, par=par:
                   e.dma_start(out=spE[:, par * 1024:(par + 1) * 1024],
                               in_=attn_scr[tau * 128:(tau + 1) * 128, :]),
                   reads=["attn_scr"], writes=[f"spE{par}"], dma=True)
                for eta in range(2):
                    sl = slice(eta * 512, (eta + 1) * 512)
                    em("v", lambda e, pwi=pwi, par=par, sl=sl:
                       e.scalar_tensor_tensor(out=ostage[par][:, sl], in0=pw[pwi][:, sl],
                                              scalar=SC * SC, in1=xstage[par][:, sl],
                                              op0=ALU.mult, op1=ALU.add),
                       reads=[f"pw{pwi}", sp], writes=[op_])
                    em("v", lambda e, par=par, sl=sl:
                       e.tensor_add(ostage[par][:, sl], ostage[par][:, sl],
                                    spE[:, par * 1024:(par + 1) * 1024][:, sl]),
                       reads=[op_, f"spE{par}"], writes=[op_])
                em("sd", lambda e, b=b, tau=tau, par=par:
                   e.dma_start(out=out_d[b, tau * 128:(tau + 1) * 128, :], in_=ostage[par]),
                   reads=[op_], dma=True)

    for b in range(NB):
        if b == 0:
            prep_attn("Wq")
            prefetch_wbuf(b, 0)
            phase_ln(b)
            phase_proj(b, 0)
            prep_attn_pend("Wk")
            prep_attn_pend("Wv")
        else:
            phase_ln_proj(b)
        phase_fft(b, 0)
        prefetch_wbuf(b, 1)
        if b == 0:
            prep_attn_pend("Wo")
            prep_pending.extend((src, dst, tk, True) for (src, dst, tk) in ffn_chunks)
        phase_proj(b, 1)
        phase_fft(b, 1)
        prefetch_wbuf(b, 2)
        phase_proj(b, 2)
        if b == 0:
            prep_service(99)
            prep_flush()
        phase_middle_pre(b)
        phase_middle_a(b)
        phase_fft(b, 2)
        prefetch_wo(b)
        phase_middle_b(b)
        phase_irfft(b)
        phase_wo(b)
        phase_ffn_gu(b, range(0, 24))
        phase_down_final(b)

    fin_q, fin_g, fin_sd = S.cnt["q"], S.cnt["g"], S.cnt["sd"]
    S.emit("q", lambda e: e.wait_ge(semG, fin_g), inc=False)
    S.emit("q", lambda e: e.wait_ge(semSD, fin_sd), inc=False)
    S.emit("g", lambda e: e.wait_ge(semQ, fin_q), inc=False)

    def run_engine(skey, eng):
        for (ekey, (fn, waits, inc, amt, tag)) in S.streams[skey]:
            for (pe, pv) in waits:
                eng.wait_ge(SEMS[pe], pv)
            if fn is None:
                continue
            inst = fn(eng)
            if inc and inst is not None:
                inst.then_inc(SEMS[ekey], amt)

    @block.sync
    def _(eng):
        run_engine("q", eng)

    @block.tensor
    def _(eng):
        run_engine("t", eng)

    @block.vector
    def _(eng):
        run_engine("v", eng)

    @block.scalar
    def _(eng):
        run_engine("s", eng)

    @block.gpsimd
    def _(eng):
        run_engine("g", eng)

    stack.close()
    return nc, S.nops


from concourse.bass_utils import run_bass_kernel_spmd

_CACHED = {}


def kernel(**inputs):
    x = np.ascontiguousarray(inputs["L0_attn_query"], dtype=np.float32)
    if "prog" not in _CACHED:
        _CACHED["prog"] = build_program(debug=False)[0]
        _CACHED["consts"] = build_consts()
    nc = _CACHED["prog"]
    consts = _CACHED["consts"]
    shard = x.reshape(8, NB, L, D)
    base = dict(
        Wq=np.ascontiguousarray(inputs["L0_attn_Wq"], np.float32),
        Wk=np.ascontiguousarray(inputs["L0_attn_Wk"], np.float32),
        Wv=np.ascontiguousarray(inputs["L0_attn_Wv"], np.float32),
        Wo=np.ascontiguousarray(inputs["L0_attn_Wo"], np.float32),
        bq=np.ascontiguousarray(inputs["L0_attn_bq"], np.float32).reshape(1, D),
        bk=np.ascontiguousarray(inputs["L0_attn_bk"], np.float32).reshape(1, D),
        bv=np.ascontiguousarray(inputs["L0_attn_bv"], np.float32).reshape(1, D),
        bo=np.ascontiguousarray(inputs["L0_attn_bo"], np.float32).reshape(1, D),
        wg=np.ascontiguousarray(np.asarray(inputs["L0_ffn_w_gate"], np.int32).T),
        wu=np.ascontiguousarray(np.asarray(inputs["L0_ffn_w_up"], np.int32).T),
        wd=np.ascontiguousarray(np.asarray(inputs["L0_ffn_w_down"], np.int32).T),
    )
    for k, v in consts.items():
        base[k] = np.ascontiguousarray(v)
    in_maps = [dict(base, x=np.ascontiguousarray(shard[i])) for i in range(8)]
    res = run_bass_kernel_spmd(nc, in_maps, core_ids=list(range(8)))
    out = np.concatenate(
        [np.asarray(res.results[i]["out"], dtype=np.float32) for i in range(8)], axis=0)
    return out.reshape(32, L, D)


# version 25
# speedup vs baseline: 1.0048x; 1.0048x over previous
import numpy as np
import ml_dtypes
import concourse.bass as bass
import concourse.mybir as mybir

F32 = mybir.dt.float32
BF16 = mybir.dt.bfloat16
I32 = mybir.dt.int32
AF = mybir.ActivationFunctionType
ALU = mybir.AluOpType
AX = mybir.AxisListType

NB = 4
L = 2048
D = 1024
H = 16
DH = 64
NF = 1025
HID = 3072
SC = float(1.0 / 127.0)
C1 = float(1.0 / (np.sqrt(DH) + 1e-8))
ENGS = ("t", "v", "s", "q", "g", "sd")
STREAM_OF = {"sd": "s"}


class Sched:

    def __init__(self):
        self.ops = {e: [] for e in ENGS}
        self.cnt = {e: 0 for e in ENGS}
        self.seen = {e: {p: 0 for p in ENGS} for e in ENGS}
        self.snap = {e: {0: {p: 0 for p in ENGS}} for e in ENGS}
        self.resw = {}
        self.resr = {}
        self.pend = {e: [] for e in ENGS}
        self.pend_f = {e: [] for e in ENGS}
        self.streams = {e: [] for e in ENGS if e not in STREAM_OF}
        self.nops = 0

    def _push(self, e, entry):
        self.ops[e].append(entry)
        self.streams[STREAM_OF.get(e, e)].append((e, entry))

    def emit_fence(self, e):
        bnd = self.cnt[e]
        self._push(e, (None, [(e, bnd)], False, 0, "dmafence"))
        self.seen[e][e] = bnd
        clk = dict(self.seen[e])
        clk[e] = bnd
        self.snap[e][bnd] = clk
        for kind, tk in self.pend_f[e]:
            if kind == "w":
                if self.resw.get(tk, (None, 0))[1] is None and self.resw[tk][0] == e:
                    self.resw[tk] = (e, bnd)
            else:
                lst = self.resr.get(tk, [])
                for i, (re_, rv) in enumerate(lst):
                    if re_ == e and rv is None:
                        lst[i] = (e, bnd)
        self.pend_f[e] = []

    def emit(self, e, fn, reads=(), writes=(), inc=True, dma=False, fence=True, tag=""):
        waits = {}

        def need(pe, pv):
            if pv is None:
                assert pe == e, f"cross-engine dep on un-inc'd op, tag={tag}"
                return
            if self.seen[e].get(pe, 0) < pv:
                waits[pe] = max(waits.get(pe, 0), pv)

        for tk in reads:
            w = self.resw.get(tk)
            if w is not None:
                need(w[0], w[1])
        for tk in writes:
            w = self.resw.get(tk)
            if w is not None:
                need(w[0], w[1])
            for (re_, rv) in self.resr.get(tk, ()):
                need(re_, rv)
        for pe in list(waits):
            pv = waits[pe]
            self.seen[e][pe] = max(self.seen[e][pe], pv)
            sn = self.snap[pe].get(pv)
            if sn:
                for k2, v2 in sn.items():
                    if self.seen[e].get(k2, 0) < v2:
                        self.seen[e][k2] = v2
        amt = 16 if dma else 1
        self._push(e, (fn, sorted(waits.items()), inc, amt, tag))
        self.nops += 1
        if inc:
            self.cnt[e] += amt
            nc_ = self.cnt[e]
            for kind, tk in self.pend[e]:
                if kind == "w":
                    if self.resw.get(tk, (None, 0))[1] is None and self.resw[tk][0] == e:
                        self.resw[tk] = (e, nc_)
                else:
                    lst = self.resr.get(tk, [])
                    for i, (re_, rv) in enumerate(lst):
                        if re_ == e and rv is None:
                            lst[i] = (e, nc_)
            self.pend[e] = []
            clk = dict(self.seen[e])
            clk[e] = nc_
            self.snap[e][nc_] = clk
        for tk in writes:
            if inc:
                self.resw[tk] = (e, self.cnt[e])
            else:
                self.resw[tk] = (e, None)
                self.pend[e].append(("w", tk))
            self.resr[tk] = []
        for tk in reads:
            lst = self.resr.setdefault(tk, [])
            if inc:
                lst.append((e, self.cnt[e]))
            else:
                lst.append((e, None))
                self.pend[e].append(("r", tk))
        if dma and inc:
            if fence:
                self._push(e, (None, [(e, self.cnt[e])], False, 0, "dmafence"))
                self.seen[e][e] = self.cnt[e]
                bnd = self.cnt[e]
                for kind, tk in self.pend_f[e]:
                    if kind == "w":
                        if self.resw.get(tk, (None, 0))[1] is None and self.resw[tk][0] == e:
                            self.resw[tk] = (e, bnd)
                    else:
                        lst = self.resr.get(tk, [])
                        for i, (re_, rv) in enumerate(lst):
                            if re_ == e and rv is None:
                                lst[i] = (e, bnd)
                self.pend_f[e] = []
            else:
                for tk in writes:
                    self.resw[tk] = (e, None)
                    self.pend_f[e].append(("w", tk))
                for tk in reads:
                    lst = self.resr.setdefault(tk, [])
                    for i in range(len(lst) - 1, -1, -1):
                        if lst[i] == (e, self.cnt[e]):
                            lst[i] = (e, None)
                            break
                    self.pend_f[e].append(("r", tk))


def gr(reg, lo, hi):
    return [f"{reg}g{i}" for i in range(lo // 1024, (hi + 1023) // 1024)]


def build_consts():
    n = np.arange(L)
    k = np.arange(D)
    ph = 2 * np.pi * np.outer(n, k) / L
    dft_cos = np.cos(ph).astype(ml_dtypes.bfloat16)
    dft_sinn = (-np.sin(ph)).astype(ml_dtypes.bfloat16)
    w = np.full(D, 2.0, np.float64)
    w[0] = 1.0
    phT = 2 * np.pi * np.outer(np.arange(D), n) / L
    crt = ((w[:, None] / L) * np.cos(phT)).astype(ml_dtypes.bfloat16)
    cit = (-(w[:, None] / L) * np.sin(phT)).astype(ml_dtypes.bfloat16)
    crt_nyq = ((1.0 / L) * np.where(n % 2 == 0, 1.0, -1.0)).astype(ml_dtypes.bfloat16)[None, :]
    altcol = np.where(np.arange(128) % 2 == 0, 1.0, -1.0).astype(ml_dtypes.bfloat16)[:, None]
    eexp = np.zeros((H, D), ml_dtypes.bfloat16)
    for h in range(H):
        eexp[h, h * DH:(h + 1) * DH] = 1.0
    identb = np.eye(128, dtype=ml_dtypes.bfloat16)
    identf = np.eye(128, dtype=np.float32)
    return dict(dft_cos=dft_cos, dft_sinn=dft_sinn, crt=crt, cit=cit,
                crt_nyq=crt_nyq, altcol=altcol, eexp=eexp,
                identb=identb, identf=identf)


def build_program(debug=False):
    nc = bass.Bass("TRN2", target_bir_lowering=False, debug=False)
    S = Sched()
    em = S.emit

    x_in = nc.dram_tensor("x", [NB, L, D], F32, kind="ExternalInput").ap()
    W_ins = {nm: nc.dram_tensor(nm, [D, D], F32, kind="ExternalInput").ap()
             for nm in ("Wq", "Wk", "Wv", "Wo")}
    b_ins = {nm: nc.dram_tensor(nm, [1, D], F32, kind="ExternalInput").ap()
             for nm in ("bq", "bk", "bv", "bo")}
    wg_in = nc.dram_tensor("wg", [D, HID], I32, kind="ExternalInput").ap()
    wu_in = nc.dram_tensor("wu", [D, HID], I32, kind="ExternalInput").ap()
    wd_in = nc.dram_tensor("wd", [HID, D], I32, kind="ExternalInput").ap()
    c_cos = nc.dram_tensor("dft_cos", [L, D], BF16, kind="ExternalInput").ap()
    c_sinn = nc.dram_tensor("dft_sinn", [L, D], BF16, kind="ExternalInput").ap()
    c_crt = nc.dram_tensor("crt", [D, L], BF16, kind="ExternalInput").ap()
    c_cit = nc.dram_tensor("cit", [D, L], BF16, kind="ExternalInput").ap()
    c_crtnyq = nc.dram_tensor("crt_nyq", [1, L], BF16, kind="ExternalInput").ap()
    c_altcol = nc.dram_tensor("altcol", [128, 1], BF16, kind="ExternalInput").ap()
    c_eexp = nc.dram_tensor("eexp", [H, D], BF16, kind="ExternalInput").ap()
    c_identb = nc.dram_tensor("identb", [128, 128], BF16, kind="ExternalInput").ap()
    c_identf = nc.dram_tensor("identf", [128, 128], F32, kind="ExternalInput").ap()
    out_d = nc.dram_tensor("out", [NB, L, D], F32, kind="ExternalOutput").ap()

    w_scr = {nm: nc.dram_tensor(nm.lower() + "_s", [D, D], BF16, kind="Internal").ap()
             for nm in ("Wq", "Wk", "Wv", "Wo")}
    wgT_s = nc.dram_tensor("wgT_s", [D, HID], BF16, kind="Internal").ap()
    wuT_s = nc.dram_tensor("wuT_s", [D, HID], BF16, kind="Internal").ap()
    wdT_s = nc.dram_tensor("wdT_s", [HID, D], BF16, kind="Internal").ap()
    attn_scr = nc.dram_tensor("attn_scr", [L, D], BF16, kind="Internal").ap()

    dbg = {}
    if debug:
        for nm, shp, dt in [("d_xnT", [128, 8, L], BF16), ("d_qr", [128, 8, D], BF16),
                            ("d_qi", [128, 8, D], BF16), ("d_kr", [128, 8, D], BF16),
                            ("d_vr", [128, 8, D], BF16), ("d_eT", [H, NF], F32),
                            ("d_scT", [H, NF], F32), ("d_attnT", [H, NF], BF16),
                            ("d_um", [H, NF], BF16), ("d_xre", [128, 8, D], BF16),
                            ("d_xim", [128, 8, D], BF16), ("d_outT", [128, 8, L], BF16),
                            ("d_attn", [L, D], BF16), ("d_qn", [1, D], BF16),
                            ("d_qmr", [1, D], BF16), ("d_qmi", [1, D], BF16),
                            ("d_sig", [128, 16, D], BF16), ("d_h", [128, 8, L], BF16)]:
            dbg[nm] = nc.dram_tensor(nm, shp, dt, kind="ExternalOutput").ap()

    from contextlib import ExitStack
    stack = ExitStack()
    E = stack.enter_context

    r1 = E(nc.sbuf_tensor("r1", [128, 16384], BF16))
    r2 = E(nc.sbuf_tensor("r2", [128, 16384], BF16))
    r3 = E(nc.sbuf_tensor("r3", [128, 8192], BF16))
    r4 = E(nc.sbuf_tensor("r4", [128, 8192], BF16))
    r5 = E(nc.sbuf_tensor("r5", [128, 8192], BF16))
    r6 = E(nc.sbuf_tensor("r6", [128, 8192], BF16))
    strm = E(nc.sbuf_tensor("strm", [128, 8192], BF16))
    spA = E(nc.sbuf_tensor("spA", [128, 1040], F32))
    spB = E(nc.sbuf_tensor("spB", [128, 1040], F32))
    spC = E(nc.sbuf_tensor("spC", [128, 1040], F32))
    spD = E(nc.sbuf_tensor("spD", [128, 1040], F32))
    spE = E(nc.sbuf_tensor("spE", [128, 2048], BF16))
    e_fh = E(nc.sbuf_tensor("e_fh", [128, 8, 16], F32))
    s_fh = E(nc.sbuf_tensor("s_fh", [128, 8, 16], F32))
    max8 = E(nc.sbuf_tensor("max8", [16, 8], F32))
    enrow = E(nc.sbuf_tensor("enrow", [1, 16], F32))
    snrow = E(nc.sbuf_tensor("snrow", [1, 16], F32))
    mxx = E(nc.sbuf_tensor("mxx", [16, 1], F32))
    negmx = E(nc.sbuf_tensor("negmx", [16, 1], F32))
    smr = E(nc.sbuf_tensor("smr", [16, 1], F32))
    rsm = E(nc.sbuf_tensor("rsm", [16, 1], F32))
    msum = [E(nc.sbuf_tensor(f"msum{i}", [128, 1], F32)) for i in range(4)]
    ssq = [E(nc.sbuf_tensor(f"ssq{i}", [128, 1], F32)) for i in range(4)]
    meanv = [E(nc.sbuf_tensor(f"meanv{i}", [128, 1], F32)) for i in range(4)]
    m2v = [E(nc.sbuf_tensor(f"m2v{i}", [128, 1], F32)) for i in range(4)]
    varv = [E(nc.sbuf_tensor(f"varv{i}", [128, 1], F32)) for i in range(4)]
    sdv = [E(nc.sbuf_tensor(f"sdv{i}", [128, 1], F32)) for i in range(4)]
    rstd = [E(nc.sbuf_tensor(f"rstd{i}", [128, 1], F32)) for i in range(4)]
    epsv = E(nc.sbuf_tensor("epsv", [128, 1], F32))
    tmpA = E(nc.sbuf_tensor("tmpA", [128, 512], F32))
    tmpB = E(nc.sbuf_tensor("tmpB", [128, 512], F32))
    tmpC = E(nc.sbuf_tensor("tmpC", [1, 1024], F32))
    qmr = E(nc.sbuf_tensor("qmr", [1, 1024], BF16))
    qmi = E(nc.sbuf_tensor("qmi", [1, 1024], BF16))
    qtr = E(nc.sbuf_tensor("qtr", [16, 1024], BF16))
    qti = E(nc.sbuf_tensor("qti", [16, 1024], BF16))
    umF = E(nc.sbuf_tensor("umF", [128, 8, 16], BF16))
    qtiT = E(nc.sbuf_tensor("qtiT", [128, 8, 16], BF16))
    eexp_t = E(nc.sbuf_tensor("eexp_t", [16, 1024], BF16))
    identb_t = E(nc.sbuf_tensor("identb_t", [128, 128], BF16))
    identf_t = E(nc.sbuf_tensor("identf_t", [128, 128], F32))
    altcol_t = E(nc.sbuf_tensor("altcol_t", [128, 1], BF16))
    onesrow = E(nc.sbuf_tensor("onesrow", [1, 128], BF16))
    onescol = E(nc.sbuf_tensor("onescol", [128, 1], BF16))
    bias_t = {nm: E(nc.sbuf_tensor(nm + "_t", [1, 1024], BF16))
              for nm in ("bq", "bk", "bv", "bo")}
    nyq_qr = E(nc.sbuf_tensor("nyq_qr", [1, 1024], BF16))
    nyq_kr = E(nc.sbuf_tensor("nyq_kr", [1, 1024], BF16))
    nyq_vr = E(nc.sbuf_tensor("nyq_vr", [1, 1024], BF16))
    nyq_xre = E(nc.sbuf_tensor("nyq_xre", [1, 1024], BF16))
    crtn_t = E(nc.sbuf_tensor("crtn_t", [1, 1024], BF16))
    pw = [E(nc.psum_tensor(f"pw{i}", [128, 1024], F32)) for i in range(4)]

    semT = E(nc.semaphore("semT"))
    semV = E(nc.semaphore("semV"))
    semS = E(nc.semaphore("semS"))
    semQ = E(nc.semaphore("semQ"))
    semG = E(nc.semaphore("semG"))
    semSD = E(nc.semaphore("semSD"))
    SEMS = {"t": semT, "v": semV, "s": semS, "q": semQ, "g": semG, "sd": semSD}
    block = E(nc.Block())

    def rv(t, a, b):
        return t[:, 0:a * b].rearrange("p (a b) -> p a b", a=a)

    xnT = rv(r1, 8, 2048)
    sig = rv(r2, 16, 1024)
    crci = rv(r2, 16, 1024)
    qr_v = rv(r3, 8, 1024)
    qi_v = rv(r4, 8, 1024)
    kr_v = rv(r5, 8, 1024)
    vr_v = rv(r6, 8, 1024)
    xre_v = qr_v
    xim_v = qi_v
    cmS = r4[0:16, 0:2048]
    wqim = r4[0:16, 2048:3072]
    outT_lo = rv(r5, 4, 2048)
    outT_hi = rv(r6, 4, 2048)

    def outT(chi):
        return outT_lo[:, chi, :] if chi < 4 else outT_hi[:, chi - 4, :]

    def outT_tok(chi, lo=0, hi=2048):
        reg = "r5" if chi < 4 else "r6"
        base = (chi % 4) * 2048
        return gr(reg, base + lo, base + hi)

    hTv = [rv(r2, 8, 2048), rv(r3, 4, 2048), rv(r4, 4, 2048),
           rv(r5, 4, 2048), rv(r6, 4, 2048)]

    def hT(hc):
        if hc < 8:
            return hTv[0][:, hc, :]
        return hTv[1 + (hc - 8) // 4][:, (hc - 8) % 4, :]

    def hT_tok(hc, lo=0, hi=2048):
        reg = "r2" if hc < 8 else ("r3", "r4", "r5", "r6")[(hc - 8) // 4]
        base = (hc * 2048 if hc < 8 else ((hc - 8) % 4) * 2048)
        return gr(reg, base + lo, base + hi)

    wdres_a = rv(r1, 16, 1024)
    wdres_b = rv(strm, 8, 1024)

    def wdres(hc):
        return wdres_a[:, hc, :] if hc < 16 else wdres_b[:, hc - 16, :]

    def wdres_tok(hc):
        return gr("r1", hc * 1024, (hc + 1) * 1024) if hc < 16 else \
            gr("st", (hc - 16) * 1024, (hc - 15) * 1024)

    wbuf = rv(strm, 8, 1024)
    cosslot = rv(strm, 32, 128)
    sinslot4 = rv(strm, 64, 128)
    wgslot = rv(strm, 8, 1024)
    siluslot = rv(strm, 8, 1024)

    eT = spA[0:16, 0:1025]
    scT = spB[0:16, 0:1025]
    attnT = spC[:].bitcast(BF16)[0:16, 0:1025]
    um = spD[:].bitcast(BF16)[0:16, 0:1025]
    xstage = [spA[:, 0:1024], spB[:, 0:1024]]
    ostage = [spC[:, 0:1024], spD[:, 0:1024]]
    istage = [spA[:].bitcast(I32)[:, 0:1024], spB[:].bitcast(I32)[:, 0:1024]]
    bstage = [spC[:].bitcast(BF16)[:, 0:2048], spD[:].bitcast(BF16)[:, 0:2048]]
    lnx = [r5[:].bitcast(F32)[:, i * 1024:(i + 1) * 1024] for i in range(4)]
    LNX_TK = [gr("r5", i * 2048, (i + 1) * 2048) for i in range(4)]
    lnxn = [r6[:, 0:1024], r6[:, 1024:2048], r6[:, 6144:7168], r6[:, 7168:8192]]
    LNXN_TK = [gr("r6", 0, 1024), gr("r6", 1024, 2048),
               gr("r6", 6144, 7168), gr("r6", 7168, 8192)]
    sqdump = [r6[:].bitcast(F32)[:, 1024:2048], r6[:].bitcast(F32)[:, 2048:3072]]
    SQD_TK = [gr("r6", 2048, 4096), gr("r6", 4096, 6144)]
    pwb = [p[:].bitcast(BF16) for p in pw]

    em("q", lambda e: e.dma_start(out=identb_t[:], in_=c_identb), writes=["identb"], dma=True)
    em("q", lambda e: e.dma_start(out=identf_t[:], in_=c_identf), writes=["identf"], dma=True)
    em("q", lambda e: e.dma_start(out=altcol_t[:], in_=c_altcol), writes=["altcol"], dma=True)
    em("q", lambda e: e.dma_start(out=eexp_t[:], in_=c_eexp), writes=["eexp"], dma=True)
    em("v", lambda e: e.memset(onesrow[:], 1.0), writes=["onesrow"])
    em("v", lambda e: e.memset(epsv[:], 1e-5), writes=["epsv"])
    em("v", lambda e: e.memset(onescol[:], 1.0), writes=["onescol"])
    for nm in ("bq", "bk", "bv", "bo"):
        em("q", lambda e, nm=nm: e.dma_start(out=lnx[0][0:1, :], in_=b_ins[nm]),
           writes=LNX_TK[0], dma=True)
        em("v", lambda e, nm=nm: e.tensor_copy(bias_t[nm][:], lnx[0][0:1, :]),
           reads=LNX_TK[0], writes=[nm])
    sp4 = [spA[:, 0:1024], spB[:, 0:1024], spC[:, 0:1024], spD[:, 0:1024]]
    sp4i = [t.bitcast(I32) for t in sp4]
    SP4 = [["spA"], ["spB"], ["spC"], ["spD"]]
    spEh = [spE[:, 0:1024], spE[:, 1024:2048]]
    prep_n = {"i": 0}
    prep_pipe = []

    def prep_one(src_ap, dst_ap, dtk, int32):
        i = prep_n["i"]
        prep_n["i"] += 1
        par, ph = i % 4, i % 2
        it = sp4i[par] if int32 else sp4[par]
        em("q", lambda e, it=it, src_ap=src_ap:
           e.dma_start(out=it, in_=src_ap), writes=SP4[par], dma=True)

        def cast_store(i=i, it=it, par=par, ph=ph, dst_ap=dst_ap, dtk=dtk):
            em("v" if i % 2 == 0 else "s", lambda e, it=it, ph=ph, i=i:
               (e.tensor_copy if i % 2 == 0 else e.copy)(spEh[ph], it),
               reads=SP4[par], writes=[f"spE{ph}"])
            em("q", lambda e, ph=ph, dst_ap=dst_ap: e.dma_start(out=dst_ap, in_=spEh[ph]),
               reads=[f"spE{ph}"], writes=[dtk], dma=True)

        prep_pipe.append(cast_store)
        if len(prep_pipe) > 3:
            prep_pipe.pop(0)()

    def prep_flush():
        while prep_pipe:
            prep_pipe.pop(0)()

    prep_pending = []

    def prep_service(k):
        for _ in range(k):
            if prep_pending:
                prep_one(*prep_pending.pop(0))

    def prep_attn(nm):
        for dc in range(8):
            prep_one(W_ins[nm][dc * 128:(dc + 1) * 128, :],
                     w_scr[nm][dc * 128:(dc + 1) * 128, :], f"scr{nm}", False)
        prep_flush()

    def prep_attn_pend(nm):
        for dc in range(8):
            prep_pending.append((W_ins[nm][dc * 128:(dc + 1) * 128, :],
                                 w_scr[nm][dc * 128:(dc + 1) * 128, :], f"scr{nm}", False))

    def prep_chunk(idx, src_ap, dst_ap, dtk):
        prep_one(src_ap, dst_ap, dtk, True)

    ffn_chunks = []
    for hc in range(24):
        ffn_chunks.append((wd_in[hc * 128:(hc + 1) * 128, :], wdT_s[hc * 128:(hc + 1) * 128, :], "scrwd"))
    prep_state = {"i": 0}

    def prep_some(n):
        i0 = prep_state["i"]
        for i in range(i0, min(i0 + n, len(ffn_chunks))):
            prep_chunk(i, *ffn_chunks[i])
        prep_state["i"] = min(i0 + n, len(ffn_chunks))
        prep_flush()

    def phase_ln(b):
        for tau in range(16):
            par = tau % 4
            pwi = 2 + (tau % 2)
            em("q", lambda e, b=b, tau=tau, par=par:
               e.dma_start(out=lnx[par], in_=x_in[b, tau * 128:(tau + 1) * 128, :]),
               writes=LNX_TK[par], dma=True, tag=f"lnx{tau}")
            em("v", lambda e, par=par: e.reduce_sum(msum[par][:], lnx[par], axis=AX.X),
               reads=LNX_TK[par], writes=[f"msum{par}"])
            em("s", lambda e, par=par: e.activation(sqdump[par % 2], lnx[par], AF.Square,
                                                    accum_out=ssq[par][:]),
               reads=LNX_TK[par], writes=[f"ssq{par}"] + SQD_TK[par % 2])
            em("v", lambda e, par=par: e.tensor_scalar(out=meanv[par][:], in0=msum[par][:],
                                                       scalar1=1.0 / D, scalar2=None, op0=ALU.mult),
               reads=[f"msum{par}"], writes=[f"meanv{par}"])
            em("v", lambda e, par=par: e.tensor_mul(m2v[par][:], meanv[par][:], meanv[par][:]),
               reads=[f"meanv{par}"], writes=[f"m2v{par}"])
            em("v", lambda e, par=par: e.scalar_tensor_tensor(out=varv[par][:], in0=ssq[par][:],
                                                              scalar=1.0 / D, in1=m2v[par][:],
                                                              op0=ALU.mult, op1=ALU.subtract),
               reads=[f"ssq{par}", f"m2v{par}"], writes=[f"varv{par}"])
            em("s", lambda e, par=par: e.activation(sdv[par][:], varv[par][:], AF.Sqrt, bias=epsv[:]),
               reads=[f"varv{par}", "epsv"], writes=[f"sdv{par}"])
            em("v", lambda e, par=par: e.reciprocal(rstd[par][:], sdv[par][:]),
               reads=[f"sdv{par}"], writes=[f"rstd{par}"])
            em("v", lambda e, par=par: e.tensor_scalar(out=lnxn[par], in0=lnx[par],
                                                       scalar1=meanv[par][:], scalar2=rstd[par][:],
                                                       op0=ALU.subtract, op1=ALU.mult),
               reads=LNX_TK[par] + [f"meanv{par}", f"rstd{par}"], writes=LNXN_TK[par])
            for dc in range(8):
                em("t", lambda e, par=par, dc=dc, pwi=pwi:
                   e.transpose(pwb[pwi][:, dc * 128:(dc + 1) * 128],
                               lnxn[par][:, dc * 128:(dc + 1) * 128], identb_t[:]),
                   reads=LNXN_TK[par] + ["identb"], writes=[f"pw{pwi}"], inc=(dc == 7))
            em("s", lambda e, tau=tau, pwi=pwi:
               e.copy(xnT[:, :, tau * 128:(tau + 1) * 128],
                      pwb[pwi][:, 0:1024].rearrange("p (dc c) -> p dc c", dc=8)),
               reads=[f"pw{pwi}"], writes=[f"r1c{tau}"])
        if debug and b == 0:
            em("g", lambda e: e.dma_start(out=dbg["d_xnT"], in_=xnT),
               reads=[f"r1c{t}" for t in range(16)], dma=True)

    def prefetch_wbuf(b, si):
        nm = ("Wq", "Wk", "Wv")[si]
        for dc in range(8):
            em("q", lambda e, nm=nm, dc=dc:
               e.dma_start(out=wbuf[:, dc, :], in_=w_scr[nm][dc * 128:(dc + 1) * 128, :]),
               reads=[f"scr{nm}"], writes=gr("st", dc * 1024, (dc + 1) * 1024), dma=True)

    def phase_proj(b, si):
        bias = bias_t[("bq", "bk", "bv")[si]]
        btk = ("bq", "bk", "bv")[si]
        reg = "st"
        wv = wbuf
        for tau in range(16):
            pwi = tau % 4
            stok = gr("r2", tau * 1024, (tau + 1) * 1024)
            for dc in range(8):
                for eta in range(2):
                    em("t", lambda e, tau=tau, dc=dc, eta=eta, pwi=pwi, wv=wv:
                       e.matmul(pw[pwi][:, eta * 512:(eta + 1) * 512],
                                xnT[:, dc, tau * 128:(tau + 1) * 128],
                                wv[:, dc, eta * 512:(eta + 1) * 512],
                                start=(dc == 0), stop=False),
                       reads=[f"r1c{tau}"] + gr(reg, dc * 1024, (dc + 1) * 1024),
                       writes=[f"pw{pwi}"], inc=False)
            for eta in range(2):
                em("t", lambda e, eta=eta, pwi=pwi, bias=bias:
                   e.matmul(pw[pwi][:, eta * 512:(eta + 1) * 512], onesrow[0:1, :],
                            bias[0:1, eta * 512:(eta + 1) * 512], start=False, stop=True),
                   reads=["onesrow", btk], writes=[f"pw{pwi}"], inc=(eta == 1))
            if tau % 2 == 0:
                em("s", lambda e, tau=tau, pwi=pwi: e.copy(sig[:, tau, :], pw[pwi][:, :]),
                   reads=[f"pw{pwi}"], writes=stok)
            else:
                em("v", lambda e, tau=tau, pwi=pwi: e.tensor_copy(sig[:, tau, :], pw[pwi][:, :]),
                   reads=[f"pw{pwi}"], writes=stok)
            if b == 0 and si >= 1:
                prep_service(1)
        if debug and b == 0 and si == 0:
            em("g", lambda e: e.dma_start(out=dbg["d_sig"], in_=sig),
               reads=gr("r2", 0, 16384), dma=True)

    def phase_ln_proj(b):
        def lndma(tau):
            par = tau % 4
            em("q", lambda e, b=b, tau=tau, par=par:
               e.dma_start(out=lnx[par], in_=x_in[b, tau * 128:(tau + 1) * 128, :]),
               writes=LNX_TK[par], dma=True, tag=f"lnx{tau}")

        def lnchain(tau):
            par = tau % 4
            em("v", lambda e, par=par: e.reduce_sum(msum[par][:], lnx[par], axis=AX.X),
               reads=LNX_TK[par], writes=[f"msum{par}"])
            em("s", lambda e, par=par: e.activation(sqdump[par % 2], lnx[par], AF.Square,
                                                    accum_out=ssq[par][:]),
               reads=LNX_TK[par], writes=[f"ssq{par}"] + SQD_TK[par % 2])
            em("v", lambda e, par=par: e.tensor_scalar(out=meanv[par][:], in0=msum[par][:],
                                                       scalar1=1.0 / D, scalar2=None, op0=ALU.mult),
               reads=[f"msum{par}"], writes=[f"meanv{par}"])
            em("v", lambda e, par=par: e.tensor_mul(m2v[par][:], meanv[par][:], meanv[par][:]),
               reads=[f"meanv{par}"], writes=[f"m2v{par}"])
            em("v", lambda e, par=par: e.scalar_tensor_tensor(out=varv[par][:], in0=ssq[par][:],
                                                              scalar=1.0 / D, in1=m2v[par][:],
                                                              op0=ALU.mult, op1=ALU.subtract),
               reads=[f"ssq{par}", f"m2v{par}"], writes=[f"varv{par}"])
            em("s", lambda e, par=par: e.activation(sdv[par][:], varv[par][:], AF.Sqrt, bias=epsv[:]),
               reads=[f"varv{par}", "epsv"], writes=[f"sdv{par}"])
            em("v", lambda e, par=par: e.reciprocal(rstd[par][:], sdv[par][:]),
               reads=[f"sdv{par}"], writes=[f"rstd{par}"])
            em("v", lambda e, par=par: e.tensor_scalar(out=lnxn[par], in0=lnx[par],
                                                       scalar1=meanv[par][:], scalar2=rstd[par][:],
                                                       op0=ALU.subtract, op1=ALU.mult),
               reads=LNX_TK[par] + [f"meanv{par}", f"rstd{par}"], writes=LNXN_TK[par])

        def transp(tau):
            par = tau % 4
            pwi = 2 + (tau % 2)
            for dc in range(8):
                em("t", lambda e, par=par, dc=dc, pwi=pwi:
                   e.transpose(pwb[pwi][:, dc * 128:(dc + 1) * 128],
                               lnxn[par][:, dc * 128:(dc + 1) * 128], identb_t[:]),
                   reads=LNXN_TK[par] + ["identb"], writes=[f"pw{pwi}"], inc=(dc == 7))
            em("s", lambda e, tau=tau, pwi=pwi:
               e.copy(xnT[:, :, tau * 128:(tau + 1) * 128],
                      pwb[pwi][:, 0:1024].rearrange("p (dc c) -> p dc c", dc=8)),
               reads=[f"pw{pwi}"], writes=[f"r1c{tau}"])

        def projmms(tau):
            ppw = tau % 2
            stok = gr("r2", tau * 1024, (tau + 1) * 1024)
            for dc in range(8):
                for eta in range(2):
                    em("t", lambda e, tau=tau, dc=dc, eta=eta, ppw=ppw:
                       e.matmul(pw[ppw][:, eta * 512:(eta + 1) * 512],
                                xnT[:, dc, tau * 128:(tau + 1) * 128],
                                wbuf[:, dc, eta * 512:(eta + 1) * 512],
                                start=(dc == 0), stop=False),
                       reads=[f"r1c{tau}"] + gr("st", dc * 1024, (dc + 1) * 1024),
                       writes=[f"pw{ppw}"], inc=False)
            for eta in range(2):
                em("t", lambda e, eta=eta, ppw=ppw:
                   e.matmul(pw[ppw][:, eta * 512:(eta + 1) * 512], onesrow[0:1, :],
                            bias_t["bq"][0:1, eta * 512:(eta + 1) * 512], start=False, stop=True),
                   reads=["onesrow", "bq"], writes=[f"pw{ppw}"], inc=(eta == 1))
            em("v", lambda e, tau=tau, ppw=ppw: e.tensor_copy(sig[:, tau, :], pw[ppw][:, :]),
               reads=[f"pw{ppw}"], writes=stok)

        for tau in range(4):
            lndma(tau)
        prefetch_wbuf(b, 0)
        for tau in range(16):
            lnchain(tau)
            if tau + 4 < 16:
                lndma(tau + 4)
            transp(tau)
            if tau >= 1:
                projmms(tau - 1)
        projmms(15)

    def phase_fft(b, si):
        tlist = {0: [("qr", c_cos, 0, qr_v, "r3", 0), ("qi", c_sinn, 1, qi_v, "r4", 2)],
                 1: [("kr", c_cos, 0, kr_v, "r5", 0)],
                 2: [("vr", c_cos, 0, vr_v, "r6", 0)]}[si]
        for phi in range(8):
            par = phi % 2
            for (nm, mat, kind, dstv, reg, pwbase) in tlist:
                slotv = cosslot[:, par * 16:(par + 1) * 16, :] if kind == 0 else \
                    sinslot4[:, 32 + par * 16:32 + (par + 1) * 16, :]
                slot_tk = gr("st", par * 2048, (par + 1) * 2048) if kind == 0 else \
                    gr("st", 4096 + par * 2048, 4096 + (par + 1) * 2048)
                qq = "q"
                em(qq, lambda e, mat=mat, phi=phi, slotv=slotv:
                   e.dma_start(out=slotv, in_=mat[:, phi * 128:(phi + 1) * 128]
                               .rearrange("(tau p) c -> p tau c", p=128)),
                   writes=slot_tk, dma=True, tag=f"fftdma{nm}{phi}")
                pwi = pwbase + par
                for tau in range(16):
                    for eta in range(2):
                        em("t", lambda e, slotv=slotv, tau=tau, eta=eta, pwi=pwi:
                           e.matmul(pw[pwi][:, eta * 512:(eta + 1) * 512],
                                    slotv[:, tau, :],
                                    sig[:, tau, eta * 512:(eta + 1) * 512],
                                    start=(tau == 0), stop=(tau == 15)),
                           reads=slot_tk + gr("r2", tau * 1024, (tau + 1) * 1024),
                           writes=[f"pw{pwi}"], inc=(tau == 15 and eta == 1))
                eng = "s" if kind == 0 else "v"
                em(eng, lambda e, dstv=dstv, phi=phi, pwi=pwi, eng=eng:
                   (e.copy if eng == "s" else e.tensor_copy)(dstv[:, phi, :], pw[pwi][:, :]),
                   reads=[f"pw{pwi}"], writes=gr(reg, phi * 1024, (phi + 1) * 1024))
            if si == 0:
                for eta in range(2):
                    sl = slice(eta * 512, (eta + 1) * 512)
                    qtk = gr("r3", phi * 1024 + eta * 512, phi * 1024 + (eta + 1) * 512)
                    itk = gr("r4", phi * 1024 + eta * 512, phi * 1024 + (eta + 1) * 512)
                    em("v", lambda e, phi=phi, sl=sl: e.tensor_mul(tmpA[:], qr_v[:, phi, sl], qr_v[:, phi, sl]),
                       reads=qtk, writes=["tmpA"])
                    em("v", lambda e, phi=phi, sl=sl: e.tensor_mul(tmpB[:], qi_v[:, phi, sl], qi_v[:, phi, sl]),
                       reads=itk, writes=["tmpB"])
                    em("v", lambda e: e.tensor_add(tmpA[:], tmpA[:], tmpB[:]),
                       reads=["tmpA", "tmpB"], writes=["tmpA"])
                    em("s", lambda e: e.activation(tmpA[:], tmpA[:], AF.Sqrt),
                       reads=["tmpA"], writes=["tmpA"])
                    em("v", lambda e, phi=phi, eta=eta:
                       e.tensor_reduce(out=e_fh[:, phi, eta * 8:(eta + 1) * 8].unsqueeze(2),
                                       in_=tmpA[:].rearrange("p (g d) -> p g d", g=8),
                                       axis=AX.X, op=ALU.add),
                       reads=["tmpA"], writes=["e_fh"])
            if si == 1:
                for eta in range(2):
                    sl = slice(eta * 512, (eta + 1) * 512)
                    qtk = gr("r3", phi * 1024 + eta * 512, phi * 1024 + (eta + 1) * 512)
                    ktk = gr("r5", phi * 1024 + eta * 512, phi * 1024 + (eta + 1) * 512)
                    em("v", lambda e, phi=phi, sl=sl: e.tensor_mul(tmpB[:], qr_v[:, phi, sl], kr_v[:, phi, sl]),
                       reads=qtk + ktk, writes=["tmpB"])
                    em("v", lambda e, phi=phi, eta=eta:
                       e.tensor_reduce(out=s_fh[:, phi, eta * 8:(eta + 1) * 8].unsqueeze(2),
                                       in_=tmpB[:].rearrange("p (g d) -> p g d", g=8),
                                       axis=AX.X, op=ALU.add),
                       reads=["tmpB"], writes=["s_fh"])
            if b == 0:
                prep_service(1)
        nyq_dst, nyq_tk = [(nyq_qr, "nyq_qr"), (nyq_kr, "nyq_kr"), (nyq_vr, "nyq_vr")][si]
        for tau in range(16):
            for eta in range(2):
                em("t", lambda e, tau=tau, eta=eta:
                   e.matmul(pw[3][0:1, eta * 512:(eta + 1) * 512], altcol_t[:],
                            sig[:, tau, eta * 512:(eta + 1) * 512],
                            start=(tau == 0), stop=(tau == 15)),
                   reads=["altcol"] + gr("r2", tau * 1024, (tau + 1) * 1024),
                   writes=["pw3"], inc=(tau == 15 and eta == 1))
        em("s", lambda e, nyq_dst=nyq_dst: e.copy(nyq_dst[:], pw[3][0:1, :]),
           reads=["pw3"], writes=[nyq_tk])
        if si == 0:
            em("v", lambda e: e.tensor_mul(tmpC[:], nyq_qr[:], nyq_qr[:]),
               reads=["nyq_qr"], writes=["tmpC"])
            em("s", lambda e: e.activation(tmpC[:], tmpC[:], AF.Sqrt),
               reads=["tmpC"], writes=["tmpC"])
            em("v", lambda e: e.tensor_reduce(out=enrow[:].unsqueeze(2),
                                              in_=tmpC[:].rearrange("p (g d) -> p g d", g=16),
                                              axis=AX.X, op=ALU.add),
               reads=["tmpC"], writes=["enrow"])
        if si == 1:
            em("v", lambda e: e.tensor_mul(tmpC[:], nyq_qr[:], nyq_kr[:]),
               reads=["nyq_qr", "nyq_kr"], writes=["tmpC"])
            em("v", lambda e: e.tensor_reduce(out=snrow[:].unsqueeze(2),
                                              in_=tmpC[:].rearrange("p (g d) -> p g d", g=16),
                                              axis=AX.X, op=ALU.add),
               reads=["tmpC"], writes=["snrow"])
        if debug and b == 0:
            dv, dr = {0: (qr_v, "r3"), 1: (kr_v, "r5"), 2: (vr_v, "r6")}[si]
            dn = {0: "d_qr", 1: "d_kr", 2: "d_vr"}[si]
            em("g", lambda e, dv=dv, dn=dn: e.dma_start(out=dbg[dn], in_=dv),
               reads=gr(dr, 0, 8192), dma=True)
            if si == 0:
                em("g", lambda e: e.dma_start(out=dbg["d_qi"], in_=qi_v), reads=gr("r4", 0, 8192), dma=True)
                em("g", lambda e: e.dma_start(out=dbg["d_qn"], in_=nyq_qr[:]), reads=["nyq_qr"], dma=True)

    def phase_middle_pre(b):
        for (src, srctk, dst, dsttk, nrow, nrtk, pwi) in [
                (e_fh, "e_fh", eT, "spA", enrow, "enrow", 2),
                (s_fh, "s_fh", scT, "spB", snrow, "snrow", 3)]:
            for phi in range(8):
                em("t", lambda e, src=src, phi=phi, pwi=pwi:
                   e.transpose(pw[pwi][0:16, phi * 128:(phi + 1) * 128],
                               src[:, phi, :], identf_t[:]),
                   reads=[srctk, "identf"], writes=[f"pw{pwi}"], inc=(phi == 7))
            em("s", lambda e, dst=dst, pwi=pwi: e.copy(dst[:, 0:1024], pw[pwi][0:16, :]),
               reads=[f"pw{pwi}"], writes=[dsttk])
            em("t", lambda e, nrow=nrow, pwi=pwi:
               e.transpose(pw[pwi][0:16, 0:1], nrow[:], identf_t[0:1, 0:1]),
               reads=[nrtk, "identf", dsttk], writes=[f"pw{pwi}"])
            em("s", lambda e, dst=dst, pwi=pwi: e.copy(dst[:, 1024:1025], pw[pwi][0:16, 0:1]),
               reads=[f"pw{pwi}"], writes=[dsttk])

    def phase_middle_a(b):
        for it in range(8):
            em("v", lambda e: e.max(out=max8[:], in_=eT), reads=["spA"], writes=["max8"])
            em("v", lambda e: e.match_replace(out=eT, in_to_replace=max8[:], in_values=eT,
                                              imm_value=-1.0),
               reads=["max8", "spA"], writes=["spA"])
        if debug and b == 0:
            em("g", lambda e: e.dma_start(out=dbg["d_eT"], in_=eT), reads=["spA"], dma=True)
            em("g", lambda e: e.dma_start(out=dbg["d_scT"], in_=scT), reads=["spB"], dma=True)
        em("v", lambda e: e.tensor_scalar(out=um, in0=eT, scalar1=0.0, scalar2=None,
                                          op0=ALU.is_ge), reads=["spA"], writes=["spD"])
        em("v", lambda e: e.tensor_scalar(out=scT, in0=scT, scalar1=C1, scalar2=None,
                                          op0=ALU.mult), reads=["spB"], writes=["spB"])
        em("v", lambda e: e.scalar_tensor_tensor(out=scT, in0=um, scalar=-1e9, in1=scT,
                                                 op0=ALU.mult, op1=ALU.add),
           reads=["spD", "spB"], writes=["spB"])
        em("v", lambda e: e.reduce_max(mxx[:], scT, axis=AX.X), reads=["spB"], writes=["mxx"])
        em("v", lambda e: e.tensor_scalar(out=negmx[:], in0=mxx[:], scalar1=-1.0, scalar2=None,
                                          op0=ALU.mult), reads=["mxx"], writes=["negmx"])
        em("s", lambda e: e.activation(scT, scT, AF.Exp, bias=negmx[:]),
           reads=["spB", "negmx"], writes=["spB"])
        em("v", lambda e: e.reduce_sum(smr[:], scT, axis=AX.X), reads=["spB"], writes=["smr"])
        em("v", lambda e: e.reciprocal(rsm[:], smr[:]), reads=["smr"], writes=["rsm"])
        em("v", lambda e: e.tensor_scalar(out=attnT, in0=scT, scalar1=rsm[:], scalar2=None,
                                          op0=ALU.mult), reads=["spB", "rsm"], writes=["spC"])

    def phase_middle_b(b):
        for (vsp, reg, nrow, nrtk, dst, dtk, pwi) in [
                (qr_v, "r3", nyq_qr, "nyq_qr", qmr, "qmr", 0),
                (qi_v, "r4", None, None, qmi, "qmi", 1)]:
            for phi in range(8):
                last = (phi == 7 and nrow is None)
                for eta in range(2):
                    em("t", lambda e, vsp=vsp, phi=phi, eta=eta, pwi=pwi, last=last:
                       e.matmul(pw[pwi][0:1, eta * 512:(eta + 1) * 512], onescol[:],
                                vsp[:, phi, eta * 512:(eta + 1) * 512],
                                start=(phi == 0), stop=last),
                       reads=gr(reg, phi * 1024, (phi + 1) * 1024) + ["onescol"],
                       writes=[f"pw{pwi}"], inc=(last and eta == 1))
            if nrow is not None:
                for eta in range(2):
                    em("t", lambda e, nrow=nrow, eta=eta, pwi=pwi:
                       e.matmul(pw[pwi][0:1, eta * 512:(eta + 1) * 512], onesrow[0:1, 0:1],
                                nrow[0:1, eta * 512:(eta + 1) * 512], start=False, stop=True),
                       reads=[nrtk, "onesrow"], writes=[f"pw{pwi}"], inc=(eta == 1))
            em("s", lambda e, dst=dst, pwi=pwi: e.mul(dst[:], pw[pwi][0:1, :], 1.0 / NF),
               reads=[f"pw{pwi}"], writes=[dtk])
        for (qrow, qrtk, qt, qttk, pwi) in [(qmr, "qmr", qtr, "qtr", 0), (qmi, "qmi", qti, "qti", 1)]:
            for eta in range(2):
                em("t", lambda e, qrow=qrow, eta=eta, pwi=pwi:
                   e.matmul(pw[pwi][0:16, eta * 512:(eta + 1) * 512], onesrow[0:1, 0:16],
                            qrow[0:1, eta * 512:(eta + 1) * 512], start=True, stop=True),
                   reads=[qrtk, "onesrow"], writes=[f"pw{pwi}"], inc=(eta == 1))
            em("v", lambda e, qt=qt, pwi=pwi: e.tensor_mul(qt[:], pw[pwi][0:16, 0:1024], eexp_t[:]),
               reads=[f"pw{pwi}", "eexp"], writes=[qttk])
        if debug and b == 0:
            em("g", lambda e: e.dma_start(out=dbg["d_attnT"], in_=attnT), reads=["spC"], dma=True)
            em("g", lambda e: e.dma_start(out=dbg["d_um"], in_=um), reads=["spD"], dma=True)
            em("g", lambda e: e.dma_start(out=dbg["d_qmr"], in_=qmr[:]), reads=["qmr"], dma=True)
            em("g", lambda e: e.dma_start(out=dbg["d_qmi"], in_=qmi[:]), reads=["qmi"], dma=True)
        for phi in range(8):
            for eta in range(2):
                em("t", lambda e, phi=phi, eta=eta:
                   e.matmul(pw[0][:, eta * 512:(eta + 1) * 512],
                            attnT[:, phi * 128:(phi + 1) * 128],
                            eexp_t[:, eta * 512:(eta + 1) * 512], start=True, stop=True),
                   reads=["spC", "eexp"], writes=["pw0"], inc=(eta == 1))
                em("t", lambda e, phi=phi, eta=eta:
                   e.matmul(pw[1][:, eta * 512:(eta + 1) * 512],
                            um[:, phi * 128:(phi + 1) * 128],
                            qtr[:, eta * 512:(eta + 1) * 512], start=True, stop=True),
                   reads=["spD", "qtr"], writes=["pw1"], inc=(eta == 1))
            em("t", lambda e, phi=phi:
               e.transpose(pwb[2][:, phi * 16:(phi + 1) * 16],
                           um[:, phi * 128:(phi + 1) * 128], identb_t[0:16, 0:16]),
               reads=["spD", "identb"], writes=["pw2"], inc=(phi == 7))
            for eta in range(2):
                sl = slice(eta * 512, (eta + 1) * 512)
                vtk = gr("r6", phi * 1024 + eta * 512, phi * 1024 + (eta + 1) * 512)
                xtk = gr("r3", phi * 1024 + eta * 512, phi * 1024 + (eta + 1) * 512)
                em("v", lambda e, phi=phi, sl=sl: e.tensor_mul(tmpA[:], pw[0][:, sl], vr_v[:, phi, sl]),
                   reads=["pw0"] + vtk, writes=["tmpA"])
                em("v", lambda e, phi=phi, sl=sl: e.tensor_add(xre_v[:, phi, sl], tmpA[:], pw[1][:, sl]),
                   reads=["tmpA", "pw1"], writes=xtk)
        em("s", lambda e: e.copy(umF[:], pwb[2][:, 0:128].rearrange("p (a c) -> p a c", a=8)),
           reads=["pw2"], writes=["umF"])
        for c in range(8):
            em("t", lambda e, c=c:
               e.transpose(pwb[3][:, c * 16:(c + 1) * 16],
                           qti[:, c * 128:(c + 1) * 128], identb_t[0:16, 0:16]),
               reads=["qti", "identb"], writes=["pw3"], inc=(c == 7))
        em("s", lambda e: e.copy(qtiT[:], pwb[3][:, 0:128].rearrange("p (a c) -> p a c", a=8)),
           reads=["pw3"], writes=["qtiT"])
        for eta in range(2):
            for c in range(8):
                em("t", lambda e, c=c, eta=eta:
                   e.matmul(pw[2][0:16, eta * 512:(eta + 1) * 512], qtiT[:, c, :],
                            wbuf[:, c, eta * 512:(eta + 1) * 512],
                            start=(c == 0), stop=(c == 7)),
                   reads=["qtiT"] + gr("st", c * 1024, (c + 1) * 1024),
                   writes=["pw2"], inc=(c == 7 and eta == 1))
        em("s", lambda e: e.copy(wqim[:], pw[2][0:16, :]), reads=["pw2"], writes=gr("r4", 2048, 3072))
        for eta in range(2):
            em("t", lambda e, eta=eta:
               e.matmul(pw[0][0:1, eta * 512:(eta + 1) * 512], attnT[:, 1024:1025],
                        eexp_t[:, eta * 512:(eta + 1) * 512], start=True, stop=True),
               reads=["spC", "eexp"], writes=["pw0"], inc=(eta == 1))
            em("t", lambda e, eta=eta:
               e.matmul(pw[1][0:1, eta * 512:(eta + 1) * 512], um[:, 1024:1025],
                        qtr[:, eta * 512:(eta + 1) * 512], start=True, stop=True),
               reads=["spD", "qtr"], writes=["pw1"], inc=(eta == 1))
        em("v", lambda e: e.tensor_mul(tmpC[:], pw[0][0:1, :], nyq_vr[:]),
           reads=["pw0", "nyq_vr"], writes=["tmpC"])
        em("v", lambda e: e.tensor_add(nyq_xre[:], tmpC[:], pw[1][0:1, :]),
           reads=["tmpC", "pw1"], writes=["nyq_xre"])
        if debug and b == 0:
            em("g", lambda e: e.dma_start(out=dbg["d_xre"], in_=xre_v), reads=gr("r3", 0, 8192), dma=True)

    def phase_irfft(b):
        for th in range(2):
            tsl = slice(th * 1024, (th + 1) * 1024)
            em("q", lambda e, tsl=tsl: e.dma_start(
                out=crci[:, 0:8, :],
                in_=c_crt[:, tsl].rearrange("(phi p) t -> p phi t", p=128)),
               writes=gr("r2", 0, 8192), dma=True)
            em("q", lambda e, tsl=tsl: e.dma_start(
                out=crci[:, 8:16, :],
                in_=c_cit[:, tsl].rearrange("(phi p) t -> p phi t", p=128)),
               writes=gr("r2", 8192, 16384), dma=True)
            em("q", lambda e, tsl=tsl: e.dma_start(out=crtn_t[:], in_=c_crtnyq[:, tsl]),
               writes=["crtn"], dma=True)
            for half in range(2):
                hsl = slice(half * 512, (half + 1) * 512)
                for phi in range(8):
                    em("t", lambda e, phi=phi, hsl=hsl, half=half:
                       e.matmul(pw[3][0:16, hsl], umF[:, phi, :], crci[:, 8 + phi, hsl],
                                start=(phi == 0), stop=(phi == 7)),
                       reads=["umF"] + gr("r2", 8192 + phi * 1024, 8192 + (phi + 1) * 1024),
                       writes=["pw3"], inc=(phi == 7 and half == 1))
            em("s", lambda e, th=th: e.copy(cmS[:, th * 1024:(th + 1) * 1024], pw[3][0:16, :]),
               reads=["pw3"], writes=gr("r4", 0, 2048))
            for chi in range(8):
                pwi = chi % 2
                for half in range(2):
                    hsl = slice(half * 512, (half + 1) * 512)
                    for phi in range(8):
                        em("t", lambda e, chi=chi, phi=phi, hsl=hsl, pwi=pwi:
                           e.matmul(pw[pwi][:, hsl], xre_v[:, phi, chi * 128:(chi + 1) * 128],
                                    crci[:, phi, hsl], start=(phi == 0), stop=False),
                           reads=gr("r3", phi * 1024 + chi * 128, phi * 1024 + (chi + 1) * 128) +
                           gr("r2", phi * 1024, (phi + 1) * 1024),
                           writes=[f"pw{pwi}"], inc=False)
                    em("t", lambda e, chi=chi, hsl=hsl, pwi=pwi, half=half:
                       e.matmul(pw[pwi][:, hsl], nyq_xre[0:1, chi * 128:(chi + 1) * 128],
                                crtn_t[0:1, hsl], start=False, stop=True),
                       reads=["nyq_xre", "crtn"], writes=[f"pw{pwi}"], inc=(half == 1))
                eng = "s" if chi % 2 == 0 else "v"
                em(eng, lambda e, chi=chi, th=th, pwi=pwi, eng=eng:
                   (e.copy if eng == "s" else e.tensor_copy)(
                       outT(chi)[:, th * 1024:(th + 1) * 1024], pw[pwi][:, :]),
                   reads=[f"pw{pwi}"], writes=outT_tok(chi, th * 1024, (th + 1) * 1024))
        if debug and b == 0:
            em("g", lambda e: e.dma_start(out=dbg["d_outT"][:, 0:4, :], in_=outT_lo),
               reads=gr("r5", 0, 8192), dma=True)
            em("g", lambda e: e.dma_start(out=dbg["d_outT"][:, 4:8, :], in_=outT_hi),
               reads=gr("r6", 0, 8192), dma=True)

    def prefetch_wo(b):
        for dc in range(8):
            em("q", lambda e, dc=dc:
               e.dma_start(out=wbuf[:, dc, :], in_=w_scr["Wo"][dc * 128:(dc + 1) * 128, :]),
               reads=["scrWo"], writes=gr("st", dc * 1024, (dc + 1) * 1024), dma=True)

    def phase_wo(b):
        for tau in range(16):
            pwi = tau % 4
            par = tau % 2
            for dc in range(8):
                for eta in range(2):
                    em("t", lambda e, tau=tau, dc=dc, eta=eta, pwi=pwi:
                       e.matmul(pw[pwi][:, eta * 512:(eta + 1) * 512],
                                outT(dc)[:, tau * 128:(tau + 1) * 128],
                                wbuf[:, dc, eta * 512:(eta + 1) * 512],
                                start=(dc == 0), stop=False),
                       reads=outT_tok(dc, tau * 128, (tau + 1) * 128) +
                       gr("st", dc * 1024, (dc + 1) * 1024),
                       writes=[f"pw{pwi}"], inc=False)
            for eta in range(2):
                em("t", lambda e, tau=tau, eta=eta, pwi=pwi:
                   e.matmul(pw[pwi][:, eta * 512:(eta + 1) * 512],
                            cmS[:, tau * 128:(tau + 1) * 128],
                            wqim[:, eta * 512:(eta + 1) * 512], start=False, stop=False),
                   reads=gr("r4", 0, 3072), writes=[f"pw{pwi}"], inc=False)
                em("t", lambda e, eta=eta, pwi=pwi:
                   e.matmul(pw[pwi][:, eta * 512:(eta + 1) * 512], onesrow[0:1, :],
                            bias_t["bo"][0:1, eta * 512:(eta + 1) * 512], start=False, stop=True),
                   reads=["onesrow", "bo"], writes=[f"pw{pwi}"], inc=(eta == 1))
            em("s", lambda e, pwi=pwi, par=par:
               e.copy(spE[:, par * 1024:(par + 1) * 1024], pw[pwi][:, :]),
               reads=[f"pw{pwi}"], writes=[f"spE{par}"])
            em("q", lambda e, tau=tau, par=par:
               e.dma_start(out=attn_scr[tau * 128:(tau + 1) * 128, :],
                           in_=spE[:, par * 1024:(par + 1) * 1024]),
               reads=[f"spE{par}"], writes=["attn_scr"], dma=True)
        if debug and b == 0:
            em("g", lambda e: e.dma_start(out=dbg["d_attn"], in_=attn_scr),
               reads=["attn_scr"], dma=True)

    def phase_ffn_gu(b, hclist):
        stg = [spA[:].bitcast(I32)[:, 0:1024], spB[:].bitcast(I32)[:, 0:1024],
               spC[:].bitcast(I32)[:, 0:1024], spD[:].bitcast(I32)[:, 0:1024]]
        stk = [["spA"], ["spB"], ["spC"], ["spD"]]
        for hc in hclist:
            par = hc % 2
            pg, pu = (0, 1) if par == 0 else (2, 3)
            em("q", lambda e, hc=hc, par=par:
               e.dma_start(out=stg[par].rearrange("p (dc c) -> p dc c", dc=8),
                           in_=wg_in[:, hc * 128:(hc + 1) * 128]
                           .rearrange("(dc p) c -> p dc c", p=128)),
               writes=stk[par], dma=True)
            em("s", lambda e, hc=hc, par=par:
               e.copy(wgslot[:, par, :], stg[par]),
               reads=stk[par], writes=gr("st", par * 1024, (par + 1) * 1024))
            em("q", lambda e, hc=hc, par=par:
               e.dma_start(out=stg[2 + par].rearrange("p (dc c) -> p dc c", dc=8),
                           in_=wu_in[:, hc * 128:(hc + 1) * 128]
                           .rearrange("(dc p) c -> p dc c", p=128)),
               writes=stk[2 + par], dma=True)
            em("v", lambda e, hc=hc, par=par:
               e.tensor_copy(wgslot[:, 2 + par, :], stg[2 + par]),
               reads=stk[2 + par], writes=gr("st", (2 + par) * 1024, (3 + par) * 1024))
            for th in range(2):
                wgv = wgslot[:, par, :].rearrange("p (dc c) -> p dc c", dc=8)
                wuv = wgslot[:, 2 + par, :].rearrange("p (dc c) -> p dc c", dc=8)
                for dc in range(8):
                    for tq in range(2):
                        em("t", lambda e, wgv=wgv, dc=dc, th=th, tq=tq, pg=pg:
                           e.matmul(pw[pg][:, tq * 512:(tq + 1) * 512], wgv[:, dc, :],
                                    xnT[:, dc, th * 1024 + tq * 512:th * 1024 + (tq + 1) * 512],
                                    start=(dc == 0), stop=(dc == 7)),
                           reads=gr("st", par * 1024, (par + 1) * 1024) +
                           [f"r1c{th * 8 + tq * 4 + i}" for i in range(4)],
                           writes=[f"pw{pg}"], inc=(dc == 7 and tq == 1))
                for dc in range(8):
                    for tq in range(2):
                        em("t", lambda e, wuv=wuv, dc=dc, th=th, tq=tq, pu=pu:
                           e.matmul(pw[pu][:, tq * 512:(tq + 1) * 512], wuv[:, dc, :],
                                    xnT[:, dc, th * 1024 + tq * 512:th * 1024 + (tq + 1) * 512],
                                    start=(dc == 0), stop=(dc == 7)),
                           reads=gr("st", (2 + par) * 1024, (3 + par) * 1024) +
                           [f"r1c{th * 8 + tq * 4 + i}" for i in range(4)],
                           writes=[f"pw{pu}"], inc=(dc == 7 and tq == 1))
                em("s", lambda e, par=par, pg=pg:
                   e.activation(siluslot[:, 4 + par, :], pw[pg][:, :], AF.Silu, scale=SC),
                   reads=[f"pw{pg}"], writes=gr("st", (4 + par) * 1024, (5 + par) * 1024))
                em("v", lambda e, hc=hc, th=th, par=par, pu=pu:
                   e.tensor_mul(hT(hc)[:, th * 1024:(th + 1) * 1024],
                                siluslot[:, 4 + par, :], pw[pu][:, :]),
                   reads=gr("st", (4 + par) * 1024, (5 + par) * 1024) + [f"pw{pu}"],
                   writes=hT_tok(hc, th * 1024, (th + 1) * 1024))
        if debug and b == 0:
            em("g", lambda e: e.dma_start(out=dbg["d_h"], in_=hTv[0]),
               reads=gr("r2", 0, 16384), dma=True)

    def phase_down_final(b):
        em("q", lambda e: e.dma_start(out=wdres_a,
                                      in_=wdT_s[0:2048, :].rearrange("(hc p) d -> p hc d", p=128)),
           reads=["scrwd"], writes=gr("r1", 0, 16384) + [f"r1c{t}" for t in range(16)], dma=True)
        em("q", lambda e: e.dma_start(out=wdres_b,
                                      in_=wdT_s[2048:3072, :].rearrange("(hc p) d -> p hc d", p=128)),
           reads=["scrwd"], writes=gr("st", 0, 8192), dma=True)
        for tp in range(8):
            taus = (2 * tp, 2 * tp + 1)
            for i, tau in enumerate(taus):
                pwi = 2 * (tp % 2) + i
                for hc in range(24):
                    for eta in range(2):
                        em("t", lambda e, tau=tau, hc=hc, eta=eta, pwi=pwi:
                           e.matmul(pw[pwi][:, eta * 512:(eta + 1) * 512],
                                    hT(hc)[:, tau * 128:(tau + 1) * 128],
                                    wdres(hc)[:, eta * 512:(eta + 1) * 512],
                                    start=(hc == 0), stop=(hc == 23)),
                           reads=hT_tok(hc, tau * 128, (tau + 1) * 128) + wdres_tok(hc),
                           writes=[f"pw{pwi}"], inc=(hc == 23 and eta == 1))
            for i, tau in enumerate(taus):
                pwi = 2 * (tp % 2) + i
                par = tau % 2
                sp = ("spA", "spB")[par]
                op_ = ("spC", "spD")[par]
                em("q", lambda e, b=b, tau=tau, par=par:
                   e.dma_start(out=xstage[par], in_=x_in[b, tau * 128:(tau + 1) * 128, :]),
                   writes=[sp], dma=True)
                em("q", lambda e, tau=tau, par=par:
                   e.dma_start(out=spE[:, par * 1024:(par + 1) * 1024],
                               in_=attn_scr[tau * 128:(tau + 1) * 128, :]),
                   reads=["attn_scr"], writes=[f"spE{par}"], dma=True)
                for eta in range(2):
                    sl = slice(eta * 512, (eta + 1) * 512)
                    em("v", lambda e, pwi=pwi, par=par, sl=sl:
                       e.scalar_tensor_tensor(out=ostage[par][:, sl], in0=pw[pwi][:, sl],
                                              scalar=SC * SC, in1=xstage[par][:, sl],
                                              op0=ALU.mult, op1=ALU.add),
                       reads=[f"pw{pwi}", sp], writes=[op_])
                    em("v", lambda e, par=par, sl=sl:
                       e.tensor_add(ostage[par][:, sl], ostage[par][:, sl],
                                    spE[:, par * 1024:(par + 1) * 1024][:, sl]),
                       reads=[op_, f"spE{par}"], writes=[op_])
                em("sd", lambda e, b=b, tau=tau, par=par:
                   e.dma_start(out=out_d[b, tau * 128:(tau + 1) * 128, :], in_=ostage[par]),
                   reads=[op_], dma=True)

    for b in range(NB):
        if b == 0:
            prep_attn("Wq")
            prefetch_wbuf(b, 0)
            phase_ln(b)
            phase_proj(b, 0)
            prep_attn_pend("Wk")
            prep_attn_pend("Wv")
        else:
            phase_ln_proj(b)
        phase_fft(b, 0)
        if b == 0:
            prep_flush()
        prefetch_wbuf(b, 1)
        if b == 0:
            prep_attn_pend("Wo")
            prep_pending.extend((src, dst, tk, True) for (src, dst, tk) in ffn_chunks)
        phase_proj(b, 1)
        phase_fft(b, 1)
        prefetch_wbuf(b, 2)
        phase_proj(b, 2)
        if b == 0:
            prep_service(99)
            prep_flush()
        phase_middle_pre(b)
        phase_middle_a(b)
        phase_fft(b, 2)
        prefetch_wo(b)
        phase_middle_b(b)
        phase_irfft(b)
        phase_wo(b)
        phase_ffn_gu(b, range(0, 24))
        phase_down_final(b)

    fin_q, fin_g, fin_sd = S.cnt["q"], S.cnt["g"], S.cnt["sd"]
    S.emit("q", lambda e: e.wait_ge(semG, fin_g), inc=False)
    S.emit("q", lambda e: e.wait_ge(semSD, fin_sd), inc=False)
    S.emit("g", lambda e: e.wait_ge(semQ, fin_q), inc=False)

    def run_engine(skey, eng):
        for (ekey, (fn, waits, inc, amt, tag)) in S.streams[skey]:
            for (pe, pv) in waits:
                eng.wait_ge(SEMS[pe], pv)
            if fn is None:
                continue
            inst = fn(eng)
            if inc and inst is not None:
                inst.then_inc(SEMS[ekey], amt)

    @block.sync
    def _(eng):
        run_engine("q", eng)

    @block.tensor
    def _(eng):
        run_engine("t", eng)

    @block.vector
    def _(eng):
        run_engine("v", eng)

    @block.scalar
    def _(eng):
        run_engine("s", eng)

    @block.gpsimd
    def _(eng):
        run_engine("g", eng)

    stack.close()
    return nc, S.nops


from concourse.bass_utils import run_bass_kernel_spmd

_CACHED = {}


def kernel(**inputs):
    x = np.ascontiguousarray(inputs["L0_attn_query"], dtype=np.float32)
    if "prog" not in _CACHED:
        _CACHED["prog"] = build_program(debug=False)[0]
        _CACHED["consts"] = build_consts()
    nc = _CACHED["prog"]
    consts = _CACHED["consts"]
    shard = x.reshape(8, NB, L, D)
    base = dict(
        Wq=np.ascontiguousarray(inputs["L0_attn_Wq"], np.float32),
        Wk=np.ascontiguousarray(inputs["L0_attn_Wk"], np.float32),
        Wv=np.ascontiguousarray(inputs["L0_attn_Wv"], np.float32),
        Wo=np.ascontiguousarray(inputs["L0_attn_Wo"], np.float32),
        bq=np.ascontiguousarray(inputs["L0_attn_bq"], np.float32).reshape(1, D),
        bk=np.ascontiguousarray(inputs["L0_attn_bk"], np.float32).reshape(1, D),
        bv=np.ascontiguousarray(inputs["L0_attn_bv"], np.float32).reshape(1, D),
        bo=np.ascontiguousarray(inputs["L0_attn_bo"], np.float32).reshape(1, D),
        wg=np.ascontiguousarray(np.asarray(inputs["L0_ffn_w_gate"], np.int32).T),
        wu=np.ascontiguousarray(np.asarray(inputs["L0_ffn_w_up"], np.int32).T),
        wd=np.ascontiguousarray(np.asarray(inputs["L0_ffn_w_down"], np.int32).T),
    )
    for k, v in consts.items():
        base[k] = np.ascontiguousarray(v)
    in_maps = [dict(base, x=np.ascontiguousarray(shard[i])) for i in range(8)]
    res = run_bass_kernel_spmd(nc, in_maps, core_ids=list(range(8)))
    out = np.concatenate(
        [np.asarray(res.results[i]["out"], dtype=np.float32) for i in range(8)], axis=0)
    return out.reshape(32, L, D)


# version 26
# speedup vs baseline: 1.0118x; 1.0069x over previous
import numpy as np
import ml_dtypes
import concourse.bass as bass
import concourse.mybir as mybir

F32 = mybir.dt.float32
BF16 = mybir.dt.bfloat16
I32 = mybir.dt.int32
AF = mybir.ActivationFunctionType
ALU = mybir.AluOpType
AX = mybir.AxisListType

NB = 4
L = 2048
D = 1024
H = 16
DH = 64
NF = 1025
HID = 3072
SC = float(1.0 / 127.0)
C1 = float(1.0 / (np.sqrt(DH) + 1e-8))
ENGS = ("t", "v", "s", "q", "g", "sd")
STREAM_OF = {"sd": "s"}


class Sched:

    def __init__(self):
        self.ops = {e: [] for e in ENGS}
        self.cnt = {e: 0 for e in ENGS}
        self.seen = {e: {p: 0 for p in ENGS} for e in ENGS}
        self.snap = {e: {0: {p: 0 for p in ENGS}} for e in ENGS}
        self.resw = {}
        self.resr = {}
        self.pend = {e: [] for e in ENGS}
        self.pend_f = {e: [] for e in ENGS}
        self.streams = {e: [] for e in ENGS if e not in STREAM_OF}
        self.nops = 0

    def _push(self, e, entry):
        self.ops[e].append(entry)
        self.streams[STREAM_OF.get(e, e)].append((e, entry))

    def emit_fence(self, e):
        bnd = self.cnt[e]
        self._push(e, (None, [(e, bnd)], False, 0, "dmafence"))
        self.seen[e][e] = bnd
        clk = dict(self.seen[e])
        clk[e] = bnd
        self.snap[e][bnd] = clk
        for kind, tk in self.pend_f[e]:
            if kind == "w":
                if self.resw.get(tk, (None, 0))[1] is None and self.resw[tk][0] == e:
                    self.resw[tk] = (e, bnd)
            else:
                lst = self.resr.get(tk, [])
                for i, (re_, rv) in enumerate(lst):
                    if re_ == e and rv is None:
                        lst[i] = (e, bnd)
        self.pend_f[e] = []

    def emit(self, e, fn, reads=(), writes=(), inc=True, dma=False, fence=True, tag=""):
        waits = {}

        def need(pe, pv):
            if pv is None:
                assert pe == e, f"cross-engine dep on un-inc'd op, tag={tag}"
                return
            if self.seen[e].get(pe, 0) < pv:
                waits[pe] = max(waits.get(pe, 0), pv)

        for tk in reads:
            w = self.resw.get(tk)
            if w is not None:
                need(w[0], w[1])
        for tk in writes:
            w = self.resw.get(tk)
            if w is not None:
                need(w[0], w[1])
            for (re_, rv) in self.resr.get(tk, ()):
                need(re_, rv)
        for pe in list(waits):
            pv = waits[pe]
            self.seen[e][pe] = max(self.seen[e][pe], pv)
            sn = self.snap[pe].get(pv)
            if sn:
                for k2, v2 in sn.items():
                    if self.seen[e].get(k2, 0) < v2:
                        self.seen[e][k2] = v2
        amt = 16 if dma else 1
        self._push(e, (fn, sorted(waits.items()), inc, amt, tag))
        self.nops += 1
        if inc:
            self.cnt[e] += amt
            nc_ = self.cnt[e]
            for kind, tk in self.pend[e]:
                if kind == "w":
                    if self.resw.get(tk, (None, 0))[1] is None and self.resw[tk][0] == e:
                        self.resw[tk] = (e, nc_)
                else:
                    lst = self.resr.get(tk, [])
                    for i, (re_, rv) in enumerate(lst):
                        if re_ == e and rv is None:
                            lst[i] = (e, nc_)
            self.pend[e] = []
            clk = dict(self.seen[e])
            clk[e] = nc_
            self.snap[e][nc_] = clk
        for tk in writes:
            if inc:
                self.resw[tk] = (e, self.cnt[e])
            else:
                self.resw[tk] = (e, None)
                self.pend[e].append(("w", tk))
            self.resr[tk] = []
        for tk in reads:
            lst = self.resr.setdefault(tk, [])
            if inc:
                lst.append((e, self.cnt[e]))
            else:
                lst.append((e, None))
                self.pend[e].append(("r", tk))
        if dma and inc:
            if fence:
                self._push(e, (None, [(e, self.cnt[e])], False, 0, "dmafence"))
                self.seen[e][e] = self.cnt[e]
                bnd = self.cnt[e]
                for kind, tk in self.pend_f[e]:
                    if kind == "w":
                        if self.resw.get(tk, (None, 0))[1] is None and self.resw[tk][0] == e:
                            self.resw[tk] = (e, bnd)
                    else:
                        lst = self.resr.get(tk, [])
                        for i, (re_, rv) in enumerate(lst):
                            if re_ == e and rv is None:
                                lst[i] = (e, bnd)
                self.pend_f[e] = []
            else:
                for tk in writes:
                    self.resw[tk] = (e, None)
                    self.pend_f[e].append(("w", tk))
                for tk in reads:
                    lst = self.resr.setdefault(tk, [])
                    for i in range(len(lst) - 1, -1, -1):
                        if lst[i] == (e, self.cnt[e]):
                            lst[i] = (e, None)
                            break
                    self.pend_f[e].append(("r", tk))


def gr(reg, lo, hi):
    return [f"{reg}g{i}" for i in range(lo // 1024, (hi + 1023) // 1024)]


def build_consts():
    n = np.arange(L)
    k = np.arange(D)
    ph = 2 * np.pi * np.outer(n, k) / L
    dft_cos = np.cos(ph).astype(ml_dtypes.bfloat16)
    dft_sinn = (-np.sin(ph)).astype(ml_dtypes.bfloat16)
    w = np.full(D, 2.0, np.float64)
    w[0] = 1.0
    phT = 2 * np.pi * np.outer(np.arange(D), n) / L
    crt = ((w[:, None] / L) * np.cos(phT)).astype(ml_dtypes.bfloat16)
    cit = (-(w[:, None] / L) * np.sin(phT)).astype(ml_dtypes.bfloat16)
    crt_nyq = ((1.0 / L) * np.where(n % 2 == 0, 1.0, -1.0)).astype(ml_dtypes.bfloat16)[None, :]
    altcol = np.where(np.arange(128) % 2 == 0, 1.0, -1.0).astype(ml_dtypes.bfloat16)[:, None]
    eexp = np.zeros((H, D), ml_dtypes.bfloat16)
    for h in range(H):
        eexp[h, h * DH:(h + 1) * DH] = 1.0
    identb = np.eye(128, dtype=ml_dtypes.bfloat16)
    identf = np.eye(128, dtype=np.float32)
    return dict(dft_cos=dft_cos, dft_sinn=dft_sinn, crt=crt, cit=cit,
                crt_nyq=crt_nyq, altcol=altcol, eexp=eexp,
                identb=identb, identf=identf)


def build_program(debug=False):
    nc = bass.Bass("TRN2", target_bir_lowering=False, debug=False)
    S = Sched()
    em = S.emit

    x_in = nc.dram_tensor("x", [NB, L, D], F32, kind="ExternalInput").ap()
    W_ins = {nm: nc.dram_tensor(nm, [D, D], F32, kind="ExternalInput").ap()
             for nm in ("Wq", "Wk", "Wv", "Wo")}
    b_ins = {nm: nc.dram_tensor(nm, [1, D], F32, kind="ExternalInput").ap()
             for nm in ("bq", "bk", "bv", "bo")}
    wg_in = nc.dram_tensor("wg", [D, HID], I32, kind="ExternalInput").ap()
    wu_in = nc.dram_tensor("wu", [D, HID], I32, kind="ExternalInput").ap()
    wd_in = nc.dram_tensor("wd", [HID, D], I32, kind="ExternalInput").ap()
    c_cos = nc.dram_tensor("dft_cos", [L, D], BF16, kind="ExternalInput").ap()
    c_sinn = nc.dram_tensor("dft_sinn", [L, D], BF16, kind="ExternalInput").ap()
    c_crt = nc.dram_tensor("crt", [D, L], BF16, kind="ExternalInput").ap()
    c_cit = nc.dram_tensor("cit", [D, L], BF16, kind="ExternalInput").ap()
    c_crtnyq = nc.dram_tensor("crt_nyq", [1, L], BF16, kind="ExternalInput").ap()
    c_altcol = nc.dram_tensor("altcol", [128, 1], BF16, kind="ExternalInput").ap()
    c_eexp = nc.dram_tensor("eexp", [H, D], BF16, kind="ExternalInput").ap()
    c_identb = nc.dram_tensor("identb", [128, 128], BF16, kind="ExternalInput").ap()
    c_identf = nc.dram_tensor("identf", [128, 128], F32, kind="ExternalInput").ap()
    out_d = nc.dram_tensor("out", [NB, L, D], F32, kind="ExternalOutput").ap()

    w_scr = {nm: nc.dram_tensor(nm.lower() + "_s", [D, D], BF16, kind="Internal").ap()
             for nm in ("Wq", "Wk", "Wv", "Wo")}
    wgT_s = nc.dram_tensor("wgT_s", [D, HID], BF16, kind="Internal").ap()
    wuT_s = nc.dram_tensor("wuT_s", [D, HID], BF16, kind="Internal").ap()
    wdT_s = nc.dram_tensor("wdT_s", [HID, D], BF16, kind="Internal").ap()
    attn_scr = nc.dram_tensor("attn_scr", [L, D], BF16, kind="Internal").ap()

    dbg = {}
    if debug:
        for nm, shp, dt in [("d_xnT", [128, 8, L], BF16), ("d_qr", [128, 8, D], BF16),
                            ("d_qi", [128, 8, D], BF16), ("d_kr", [128, 8, D], BF16),
                            ("d_vr", [128, 8, D], BF16), ("d_eT", [H, NF], F32),
                            ("d_scT", [H, NF], F32), ("d_attnT", [H, NF], BF16),
                            ("d_um", [H, NF], BF16), ("d_xre", [128, 8, D], BF16),
                            ("d_xim", [128, 8, D], BF16), ("d_outT", [128, 8, L], BF16),
                            ("d_attn", [L, D], BF16), ("d_qn", [1, D], BF16),
                            ("d_qmr", [1, D], BF16), ("d_qmi", [1, D], BF16),
                            ("d_sig", [128, 16, D], BF16), ("d_h", [128, 8, L], BF16)]:
            dbg[nm] = nc.dram_tensor(nm, shp, dt, kind="ExternalOutput").ap()

    from contextlib import ExitStack
    stack = ExitStack()
    E = stack.enter_context

    r1 = E(nc.sbuf_tensor("r1", [128, 16384], BF16))
    r2 = E(nc.sbuf_tensor("r2", [128, 16384], BF16))
    r3 = E(nc.sbuf_tensor("r3", [128, 8192], BF16))
    r4 = E(nc.sbuf_tensor("r4", [128, 8192], BF16))
    r5 = E(nc.sbuf_tensor("r5", [128, 8192], BF16))
    r6 = E(nc.sbuf_tensor("r6", [128, 8192], BF16))
    strm = E(nc.sbuf_tensor("strm", [128, 8192], BF16))
    spA = E(nc.sbuf_tensor("spA", [128, 1040], F32))
    spB = E(nc.sbuf_tensor("spB", [128, 1040], F32))
    spC = E(nc.sbuf_tensor("spC", [128, 1040], F32))
    spD = E(nc.sbuf_tensor("spD", [128, 1040], F32))
    spE = E(nc.sbuf_tensor("spE", [128, 2048], BF16))
    e_fh = E(nc.sbuf_tensor("e_fh", [128, 8, 16], F32))
    s_fh = E(nc.sbuf_tensor("s_fh", [128, 8, 16], F32))
    max8 = E(nc.sbuf_tensor("max8", [16, 8], F32))
    enrow = E(nc.sbuf_tensor("enrow", [1, 16], F32))
    snrow = E(nc.sbuf_tensor("snrow", [1, 16], F32))
    mxx = E(nc.sbuf_tensor("mxx", [16, 1], F32))
    negmx = E(nc.sbuf_tensor("negmx", [16, 1], F32))
    smr = E(nc.sbuf_tensor("smr", [16, 1], F32))
    rsm = E(nc.sbuf_tensor("rsm", [16, 1], F32))
    msum = [E(nc.sbuf_tensor(f"msum{i}", [128, 1], F32)) for i in range(4)]
    ssq = [E(nc.sbuf_tensor(f"ssq{i}", [128, 1], F32)) for i in range(4)]
    meanv = [E(nc.sbuf_tensor(f"meanv{i}", [128, 1], F32)) for i in range(4)]
    m2v = [E(nc.sbuf_tensor(f"m2v{i}", [128, 1], F32)) for i in range(4)]
    varv = [E(nc.sbuf_tensor(f"varv{i}", [128, 1], F32)) for i in range(4)]
    sdv = [E(nc.sbuf_tensor(f"sdv{i}", [128, 1], F32)) for i in range(4)]
    rstd = [E(nc.sbuf_tensor(f"rstd{i}", [128, 1], F32)) for i in range(4)]
    epsv = E(nc.sbuf_tensor("epsv", [128, 1], F32))
    tmpA = E(nc.sbuf_tensor("tmpA", [128, 512], F32))
    tmpB = E(nc.sbuf_tensor("tmpB", [128, 512], F32))
    tmpC = E(nc.sbuf_tensor("tmpC", [1, 1024], F32))
    qmr = E(nc.sbuf_tensor("qmr", [1, 1024], BF16))
    qmi = E(nc.sbuf_tensor("qmi", [1, 1024], BF16))
    qtr = E(nc.sbuf_tensor("qtr", [16, 1024], BF16))
    qti = E(nc.sbuf_tensor("qti", [16, 1024], BF16))
    umF = E(nc.sbuf_tensor("umF", [128, 8, 16], BF16))
    qtiT = E(nc.sbuf_tensor("qtiT", [128, 8, 16], BF16))
    eexp_t = E(nc.sbuf_tensor("eexp_t", [16, 1024], BF16))
    identb_t = E(nc.sbuf_tensor("identb_t", [128, 128], BF16))
    identf_t = E(nc.sbuf_tensor("identf_t", [128, 128], F32))
    altcol_t = E(nc.sbuf_tensor("altcol_t", [128, 1], BF16))
    onesrow = E(nc.sbuf_tensor("onesrow", [1, 128], BF16))
    onescol = E(nc.sbuf_tensor("onescol", [128, 1], BF16))
    bias_t = {nm: E(nc.sbuf_tensor(nm + "_t", [1, 1024], BF16))
              for nm in ("bq", "bk", "bv", "bo")}
    nyq_qr = E(nc.sbuf_tensor("nyq_qr", [1, 1024], BF16))
    nyq_kr = E(nc.sbuf_tensor("nyq_kr", [1, 1024], BF16))
    nyq_vr = E(nc.sbuf_tensor("nyq_vr", [1, 1024], BF16))
    nyq_xre = E(nc.sbuf_tensor("nyq_xre", [1, 1024], BF16))
    crtn_t = E(nc.sbuf_tensor("crtn_t", [1, 1024], BF16))
    crtn_b = E(nc.sbuf_tensor("crtn_b", [1, 1024], BF16))
    pw = [E(nc.psum_tensor(f"pw{i}", [128, 1024], F32)) for i in range(4)]

    semT = E(nc.semaphore("semT"))
    semV = E(nc.semaphore("semV"))
    semS = E(nc.semaphore("semS"))
    semQ = E(nc.semaphore("semQ"))
    semG = E(nc.semaphore("semG"))
    semSD = E(nc.semaphore("semSD"))
    SEMS = {"t": semT, "v": semV, "s": semS, "q": semQ, "g": semG, "sd": semSD}
    block = E(nc.Block())

    def rv(t, a, b):
        return t[:, 0:a * b].rearrange("p (a b) -> p a b", a=a)

    xnT = rv(r1, 8, 2048)
    sig = rv(r2, 16, 1024)
    crci = rv(r2, 16, 1024)
    qr_v = rv(r3, 8, 1024)
    qi_v = rv(r4, 8, 1024)
    kr_v = rv(r5, 8, 1024)
    vr_v = rv(r6, 8, 1024)
    xre_v = qr_v
    xim_v = qi_v
    cmS = r4[0:16, 0:2048]
    wqim = r4[0:16, 2048:3072]
    outT_lo = rv(r5, 4, 2048)
    outT_hi = rv(r6, 4, 2048)

    def outT(chi):
        return outT_lo[:, chi, :] if chi < 4 else outT_hi[:, chi - 4, :]

    def outT_tok(chi, lo=0, hi=2048):
        reg = "r5" if chi < 4 else "r6"
        base = (chi % 4) * 2048
        return gr(reg, base + lo, base + hi)

    hTv = [rv(r2, 8, 2048), rv(r3, 4, 2048), rv(r4, 4, 2048),
           rv(r5, 4, 2048), rv(r6, 4, 2048)]

    def hT(hc):
        if hc < 8:
            return hTv[0][:, hc, :]
        return hTv[1 + (hc - 8) // 4][:, (hc - 8) % 4, :]

    def hT_tok(hc, lo=0, hi=2048):
        reg = "r2" if hc < 8 else ("r3", "r4", "r5", "r6")[(hc - 8) // 4]
        base = (hc * 2048 if hc < 8 else ((hc - 8) % 4) * 2048)
        return gr(reg, base + lo, base + hi)

    wdres_a = rv(r1, 16, 1024)
    wdres_b = rv(strm, 8, 1024)

    def wdres(hc):
        return wdres_a[:, hc, :] if hc < 16 else wdres_b[:, hc - 16, :]

    def wdres_tok(hc):
        return gr("r1", hc * 1024, (hc + 1) * 1024) if hc < 16 else \
            gr("st", (hc - 16) * 1024, (hc - 15) * 1024)

    wbuf = rv(strm, 8, 1024)
    cosslot = rv(strm, 32, 128)
    sinslot4 = rv(strm, 64, 128)
    wgslot = rv(strm, 8, 1024)
    siluslot = rv(strm, 8, 1024)

    eT = spA[0:16, 0:1025]
    scT = spB[0:16, 0:1025]
    attnT = spC[:].bitcast(BF16)[0:16, 0:1025]
    um = spD[:].bitcast(BF16)[0:16, 0:1025]
    xstage = [spA[:, 0:1024], spB[:, 0:1024]]
    ostage = [spC[:, 0:1024], spD[:, 0:1024]]
    istage = [spA[:].bitcast(I32)[:, 0:1024], spB[:].bitcast(I32)[:, 0:1024]]
    bstage = [spC[:].bitcast(BF16)[:, 0:2048], spD[:].bitcast(BF16)[:, 0:2048]]
    lnx = [r5[:].bitcast(F32)[:, i * 1024:(i + 1) * 1024] for i in range(4)]
    LNX_TK = [gr("r5", i * 2048, (i + 1) * 2048) for i in range(4)]
    lnxn = [r6[:, 0:1024], r6[:, 1024:2048], r6[:, 6144:7168], r6[:, 7168:8192]]
    LNXN_TK = [gr("r6", 0, 1024), gr("r6", 1024, 2048),
               gr("r6", 6144, 7168), gr("r6", 7168, 8192)]
    sqdump = [r6[:].bitcast(F32)[:, 1024:2048], r6[:].bitcast(F32)[:, 2048:3072]]
    SQD_TK = [gr("r6", 2048, 4096), gr("r6", 4096, 6144)]
    pwb = [p[:].bitcast(BF16) for p in pw]

    em("q", lambda e: e.dma_start(out=identb_t[:], in_=c_identb), writes=["identb"], dma=True)
    em("q", lambda e: e.dma_start(out=identf_t[:], in_=c_identf), writes=["identf"], dma=True)
    em("q", lambda e: e.dma_start(out=altcol_t[:], in_=c_altcol), writes=["altcol"], dma=True)
    em("q", lambda e: e.dma_start(out=eexp_t[:], in_=c_eexp), writes=["eexp"], dma=True)
    em("v", lambda e: e.memset(onesrow[:], 1.0), writes=["onesrow"])
    em("v", lambda e: e.memset(epsv[:], 1e-5), writes=["epsv"])
    em("v", lambda e: e.memset(onescol[:], 1.0), writes=["onescol"])
    for nm in ("bq", "bk", "bv", "bo"):
        em("q", lambda e, nm=nm: e.dma_start(out=lnx[0][0:1, :], in_=b_ins[nm]),
           writes=LNX_TK[0], dma=True)
        em("v", lambda e, nm=nm: e.tensor_copy(bias_t[nm][:], lnx[0][0:1, :]),
           reads=LNX_TK[0], writes=[nm])
    sp4 = [spA[:, 0:1024], spB[:, 0:1024], spC[:, 0:1024], spD[:, 0:1024]]
    sp4i = [t.bitcast(I32) for t in sp4]
    SP4 = [["spA"], ["spB"], ["spC"], ["spD"]]
    spEh = [spE[:, 0:1024], spE[:, 1024:2048]]
    prep_n = {"i": 0}
    prep_pipe = []

    def prep_one(src_ap, dst_ap, dtk, int32):
        i = prep_n["i"]
        prep_n["i"] += 1
        par, ph = i % 4, i % 2
        it = sp4i[par] if int32 else sp4[par]
        em("q", lambda e, it=it, src_ap=src_ap:
           e.dma_start(out=it, in_=src_ap), writes=SP4[par], dma=True)

        def cast_store(i=i, it=it, par=par, ph=ph, dst_ap=dst_ap, dtk=dtk):
            em("v" if i % 2 == 0 else "s", lambda e, it=it, ph=ph, i=i:
               (e.tensor_copy if i % 2 == 0 else e.copy)(spEh[ph], it),
               reads=SP4[par], writes=[f"spE{ph}"])
            em("q", lambda e, ph=ph, dst_ap=dst_ap: e.dma_start(out=dst_ap, in_=spEh[ph]),
               reads=[f"spE{ph}"], writes=[dtk], dma=True)

        prep_pipe.append(cast_store)
        if len(prep_pipe) > 3:
            prep_pipe.pop(0)()

    def prep_flush():
        while prep_pipe:
            prep_pipe.pop(0)()

    prep_pending = []

    def prep_service(k):
        for _ in range(k):
            if prep_pending:
                prep_one(*prep_pending.pop(0))

    def prep_attn(nm):
        for dc in range(8):
            prep_one(W_ins[nm][dc * 128:(dc + 1) * 128, :],
                     w_scr[nm][dc * 128:(dc + 1) * 128, :], f"scr{nm}", False)
        prep_flush()

    def prep_attn_pend(nm):
        for dc in range(8):
            prep_pending.append((W_ins[nm][dc * 128:(dc + 1) * 128, :],
                                 w_scr[nm][dc * 128:(dc + 1) * 128, :], f"scr{nm}", False))

    def prep_chunk(idx, src_ap, dst_ap, dtk):
        prep_one(src_ap, dst_ap, dtk, True)

    ffn_chunks = []
    for hc in range(24):
        ffn_chunks.append((wd_in[hc * 128:(hc + 1) * 128, :], wdT_s[hc * 128:(hc + 1) * 128, :], "scrwd"))
    prep_state = {"i": 0}

    def prep_some(n):
        i0 = prep_state["i"]
        for i in range(i0, min(i0 + n, len(ffn_chunks))):
            prep_chunk(i, *ffn_chunks[i])
        prep_state["i"] = min(i0 + n, len(ffn_chunks))
        prep_flush()

    def phase_ln(b):
        for tau in range(16):
            par = tau % 4
            pwi = 2 + (tau % 2)
            em("q", lambda e, b=b, tau=tau, par=par:
               e.dma_start(out=lnx[par], in_=x_in[b, tau * 128:(tau + 1) * 128, :]),
               writes=LNX_TK[par], dma=True, tag=f"lnx{tau}")
            em("v", lambda e, par=par: e.reduce_sum(msum[par][:], lnx[par], axis=AX.X),
               reads=LNX_TK[par], writes=[f"msum{par}"])
            em("s", lambda e, par=par: e.activation(sqdump[par % 2], lnx[par], AF.Square,
                                                    accum_out=ssq[par][:]),
               reads=LNX_TK[par], writes=[f"ssq{par}"] + SQD_TK[par % 2])
            em("v", lambda e, par=par: e.tensor_scalar(out=meanv[par][:], in0=msum[par][:],
                                                       scalar1=1.0 / D, scalar2=None, op0=ALU.mult),
               reads=[f"msum{par}"], writes=[f"meanv{par}"])
            em("v", lambda e, par=par: e.tensor_mul(m2v[par][:], meanv[par][:], meanv[par][:]),
               reads=[f"meanv{par}"], writes=[f"m2v{par}"])
            em("v", lambda e, par=par: e.scalar_tensor_tensor(out=varv[par][:], in0=ssq[par][:],
                                                              scalar=1.0 / D, in1=m2v[par][:],
                                                              op0=ALU.mult, op1=ALU.subtract),
               reads=[f"ssq{par}", f"m2v{par}"], writes=[f"varv{par}"])
            em("s", lambda e, par=par: e.activation(sdv[par][:], varv[par][:], AF.Sqrt, bias=epsv[:]),
               reads=[f"varv{par}", "epsv"], writes=[f"sdv{par}"])
            em("v", lambda e, par=par: e.reciprocal(rstd[par][:], sdv[par][:]),
               reads=[f"sdv{par}"], writes=[f"rstd{par}"])
            em("v", lambda e, par=par: e.tensor_scalar(out=lnxn[par], in0=lnx[par],
                                                       scalar1=meanv[par][:], scalar2=rstd[par][:],
                                                       op0=ALU.subtract, op1=ALU.mult),
               reads=LNX_TK[par] + [f"meanv{par}", f"rstd{par}"], writes=LNXN_TK[par])
            for dc in range(8):
                em("t", lambda e, par=par, dc=dc, pwi=pwi:
                   e.transpose(pwb[pwi][:, dc * 128:(dc + 1) * 128],
                               lnxn[par][:, dc * 128:(dc + 1) * 128], identb_t[:]),
                   reads=LNXN_TK[par] + ["identb"], writes=[f"pw{pwi}"], inc=(dc == 7))
            em("s", lambda e, tau=tau, pwi=pwi:
               e.copy(xnT[:, :, tau * 128:(tau + 1) * 128],
                      pwb[pwi][:, 0:1024].rearrange("p (dc c) -> p dc c", dc=8)),
               reads=[f"pw{pwi}"], writes=[f"r1c{tau}"])
        if debug and b == 0:
            em("g", lambda e: e.dma_start(out=dbg["d_xnT"], in_=xnT),
               reads=[f"r1c{t}" for t in range(16)], dma=True)

    def prefetch_wbuf(b, si):
        nm = ("Wq", "Wk", "Wv")[si]
        for dc in range(8):
            em("q", lambda e, nm=nm, dc=dc:
               e.dma_start(out=wbuf[:, dc, :], in_=w_scr[nm][dc * 128:(dc + 1) * 128, :]),
               reads=[f"scr{nm}"], writes=gr("st", dc * 1024, (dc + 1) * 1024), dma=True)

    def phase_proj(b, si):
        bias = bias_t[("bq", "bk", "bv")[si]]
        btk = ("bq", "bk", "bv")[si]
        reg = "st"
        wv = wbuf
        for tau in range(16):
            pwi = tau % 4
            stok = gr("r2", tau * 1024, (tau + 1) * 1024)
            for dc in range(8):
                for eta in range(2):
                    em("t", lambda e, tau=tau, dc=dc, eta=eta, pwi=pwi, wv=wv:
                       e.matmul(pw[pwi][:, eta * 512:(eta + 1) * 512],
                                xnT[:, dc, tau * 128:(tau + 1) * 128],
                                wv[:, dc, eta * 512:(eta + 1) * 512],
                                start=(dc == 0), stop=False),
                       reads=[f"r1c{tau}"] + gr(reg, dc * 1024, (dc + 1) * 1024),
                       writes=[f"pw{pwi}"], inc=False)
            for eta in range(2):
                em("t", lambda e, eta=eta, pwi=pwi, bias=bias:
                   e.matmul(pw[pwi][:, eta * 512:(eta + 1) * 512], onesrow[0:1, :],
                            bias[0:1, eta * 512:(eta + 1) * 512], start=False, stop=True),
                   reads=["onesrow", btk], writes=[f"pw{pwi}"], inc=(eta == 1))
            if tau % 2 == 0:
                em("s", lambda e, tau=tau, pwi=pwi: e.copy(sig[:, tau, :], pw[pwi][:, :]),
                   reads=[f"pw{pwi}"], writes=stok)
            else:
                em("v", lambda e, tau=tau, pwi=pwi: e.tensor_copy(sig[:, tau, :], pw[pwi][:, :]),
                   reads=[f"pw{pwi}"], writes=stok)
            if b == 0 and si >= 1:
                prep_service(1)
        if debug and b == 0 and si == 0:
            em("g", lambda e: e.dma_start(out=dbg["d_sig"], in_=sig),
               reads=gr("r2", 0, 16384), dma=True)

    def phase_ln_proj(b):
        def lndma(tau):
            par = tau % 4
            em("q", lambda e, b=b, tau=tau, par=par:
               e.dma_start(out=lnx[par], in_=x_in[b, tau * 128:(tau + 1) * 128, :]),
               writes=LNX_TK[par], dma=True, tag=f"lnx{tau}")

        def lnchain(tau):
            par = tau % 4
            em("v", lambda e, par=par: e.reduce_sum(msum[par][:], lnx[par], axis=AX.X),
               reads=LNX_TK[par], writes=[f"msum{par}"])
            em("s", lambda e, par=par: e.activation(sqdump[par % 2], lnx[par], AF.Square,
                                                    accum_out=ssq[par][:]),
               reads=LNX_TK[par], writes=[f"ssq{par}"] + SQD_TK[par % 2])
            em("v", lambda e, par=par: e.tensor_scalar(out=meanv[par][:], in0=msum[par][:],
                                                       scalar1=1.0 / D, scalar2=None, op0=ALU.mult),
               reads=[f"msum{par}"], writes=[f"meanv{par}"])
            em("v", lambda e, par=par: e.tensor_mul(m2v[par][:], meanv[par][:], meanv[par][:]),
               reads=[f"meanv{par}"], writes=[f"m2v{par}"])
            em("v", lambda e, par=par: e.scalar_tensor_tensor(out=varv[par][:], in0=ssq[par][:],
                                                              scalar=1.0 / D, in1=m2v[par][:],
                                                              op0=ALU.mult, op1=ALU.subtract),
               reads=[f"ssq{par}", f"m2v{par}"], writes=[f"varv{par}"])
            em("s", lambda e, par=par: e.activation(sdv[par][:], varv[par][:], AF.Sqrt, bias=epsv[:]),
               reads=[f"varv{par}", "epsv"], writes=[f"sdv{par}"])
            em("v", lambda e, par=par: e.reciprocal(rstd[par][:], sdv[par][:]),
               reads=[f"sdv{par}"], writes=[f"rstd{par}"])
            em("v", lambda e, par=par: e.tensor_scalar(out=lnxn[par], in0=lnx[par],
                                                       scalar1=meanv[par][:], scalar2=rstd[par][:],
                                                       op0=ALU.subtract, op1=ALU.mult),
               reads=LNX_TK[par] + [f"meanv{par}", f"rstd{par}"], writes=LNXN_TK[par])

        def transp(tau):
            par = tau % 4
            pwi = 2 + (tau % 2)
            for dc in range(8):
                em("t", lambda e, par=par, dc=dc, pwi=pwi:
                   e.transpose(pwb[pwi][:, dc * 128:(dc + 1) * 128],
                               lnxn[par][:, dc * 128:(dc + 1) * 128], identb_t[:]),
                   reads=LNXN_TK[par] + ["identb"], writes=[f"pw{pwi}"], inc=(dc == 7))
            em("s", lambda e, tau=tau, pwi=pwi:
               e.copy(xnT[:, :, tau * 128:(tau + 1) * 128],
                      pwb[pwi][:, 0:1024].rearrange("p (dc c) -> p dc c", dc=8)),
               reads=[f"pw{pwi}"], writes=[f"r1c{tau}"])

        def projmms(tau):
            ppw = tau % 2
            stok = gr("r2", tau * 1024, (tau + 1) * 1024)
            for dc in range(8):
                for eta in range(2):
                    em("t", lambda e, tau=tau, dc=dc, eta=eta, ppw=ppw:
                       e.matmul(pw[ppw][:, eta * 512:(eta + 1) * 512],
                                xnT[:, dc, tau * 128:(tau + 1) * 128],
                                wbuf[:, dc, eta * 512:(eta + 1) * 512],
                                start=(dc == 0), stop=False),
                       reads=[f"r1c{tau}"] + gr("st", dc * 1024, (dc + 1) * 1024),
                       writes=[f"pw{ppw}"], inc=False)
            for eta in range(2):
                em("t", lambda e, eta=eta, ppw=ppw:
                   e.matmul(pw[ppw][:, eta * 512:(eta + 1) * 512], onesrow[0:1, :],
                            bias_t["bq"][0:1, eta * 512:(eta + 1) * 512], start=False, stop=True),
                   reads=["onesrow", "bq"], writes=[f"pw{ppw}"], inc=(eta == 1))
            em("v", lambda e, tau=tau, ppw=ppw: e.tensor_copy(sig[:, tau, :], pw[ppw][:, :]),
               reads=[f"pw{ppw}"], writes=stok)

        for tau in range(4):
            lndma(tau)
        prefetch_wbuf(b, 0)
        for tau in range(16):
            lnchain(tau)
            if tau + 4 < 16:
                lndma(tau + 4)
            transp(tau)
            if tau >= 1:
                projmms(tau - 1)
        projmms(15)

    def phase_fft(b, si):
        tlist = {0: [("qr", c_cos, 0, qr_v, "r3", 0), ("qi", c_sinn, 1, qi_v, "r4", 2)],
                 1: [("kr", c_cos, 0, kr_v, "r5", 0)],
                 2: [("vr", c_cos, 0, vr_v, "r6", 0)]}[si]
        for phi in range(8):
            par = phi % 2
            for (nm, mat, kind, dstv, reg, pwbase) in tlist:
                slotv = cosslot[:, par * 16:(par + 1) * 16, :] if kind == 0 else \
                    sinslot4[:, 32 + par * 16:32 + (par + 1) * 16, :]
                slot_tk = gr("st", par * 2048, (par + 1) * 2048) if kind == 0 else \
                    gr("st", 4096 + par * 2048, 4096 + (par + 1) * 2048)
                qq = "q"
                em(qq, lambda e, mat=mat, phi=phi, slotv=slotv:
                   e.dma_start(out=slotv, in_=mat[:, phi * 128:(phi + 1) * 128]
                               .rearrange("(tau p) c -> p tau c", p=128)),
                   writes=slot_tk, dma=True, tag=f"fftdma{nm}{phi}")
                pwi = pwbase + par
                for tau in range(16):
                    for eta in range(2):
                        em("t", lambda e, slotv=slotv, tau=tau, eta=eta, pwi=pwi:
                           e.matmul(pw[pwi][:, eta * 512:(eta + 1) * 512],
                                    slotv[:, tau, :],
                                    sig[:, tau, eta * 512:(eta + 1) * 512],
                                    start=(tau == 0), stop=(tau == 15)),
                           reads=slot_tk + gr("r2", tau * 1024, (tau + 1) * 1024),
                           writes=[f"pw{pwi}"], inc=(tau == 15 and eta == 1))
                eng = "s" if kind == 0 else "v"
                em(eng, lambda e, dstv=dstv, phi=phi, pwi=pwi, eng=eng:
                   (e.copy if eng == "s" else e.tensor_copy)(dstv[:, phi, :], pw[pwi][:, :]),
                   reads=[f"pw{pwi}"], writes=gr(reg, phi * 1024, (phi + 1) * 1024))
            if si == 0:
                for eta in range(2):
                    sl = slice(eta * 512, (eta + 1) * 512)
                    qtk = gr("r3", phi * 1024 + eta * 512, phi * 1024 + (eta + 1) * 512)
                    itk = gr("r4", phi * 1024 + eta * 512, phi * 1024 + (eta + 1) * 512)
                    em("v", lambda e, phi=phi, sl=sl: e.tensor_mul(tmpA[:], qr_v[:, phi, sl], qr_v[:, phi, sl]),
                       reads=qtk, writes=["tmpA"])
                    em("v", lambda e, phi=phi, sl=sl: e.tensor_mul(tmpB[:], qi_v[:, phi, sl], qi_v[:, phi, sl]),
                       reads=itk, writes=["tmpB"])
                    em("v", lambda e: e.tensor_add(tmpA[:], tmpA[:], tmpB[:]),
                       reads=["tmpA", "tmpB"], writes=["tmpA"])
                    em("s", lambda e: e.activation(tmpA[:], tmpA[:], AF.Sqrt),
                       reads=["tmpA"], writes=["tmpA"])
                    em("v", lambda e, phi=phi, eta=eta:
                       e.tensor_reduce(out=e_fh[:, phi, eta * 8:(eta + 1) * 8].unsqueeze(2),
                                       in_=tmpA[:].rearrange("p (g d) -> p g d", g=8),
                                       axis=AX.X, op=ALU.add),
                       reads=["tmpA"], writes=["e_fh"])
            if si == 1:
                for eta in range(2):
                    sl = slice(eta * 512, (eta + 1) * 512)
                    qtk = gr("r3", phi * 1024 + eta * 512, phi * 1024 + (eta + 1) * 512)
                    ktk = gr("r5", phi * 1024 + eta * 512, phi * 1024 + (eta + 1) * 512)
                    em("v", lambda e, phi=phi, sl=sl: e.tensor_mul(tmpB[:], qr_v[:, phi, sl], kr_v[:, phi, sl]),
                       reads=qtk + ktk, writes=["tmpB"])
                    em("v", lambda e, phi=phi, eta=eta:
                       e.tensor_reduce(out=s_fh[:, phi, eta * 8:(eta + 1) * 8].unsqueeze(2),
                                       in_=tmpB[:].rearrange("p (g d) -> p g d", g=8),
                                       axis=AX.X, op=ALU.add),
                       reads=["tmpB"], writes=["s_fh"])
            if b == 0:
                prep_service(1)
        nyq_dst, nyq_tk = [(nyq_qr, "nyq_qr"), (nyq_kr, "nyq_kr"), (nyq_vr, "nyq_vr")][si]
        for tau in range(16):
            for eta in range(2):
                em("t", lambda e, tau=tau, eta=eta:
                   e.matmul(pw[3][0:1, eta * 512:(eta + 1) * 512], altcol_t[:],
                            sig[:, tau, eta * 512:(eta + 1) * 512],
                            start=(tau == 0), stop=(tau == 15)),
                   reads=["altcol"] + gr("r2", tau * 1024, (tau + 1) * 1024),
                   writes=["pw3"], inc=(tau == 15 and eta == 1))
        em("s", lambda e, nyq_dst=nyq_dst: e.copy(nyq_dst[:], pw[3][0:1, :]),
           reads=["pw3"], writes=[nyq_tk])
        if si == 0:
            em("v", lambda e: e.tensor_mul(tmpC[:], nyq_qr[:], nyq_qr[:]),
               reads=["nyq_qr"], writes=["tmpC"])
            em("s", lambda e: e.activation(tmpC[:], tmpC[:], AF.Sqrt),
               reads=["tmpC"], writes=["tmpC"])
            em("v", lambda e: e.tensor_reduce(out=enrow[:].unsqueeze(2),
                                              in_=tmpC[:].rearrange("p (g d) -> p g d", g=16),
                                              axis=AX.X, op=ALU.add),
               reads=["tmpC"], writes=["enrow"])
        if si == 1:
            em("v", lambda e: e.tensor_mul(tmpC[:], nyq_qr[:], nyq_kr[:]),
               reads=["nyq_qr", "nyq_kr"], writes=["tmpC"])
            em("v", lambda e: e.tensor_reduce(out=snrow[:].unsqueeze(2),
                                              in_=tmpC[:].rearrange("p (g d) -> p g d", g=16),
                                              axis=AX.X, op=ALU.add),
               reads=["tmpC"], writes=["snrow"])
        if debug and b == 0:
            dv, dr = {0: (qr_v, "r3"), 1: (kr_v, "r5"), 2: (vr_v, "r6")}[si]
            dn = {0: "d_qr", 1: "d_kr", 2: "d_vr"}[si]
            em("g", lambda e, dv=dv, dn=dn: e.dma_start(out=dbg[dn], in_=dv),
               reads=gr(dr, 0, 8192), dma=True)
            if si == 0:
                em("g", lambda e: e.dma_start(out=dbg["d_qi"], in_=qi_v), reads=gr("r4", 0, 8192), dma=True)
                em("g", lambda e: e.dma_start(out=dbg["d_qn"], in_=nyq_qr[:]), reads=["nyq_qr"], dma=True)

    def phase_middle_pre(b):
        for (src, srctk, dst, dsttk, nrow, nrtk, pwi) in [
                (e_fh, "e_fh", eT, "spA", enrow, "enrow", 2),
                (s_fh, "s_fh", scT, "spB", snrow, "snrow", 3)]:
            for phi in range(8):
                em("t", lambda e, src=src, phi=phi, pwi=pwi:
                   e.transpose(pw[pwi][0:16, phi * 128:(phi + 1) * 128],
                               src[:, phi, :], identf_t[:]),
                   reads=[srctk, "identf"], writes=[f"pw{pwi}"], inc=(phi == 7))
            em("s", lambda e, dst=dst, pwi=pwi: e.copy(dst[:, 0:1024], pw[pwi][0:16, :]),
               reads=[f"pw{pwi}"], writes=[dsttk])
            em("t", lambda e, nrow=nrow, pwi=pwi:
               e.transpose(pw[pwi][0:16, 0:1], nrow[:], identf_t[0:1, 0:1]),
               reads=[nrtk, "identf", dsttk], writes=[f"pw{pwi}"])
            em("s", lambda e, dst=dst, pwi=pwi: e.copy(dst[:, 1024:1025], pw[pwi][0:16, 0:1]),
               reads=[f"pw{pwi}"], writes=[dsttk])

    def phase_middle_a(b):
        for it in range(8):
            em("v", lambda e: e.max(out=max8[:], in_=eT), reads=["spA"], writes=["max8"])
            em("v", lambda e: e.match_replace(out=eT, in_to_replace=max8[:], in_values=eT,
                                              imm_value=-1.0),
               reads=["max8", "spA"], writes=["spA"])
        if debug and b == 0:
            em("g", lambda e: e.dma_start(out=dbg["d_eT"], in_=eT), reads=["spA"], dma=True)
            em("g", lambda e: e.dma_start(out=dbg["d_scT"], in_=scT), reads=["spB"], dma=True)
        em("v", lambda e: e.tensor_scalar(out=um, in0=eT, scalar1=0.0, scalar2=None,
                                          op0=ALU.is_ge), reads=["spA"], writes=["spD"])
        em("v", lambda e: e.tensor_scalar(out=scT, in0=scT, scalar1=C1, scalar2=None,
                                          op0=ALU.mult), reads=["spB"], writes=["spB"])
        em("v", lambda e: e.scalar_tensor_tensor(out=scT, in0=um, scalar=-1e9, in1=scT,
                                                 op0=ALU.mult, op1=ALU.add),
           reads=["spD", "spB"], writes=["spB"])
        em("v", lambda e: e.reduce_max(mxx[:], scT, axis=AX.X), reads=["spB"], writes=["mxx"])
        em("v", lambda e: e.tensor_scalar(out=negmx[:], in0=mxx[:], scalar1=-1.0, scalar2=None,
                                          op0=ALU.mult), reads=["mxx"], writes=["negmx"])
        em("s", lambda e: e.activation(scT, scT, AF.Exp, bias=negmx[:]),
           reads=["spB", "negmx"], writes=["spB"])
        em("v", lambda e: e.reduce_sum(smr[:], scT, axis=AX.X), reads=["spB"], writes=["smr"])
        em("v", lambda e: e.reciprocal(rsm[:], smr[:]), reads=["smr"], writes=["rsm"])
        em("v", lambda e: e.tensor_scalar(out=attnT, in0=scT, scalar1=rsm[:], scalar2=None,
                                          op0=ALU.mult), reads=["spB", "rsm"], writes=["spC"])

    def phase_middle_b(b):
        for (vsp, reg, nrow, nrtk, dst, dtk, pwi) in [
                (qr_v, "r3", nyq_qr, "nyq_qr", qmr, "qmr", 0),
                (qi_v, "r4", None, None, qmi, "qmi", 1)]:
            for phi in range(8):
                last = (phi == 7 and nrow is None)
                for eta in range(2):
                    em("t", lambda e, vsp=vsp, phi=phi, eta=eta, pwi=pwi, last=last:
                       e.matmul(pw[pwi][0:1, eta * 512:(eta + 1) * 512], onescol[:],
                                vsp[:, phi, eta * 512:(eta + 1) * 512],
                                start=(phi == 0), stop=last),
                       reads=gr(reg, phi * 1024, (phi + 1) * 1024) + ["onescol"],
                       writes=[f"pw{pwi}"], inc=(last and eta == 1))
            if nrow is not None:
                for eta in range(2):
                    em("t", lambda e, nrow=nrow, eta=eta, pwi=pwi:
                       e.matmul(pw[pwi][0:1, eta * 512:(eta + 1) * 512], onesrow[0:1, 0:1],
                                nrow[0:1, eta * 512:(eta + 1) * 512], start=False, stop=True),
                       reads=[nrtk, "onesrow"], writes=[f"pw{pwi}"], inc=(eta == 1))
            em("s", lambda e, dst=dst, pwi=pwi: e.mul(dst[:], pw[pwi][0:1, :], 1.0 / NF),
               reads=[f"pw{pwi}"], writes=[dtk])
        for (qrow, qrtk, qt, qttk, pwi) in [(qmr, "qmr", qtr, "qtr", 0), (qmi, "qmi", qti, "qti", 1)]:
            for eta in range(2):
                em("t", lambda e, qrow=qrow, eta=eta, pwi=pwi:
                   e.matmul(pw[pwi][0:16, eta * 512:(eta + 1) * 512], onesrow[0:1, 0:16],
                            qrow[0:1, eta * 512:(eta + 1) * 512], start=True, stop=True),
                   reads=[qrtk, "onesrow"], writes=[f"pw{pwi}"], inc=(eta == 1))
            em("v", lambda e, qt=qt, pwi=pwi: e.tensor_mul(qt[:], pw[pwi][0:16, 0:1024], eexp_t[:]),
               reads=[f"pw{pwi}", "eexp"], writes=[qttk])
        if debug and b == 0:
            em("g", lambda e: e.dma_start(out=dbg["d_attnT"], in_=attnT), reads=["spC"], dma=True)
            em("g", lambda e: e.dma_start(out=dbg["d_um"], in_=um), reads=["spD"], dma=True)
            em("g", lambda e: e.dma_start(out=dbg["d_qmr"], in_=qmr[:]), reads=["qmr"], dma=True)
            em("g", lambda e: e.dma_start(out=dbg["d_qmi"], in_=qmi[:]), reads=["qmi"], dma=True)
        for phi in range(8):
            for eta in range(2):
                em("t", lambda e, phi=phi, eta=eta:
                   e.matmul(pw[0][:, eta * 512:(eta + 1) * 512],
                            attnT[:, phi * 128:(phi + 1) * 128],
                            eexp_t[:, eta * 512:(eta + 1) * 512], start=True, stop=True),
                   reads=["spC", "eexp"], writes=["pw0"], inc=(eta == 1))
                em("t", lambda e, phi=phi, eta=eta:
                   e.matmul(pw[1][:, eta * 512:(eta + 1) * 512],
                            um[:, phi * 128:(phi + 1) * 128],
                            qtr[:, eta * 512:(eta + 1) * 512], start=True, stop=True),
                   reads=["spD", "qtr"], writes=["pw1"], inc=(eta == 1))
            em("t", lambda e, phi=phi:
               e.transpose(pwb[2][:, phi * 16:(phi + 1) * 16],
                           um[:, phi * 128:(phi + 1) * 128], identb_t[0:16, 0:16]),
               reads=["spD", "identb"], writes=["pw2"], inc=(phi == 7))
            for eta in range(2):
                sl = slice(eta * 512, (eta + 1) * 512)
                vtk = gr("r6", phi * 1024 + eta * 512, phi * 1024 + (eta + 1) * 512)
                xtk = gr("r3", phi * 1024 + eta * 512, phi * 1024 + (eta + 1) * 512)
                em("v", lambda e, phi=phi, sl=sl: e.tensor_mul(tmpA[:], pw[0][:, sl], vr_v[:, phi, sl]),
                   reads=["pw0"] + vtk, writes=["tmpA"])
                em("v", lambda e, phi=phi, sl=sl: e.tensor_add(xre_v[:, phi, sl], tmpA[:], pw[1][:, sl]),
                   reads=["tmpA", "pw1"], writes=xtk)
        em("s", lambda e: e.copy(umF[:], pwb[2][:, 0:128].rearrange("p (a c) -> p a c", a=8)),
           reads=["pw2"], writes=["umF"])
        for c in range(8):
            em("t", lambda e, c=c:
               e.transpose(pwb[3][:, c * 16:(c + 1) * 16],
                           qti[:, c * 128:(c + 1) * 128], identb_t[0:16, 0:16]),
               reads=["qti", "identb"], writes=["pw3"], inc=(c == 7))
        em("s", lambda e: e.copy(qtiT[:], pwb[3][:, 0:128].rearrange("p (a c) -> p a c", a=8)),
           reads=["pw3"], writes=["qtiT"])
        for eta in range(2):
            for c in range(8):
                em("t", lambda e, c=c, eta=eta:
                   e.matmul(pw[2][0:16, eta * 512:(eta + 1) * 512], qtiT[:, c, :],
                            wbuf[:, c, eta * 512:(eta + 1) * 512],
                            start=(c == 0), stop=(c == 7)),
                   reads=["qtiT"] + gr("st", c * 1024, (c + 1) * 1024),
                   writes=["pw2"], inc=(c == 7 and eta == 1))
        em("s", lambda e: e.copy(wqim[:], pw[2][0:16, :]), reads=["pw2"], writes=gr("r4", 2048, 3072))
        for eta in range(2):
            em("t", lambda e, eta=eta:
               e.matmul(pw[0][0:1, eta * 512:(eta + 1) * 512], attnT[:, 1024:1025],
                        eexp_t[:, eta * 512:(eta + 1) * 512], start=True, stop=True),
               reads=["spC", "eexp"], writes=["pw0"], inc=(eta == 1))
            em("t", lambda e, eta=eta:
               e.matmul(pw[1][0:1, eta * 512:(eta + 1) * 512], um[:, 1024:1025],
                        qtr[:, eta * 512:(eta + 1) * 512], start=True, stop=True),
               reads=["spD", "qtr"], writes=["pw1"], inc=(eta == 1))
        em("v", lambda e: e.tensor_mul(tmpC[:], pw[0][0:1, :], nyq_vr[:]),
           reads=["pw0", "nyq_vr"], writes=["tmpC"])
        em("v", lambda e: e.tensor_add(nyq_xre[:], tmpC[:], pw[1][0:1, :]),
           reads=["tmpC", "pw1"], writes=["nyq_xre"])
        if debug and b == 0:
            em("g", lambda e: e.dma_start(out=dbg["d_xre"], in_=xre_v), reads=gr("r3", 0, 8192), dma=True)

    def phase_irfft(b):
        altv = []
        for phi in range(8):
            if phi < 5:
                altv.append((r4[:, 3072 + phi * 1024:4096 + phi * 1024],
                             gr("r4", 3072 + phi * 1024, 4096 + phi * 1024)))
            elif phi < 7:
                altv.append((spEh[phi - 5], [f"spE{phi - 5}"]))
            else:
                altv.append((tmpA[:].bitcast(BF16), ["tmpA"]))
        em("q", lambda e: e.dma_start(
            out=crci[:, 8:16, :],
            in_=c_cit[:, 0:1024].rearrange("(phi p) t -> p phi t", p=128)),
           writes=gr("r2", 8192, 16384), dma=True)
        em("sd", lambda e: e.dma_start(
            out=crci[:, 0:8, :],
            in_=c_crt[:, 0:1024].rearrange("(phi p) t -> p phi t", p=128)),
           writes=gr("r2", 0, 8192), dma=True, fence=False)
        S.emit_fence("sd")
        em("q", lambda e: e.dma_start(out=crtn_t[:], in_=c_crtnyq[:, 0:1024]),
           writes=["crtn"], dma=True)
        em("q", lambda e: e.dma_start(out=crtn_b[:], in_=c_crtnyq[:, 1024:2048]),
           writes=["crtnb"], dma=True)
        for phi in range(8):
            em("sd", lambda e, phi=phi: e.dma_start(
                out=altv[phi][0], in_=c_crt[phi * 128:(phi + 1) * 128, 1024:2048]),
               writes=altv[phi][1], dma=True, fence=False)
        for th in range(2):
            if th == 1:
                S.emit_fence("sd")
            for half in range(2):
                hsl = slice(half * 512, (half + 1) * 512)
                for phi in range(8):
                    em("t", lambda e, phi=phi, hsl=hsl, half=half:
                       e.matmul(pw[3][0:16, hsl], umF[:, phi, :], crci[:, 8 + phi, hsl],
                                start=(phi == 0), stop=(phi == 7)),
                       reads=["umF"] + gr("r2", 8192 + phi * 1024, 8192 + (phi + 1) * 1024),
                       writes=["pw3"], inc=(phi == 7 and half == 1))
            em("s", lambda e, th=th: e.copy(cmS[:, th * 1024:(th + 1) * 1024], pw[3][0:16, :]),
               reads=["pw3"], writes=gr("r4", 0, 2048))
            if th == 0:
                em("q", lambda e: e.dma_start(
                    out=crci[:, 8:16, :],
                    in_=c_cit[:, 1024:2048].rearrange("(phi p) t -> p phi t", p=128)),
                   writes=gr("r2", 8192, 16384), dma=True)
            nyqv, nyqtk = (crtn_t, "crtn") if th == 0 else (crtn_b, "crtnb")
            for chi in range(8):
                pwi = chi % 2
                for half in range(2):
                    hsl = slice(half * 512, (half + 1) * 512)
                    for phi in range(8):
                        if th == 0:
                            mv, mtk = crci[:, phi, hsl], gr("r2", phi * 1024, (phi + 1) * 1024)
                        else:
                            mv, mtk = altv[phi][0][:, hsl], altv[phi][1]
                        em("t", lambda e, chi=chi, phi=phi, hsl=hsl, pwi=pwi, mv=mv:
                           e.matmul(pw[pwi][:, hsl], xre_v[:, phi, chi * 128:(chi + 1) * 128],
                                    mv, start=(phi == 0), stop=False),
                           reads=gr("r3", phi * 1024 + chi * 128, phi * 1024 + (chi + 1) * 128) +
                           mtk, writes=[f"pw{pwi}"], inc=False)
                    em("t", lambda e, chi=chi, hsl=hsl, pwi=pwi, half=half, nyqv=nyqv:
                       e.matmul(pw[pwi][:, hsl], nyq_xre[0:1, chi * 128:(chi + 1) * 128],
                                nyqv[0:1, hsl], start=False, stop=True),
                       reads=["nyq_xre", nyqtk], writes=[f"pw{pwi}"], inc=(half == 1))
                eng = "s" if chi % 2 == 0 else "v"
                em(eng, lambda e, chi=chi, th=th, pwi=pwi, eng=eng:
                   (e.copy if eng == "s" else e.tensor_copy)(
                       outT(chi)[:, th * 1024:(th + 1) * 1024], pw[pwi][:, :]),
                   reads=[f"pw{pwi}"], writes=outT_tok(chi, th * 1024, (th + 1) * 1024))
        if debug and b == 0:
            em("g", lambda e: e.dma_start(out=dbg["d_outT"][:, 0:4, :], in_=outT_lo),
               reads=gr("r5", 0, 8192), dma=True)
            em("g", lambda e: e.dma_start(out=dbg["d_outT"][:, 4:8, :], in_=outT_hi),
               reads=gr("r6", 0, 8192), dma=True)

    def prefetch_wo(b):
        for dc in range(8):
            em("q", lambda e, dc=dc:
               e.dma_start(out=wbuf[:, dc, :], in_=w_scr["Wo"][dc * 128:(dc + 1) * 128, :]),
               reads=["scrWo"], writes=gr("st", dc * 1024, (dc + 1) * 1024), dma=True)

    def phase_wo(b):
        for tau in range(16):
            pwi = tau % 4
            par = tau % 2
            for dc in range(8):
                for eta in range(2):
                    em("t", lambda e, tau=tau, dc=dc, eta=eta, pwi=pwi:
                       e.matmul(pw[pwi][:, eta * 512:(eta + 1) * 512],
                                outT(dc)[:, tau * 128:(tau + 1) * 128],
                                wbuf[:, dc, eta * 512:(eta + 1) * 512],
                                start=(dc == 0), stop=False),
                       reads=outT_tok(dc, tau * 128, (tau + 1) * 128) +
                       gr("st", dc * 1024, (dc + 1) * 1024),
                       writes=[f"pw{pwi}"], inc=False)
            for eta in range(2):
                em("t", lambda e, tau=tau, eta=eta, pwi=pwi:
                   e.matmul(pw[pwi][:, eta * 512:(eta + 1) * 512],
                            cmS[:, tau * 128:(tau + 1) * 128],
                            wqim[:, eta * 512:(eta + 1) * 512], start=False, stop=False),
                   reads=gr("r4", 0, 3072), writes=[f"pw{pwi}"], inc=False)
                em("t", lambda e, eta=eta, pwi=pwi:
                   e.matmul(pw[pwi][:, eta * 512:(eta + 1) * 512], onesrow[0:1, :],
                            bias_t["bo"][0:1, eta * 512:(eta + 1) * 512], start=False, stop=True),
                   reads=["onesrow", "bo"], writes=[f"pw{pwi}"], inc=(eta == 1))
            em("s", lambda e, pwi=pwi, par=par:
               e.copy(spE[:, par * 1024:(par + 1) * 1024], pw[pwi][:, :]),
               reads=[f"pw{pwi}"], writes=[f"spE{par}"])
            em("q", lambda e, tau=tau, par=par:
               e.dma_start(out=attn_scr[tau * 128:(tau + 1) * 128, :],
                           in_=spE[:, par * 1024:(par + 1) * 1024]),
               reads=[f"spE{par}"], writes=["attn_scr"], dma=True)
        if debug and b == 0:
            em("g", lambda e: e.dma_start(out=dbg["d_attn"], in_=attn_scr),
               reads=["attn_scr"], dma=True)

    def phase_ffn_gu(b, hclist):
        stg = [spA[:].bitcast(I32)[:, 0:1024], spB[:].bitcast(I32)[:, 0:1024],
               spC[:].bitcast(I32)[:, 0:1024], spD[:].bitcast(I32)[:, 0:1024]]
        stk = [["spA"], ["spB"], ["spC"], ["spD"]]
        for hc in hclist:
            par = hc % 2
            pg, pu = (0, 1) if par == 0 else (2, 3)
            em("q", lambda e, hc=hc, par=par:
               e.dma_start(out=stg[par].rearrange("p (dc c) -> p dc c", dc=8),
                           in_=wg_in[:, hc * 128:(hc + 1) * 128]
                           .rearrange("(dc p) c -> p dc c", p=128)),
               writes=stk[par], dma=True)
            em("s", lambda e, hc=hc, par=par:
               e.copy(wgslot[:, par, :], stg[par]),
               reads=stk[par], writes=gr("st", par * 1024, (par + 1) * 1024))
            em("q", lambda e, hc=hc, par=par:
               e.dma_start(out=stg[2 + par].rearrange("p (dc c) -> p dc c", dc=8),
                           in_=wu_in[:, hc * 128:(hc + 1) * 128]
                           .rearrange("(dc p) c -> p dc c", p=128)),
               writes=stk[2 + par], dma=True)
            em("v", lambda e, hc=hc, par=par:
               e.tensor_copy(wgslot[:, 2 + par, :], stg[2 + par]),
               reads=stk[2 + par], writes=gr("st", (2 + par) * 1024, (3 + par) * 1024))
            for th in range(2):
                wgv = wgslot[:, par, :].rearrange("p (dc c) -> p dc c", dc=8)
                wuv = wgslot[:, 2 + par, :].rearrange("p (dc c) -> p dc c", dc=8)
                for dc in range(8):
                    for tq in range(2):
                        em("t", lambda e, wgv=wgv, dc=dc, th=th, tq=tq, pg=pg:
                           e.matmul(pw[pg][:, tq * 512:(tq + 1) * 512], wgv[:, dc, :],
                                    xnT[:, dc, th * 1024 + tq * 512:th * 1024 + (tq + 1) * 512],
                                    start=(dc == 0), stop=(dc == 7)),
                           reads=gr("st", par * 1024, (par + 1) * 1024) +
                           [f"r1c{th * 8 + tq * 4 + i}" for i in range(4)],
                           writes=[f"pw{pg}"], inc=(dc == 7 and tq == 1))
                for dc in range(8):
                    for tq in range(2):
                        em("t", lambda e, wuv=wuv, dc=dc, th=th, tq=tq, pu=pu:
                           e.matmul(pw[pu][:, tq * 512:(tq + 1) * 512], wuv[:, dc, :],
                                    xnT[:, dc, th * 1024 + tq * 512:th * 1024 + (tq + 1) * 512],
                                    start=(dc == 0), stop=(dc == 7)),
                           reads=gr("st", (2 + par) * 1024, (3 + par) * 1024) +
                           [f"r1c{th * 8 + tq * 4 + i}" for i in range(4)],
                           writes=[f"pw{pu}"], inc=(dc == 7 and tq == 1))
                em("s", lambda e, par=par, pg=pg:
                   e.activation(siluslot[:, 4 + par, :], pw[pg][:, :], AF.Silu, scale=SC),
                   reads=[f"pw{pg}"], writes=gr("st", (4 + par) * 1024, (5 + par) * 1024))
                em("v", lambda e, hc=hc, th=th, par=par, pu=pu:
                   e.tensor_mul(hT(hc)[:, th * 1024:(th + 1) * 1024],
                                siluslot[:, 4 + par, :], pw[pu][:, :]),
                   reads=gr("st", (4 + par) * 1024, (5 + par) * 1024) + [f"pw{pu}"],
                   writes=hT_tok(hc, th * 1024, (th + 1) * 1024))
        if debug and b == 0:
            em("g", lambda e: e.dma_start(out=dbg["d_h"], in_=hTv[0]),
               reads=gr("r2", 0, 16384), dma=True)

    def phase_down_final(b):
        em("q", lambda e: e.dma_start(out=wdres_a,
                                      in_=wdT_s[0:2048, :].rearrange("(hc p) d -> p hc d", p=128)),
           reads=["scrwd"], writes=gr("r1", 0, 16384) + [f"r1c{t}" for t in range(16)], dma=True)
        em("q", lambda e: e.dma_start(out=wdres_b,
                                      in_=wdT_s[2048:3072, :].rearrange("(hc p) d -> p hc d", p=128)),
           reads=["scrwd"], writes=gr("st", 0, 8192), dma=True)
        for tp in range(8):
            taus = (2 * tp, 2 * tp + 1)
            for i, tau in enumerate(taus):
                pwi = 2 * (tp % 2) + i
                for hc in range(24):
                    for eta in range(2):
                        em("t", lambda e, tau=tau, hc=hc, eta=eta, pwi=pwi:
                           e.matmul(pw[pwi][:, eta * 512:(eta + 1) * 512],
                                    hT(hc)[:, tau * 128:(tau + 1) * 128],
                                    wdres(hc)[:, eta * 512:(eta + 1) * 512],
                                    start=(hc == 0), stop=(hc == 23)),
                           reads=hT_tok(hc, tau * 128, (tau + 1) * 128) + wdres_tok(hc),
                           writes=[f"pw{pwi}"], inc=(hc == 23 and eta == 1))
            for i, tau in enumerate(taus):
                pwi = 2 * (tp % 2) + i
                par = tau % 2
                sp = ("spA", "spB")[par]
                op_ = ("spC", "spD")[par]
                em("q", lambda e, b=b, tau=tau, par=par:
                   e.dma_start(out=xstage[par], in_=x_in[b, tau * 128:(tau + 1) * 128, :]),
                   writes=[sp], dma=True)
                em("q", lambda e, tau=tau, par=par:
                   e.dma_start(out=spE[:, par * 1024:(par + 1) * 1024],
                               in_=attn_scr[tau * 128:(tau + 1) * 128, :]),
                   reads=["attn_scr"], writes=[f"spE{par}"], dma=True)
                for eta in range(2):
                    sl = slice(eta * 512, (eta + 1) * 512)
                    em("v", lambda e, pwi=pwi, par=par, sl=sl:
                       e.scalar_tensor_tensor(out=ostage[par][:, sl], in0=pw[pwi][:, sl],
                                              scalar=SC * SC, in1=xstage[par][:, sl],
                                              op0=ALU.mult, op1=ALU.add),
                       reads=[f"pw{pwi}", sp], writes=[op_])
                    em("v", lambda e, par=par, sl=sl:
                       e.tensor_add(ostage[par][:, sl], ostage[par][:, sl],
                                    spE[:, par * 1024:(par + 1) * 1024][:, sl]),
                       reads=[op_, f"spE{par}"], writes=[op_])
                em("sd", lambda e, b=b, tau=tau, par=par:
                   e.dma_start(out=out_d[b, tau * 128:(tau + 1) * 128, :], in_=ostage[par]),
                   reads=[op_], dma=True)

    for b in range(NB):
        if b == 0:
            prep_attn("Wq")
            prefetch_wbuf(b, 0)
            phase_ln(b)
            phase_proj(b, 0)
            prep_attn_pend("Wk")
            prep_attn_pend("Wv")
        else:
            phase_ln_proj(b)
        phase_fft(b, 0)
        if b == 0:
            prep_flush()
        prefetch_wbuf(b, 1)
        if b == 0:
            prep_attn_pend("Wo")
            prep_pending.extend((src, dst, tk, True) for (src, dst, tk) in ffn_chunks)
        phase_proj(b, 1)
        phase_fft(b, 1)
        prefetch_wbuf(b, 2)
        phase_proj(b, 2)
        if b == 0:
            prep_service(99)
            prep_flush()
        phase_middle_pre(b)
        phase_middle_a(b)
        phase_fft(b, 2)
        prefetch_wo(b)
        phase_middle_b(b)
        phase_irfft(b)
        phase_wo(b)
        phase_ffn_gu(b, range(0, 24))
        phase_down_final(b)

    fin_q, fin_g, fin_sd = S.cnt["q"], S.cnt["g"], S.cnt["sd"]
    S.emit("q", lambda e: e.wait_ge(semG, fin_g), inc=False)
    S.emit("q", lambda e: e.wait_ge(semSD, fin_sd), inc=False)
    S.emit("g", lambda e: e.wait_ge(semQ, fin_q), inc=False)

    def run_engine(skey, eng):
        for (ekey, (fn, waits, inc, amt, tag)) in S.streams[skey]:
            for (pe, pv) in waits:
                eng.wait_ge(SEMS[pe], pv)
            if fn is None:
                continue
            inst = fn(eng)
            if inc and inst is not None:
                inst.then_inc(SEMS[ekey], amt)

    @block.sync
    def _(eng):
        run_engine("q", eng)

    @block.tensor
    def _(eng):
        run_engine("t", eng)

    @block.vector
    def _(eng):
        run_engine("v", eng)

    @block.scalar
    def _(eng):
        run_engine("s", eng)

    @block.gpsimd
    def _(eng):
        run_engine("g", eng)

    stack.close()
    return nc, S.nops


from concourse.bass_utils import run_bass_kernel_spmd

_CACHED = {}


def kernel(**inputs):
    x = np.ascontiguousarray(inputs["L0_attn_query"], dtype=np.float32)
    if "prog" not in _CACHED:
        _CACHED["prog"] = build_program(debug=False)[0]
        _CACHED["consts"] = build_consts()
    nc = _CACHED["prog"]
    consts = _CACHED["consts"]
    shard = x.reshape(8, NB, L, D)
    base = dict(
        Wq=np.ascontiguousarray(inputs["L0_attn_Wq"], np.float32),
        Wk=np.ascontiguousarray(inputs["L0_attn_Wk"], np.float32),
        Wv=np.ascontiguousarray(inputs["L0_attn_Wv"], np.float32),
        Wo=np.ascontiguousarray(inputs["L0_attn_Wo"], np.float32),
        bq=np.ascontiguousarray(inputs["L0_attn_bq"], np.float32).reshape(1, D),
        bk=np.ascontiguousarray(inputs["L0_attn_bk"], np.float32).reshape(1, D),
        bv=np.ascontiguousarray(inputs["L0_attn_bv"], np.float32).reshape(1, D),
        bo=np.ascontiguousarray(inputs["L0_attn_bo"], np.float32).reshape(1, D),
        wg=np.ascontiguousarray(np.asarray(inputs["L0_ffn_w_gate"], np.int32).T),
        wu=np.ascontiguousarray(np.asarray(inputs["L0_ffn_w_up"], np.int32).T),
        wd=np.ascontiguousarray(np.asarray(inputs["L0_ffn_w_down"], np.int32).T),
    )
    for k, v in consts.items():
        base[k] = np.ascontiguousarray(v)
    in_maps = [dict(base, x=np.ascontiguousarray(shard[i])) for i in range(8)]
    res = run_bass_kernel_spmd(nc, in_maps, core_ids=list(range(8)))
    out = np.concatenate(
        [np.asarray(res.results[i]["out"], dtype=np.float32) for i in range(8)], axis=0)
    return out.reshape(32, L, D)


# version 28 (final)
# speedup vs baseline: 1.0229x; 1.0110x over previous
import numpy as np
import ml_dtypes
import concourse.bass as bass
import concourse.mybir as mybir

F32 = mybir.dt.float32
BF16 = mybir.dt.bfloat16
I32 = mybir.dt.int32
AF = mybir.ActivationFunctionType
ALU = mybir.AluOpType
AX = mybir.AxisListType

NB = 4
L = 2048
D = 1024
H = 16
DH = 64
NF = 1025
HID = 3072
SC = float(1.0 / 127.0)
C1 = float(1.0 / (np.sqrt(DH) + 1e-8))
ENGS = ("t", "v", "s", "q", "g", "sd")
STREAM_OF = {"sd": "s"}


class Sched:

    def __init__(self):
        self.ops = {e: [] for e in ENGS}
        self.cnt = {e: 0 for e in ENGS}
        self.seen = {e: {p: 0 for p in ENGS} for e in ENGS}
        self.snap = {e: {0: {p: 0 for p in ENGS}} for e in ENGS}
        self.resw = {}
        self.resr = {}
        self.pend = {e: [] for e in ENGS}
        self.pend_f = {e: [] for e in ENGS}
        self.streams = {e: [] for e in ENGS if e not in STREAM_OF}
        self.nops = 0

    def _push(self, e, entry):
        self.ops[e].append(entry)
        self.streams[STREAM_OF.get(e, e)].append((e, entry))

    def emit_fence(self, e):
        bnd = self.cnt[e]
        self._push(e, (None, [(e, bnd)], False, 0, "dmafence"))
        self.seen[e][e] = bnd
        clk = dict(self.seen[e])
        clk[e] = bnd
        self.snap[e][bnd] = clk
        for kind, tk in self.pend_f[e]:
            if kind == "w":
                if self.resw.get(tk, (None, 0))[1] is None and self.resw[tk][0] == e:
                    self.resw[tk] = (e, bnd)
            else:
                lst = self.resr.get(tk, [])
                for i, (re_, rv) in enumerate(lst):
                    if re_ == e and rv is None:
                        lst[i] = (e, bnd)
        self.pend_f[e] = []

    def emit(self, e, fn, reads=(), writes=(), inc=True, dma=False, fence=True, tag=""):
        waits = {}

        def need(pe, pv):
            if pv is None:
                assert pe == e, f"cross-engine dep on un-inc'd op, tag={tag}"
                return
            if self.seen[e].get(pe, 0) < pv:
                waits[pe] = max(waits.get(pe, 0), pv)

        for tk in reads:
            w = self.resw.get(tk)
            if w is not None:
                need(w[0], w[1])
        for tk in writes:
            w = self.resw.get(tk)
            if w is not None:
                need(w[0], w[1])
            for (re_, rv) in self.resr.get(tk, ()):
                need(re_, rv)
        for pe in list(waits):
            pv = waits[pe]
            self.seen[e][pe] = max(self.seen[e][pe], pv)
            sn = self.snap[pe].get(pv)
            if sn:
                for k2, v2 in sn.items():
                    if self.seen[e].get(k2, 0) < v2:
                        self.seen[e][k2] = v2
        amt = 16 if dma else 1
        self._push(e, (fn, sorted(waits.items()), inc, amt, tag))
        self.nops += 1
        if inc:
            self.cnt[e] += amt
            nc_ = self.cnt[e]
            for kind, tk in self.pend[e]:
                if kind == "w":
                    if self.resw.get(tk, (None, 0))[1] is None and self.resw[tk][0] == e:
                        self.resw[tk] = (e, nc_)
                else:
                    lst = self.resr.get(tk, [])
                    for i, (re_, rv) in enumerate(lst):
                        if re_ == e and rv is None:
                            lst[i] = (e, nc_)
            self.pend[e] = []
            clk = dict(self.seen[e])
            clk[e] = nc_
            self.snap[e][nc_] = clk
        for tk in writes:
            if inc:
                self.resw[tk] = (e, self.cnt[e])
            else:
                self.resw[tk] = (e, None)
                self.pend[e].append(("w", tk))
            self.resr[tk] = []
        for tk in reads:
            lst = self.resr.setdefault(tk, [])
            if inc:
                lst.append((e, self.cnt[e]))
            else:
                lst.append((e, None))
                self.pend[e].append(("r", tk))
        if dma and inc:
            if fence:
                self._push(e, (None, [(e, self.cnt[e])], False, 0, "dmafence"))
                self.seen[e][e] = self.cnt[e]
                bnd = self.cnt[e]
                for kind, tk in self.pend_f[e]:
                    if kind == "w":
                        if self.resw.get(tk, (None, 0))[1] is None and self.resw[tk][0] == e:
                            self.resw[tk] = (e, bnd)
                    else:
                        lst = self.resr.get(tk, [])
                        for i, (re_, rv) in enumerate(lst):
                            if re_ == e and rv is None:
                                lst[i] = (e, bnd)
                self.pend_f[e] = []
            else:
                for tk in writes:
                    self.resw[tk] = (e, None)
                    self.pend_f[e].append(("w", tk))
                for tk in reads:
                    lst = self.resr.setdefault(tk, [])
                    for i in range(len(lst) - 1, -1, -1):
                        if lst[i] == (e, self.cnt[e]):
                            lst[i] = (e, None)
                            break
                    self.pend_f[e].append(("r", tk))


def gr(reg, lo, hi):
    return [f"{reg}g{i}" for i in range(lo // 1024, (hi + 1023) // 1024)]


def build_consts():
    n = np.arange(L)
    k = np.arange(D)
    ph = 2 * np.pi * np.outer(n, k) / L
    dft_cos = np.cos(ph).astype(ml_dtypes.bfloat16)
    dft_sinn = (-np.sin(ph)).astype(ml_dtypes.bfloat16)
    w = np.full(D, 2.0, np.float64)
    w[0] = 1.0
    phT = 2 * np.pi * np.outer(np.arange(D), n) / L
    crt = ((w[:, None] / L) * np.cos(phT)).astype(ml_dtypes.bfloat16)
    cit = (-(w[:, None] / L) * np.sin(phT)).astype(ml_dtypes.bfloat16)
    crt_nyq = ((1.0 / L) * np.where(n % 2 == 0, 1.0, -1.0)).astype(ml_dtypes.bfloat16)[None, :]
    altcol = np.where(np.arange(128) % 2 == 0, 1.0, -1.0).astype(ml_dtypes.bfloat16)[:, None]
    eexp = np.zeros((H, D), ml_dtypes.bfloat16)
    for h in range(H):
        eexp[h, h * DH:(h + 1) * DH] = 1.0
    identb = np.eye(128, dtype=ml_dtypes.bfloat16)
    identf = np.eye(128, dtype=np.float32)
    return dict(dft_cos=dft_cos, dft_sinn=dft_sinn, crt=crt, cit=cit,
                crt_nyq=crt_nyq, altcol=altcol, eexp=eexp,
                identb=identb, identf=identf)


def build_program(debug=False):
    nc = bass.Bass("TRN2", target_bir_lowering=False, debug=False)
    S = Sched()
    em = S.emit

    x_in = nc.dram_tensor("x", [NB, L, D], F32, kind="ExternalInput").ap()
    W_ins = {nm: nc.dram_tensor(nm, [D, D], F32, kind="ExternalInput").ap()
             for nm in ("Wq", "Wk", "Wv", "Wo")}
    b_ins = {nm: nc.dram_tensor(nm, [1, D], F32, kind="ExternalInput").ap()
             for nm in ("bq", "bk", "bv", "bo")}
    wg_in = nc.dram_tensor("wg", [D, HID], I32, kind="ExternalInput").ap()
    wu_in = nc.dram_tensor("wu", [D, HID], I32, kind="ExternalInput").ap()
    wd_in = nc.dram_tensor("wd", [HID, D], I32, kind="ExternalInput").ap()
    c_cos = nc.dram_tensor("dft_cos", [L, D], BF16, kind="ExternalInput").ap()
    c_sinn = nc.dram_tensor("dft_sinn", [L, D], BF16, kind="ExternalInput").ap()
    c_crt = nc.dram_tensor("crt", [D, L], BF16, kind="ExternalInput").ap()
    c_cit = nc.dram_tensor("cit", [D, L], BF16, kind="ExternalInput").ap()
    c_crtnyq = nc.dram_tensor("crt_nyq", [1, L], BF16, kind="ExternalInput").ap()
    c_altcol = nc.dram_tensor("altcol", [128, 1], BF16, kind="ExternalInput").ap()
    c_eexp = nc.dram_tensor("eexp", [H, D], BF16, kind="ExternalInput").ap()
    c_identb = nc.dram_tensor("identb", [128, 128], BF16, kind="ExternalInput").ap()
    c_identf = nc.dram_tensor("identf", [128, 128], F32, kind="ExternalInput").ap()
    out_d = nc.dram_tensor("out", [NB, L, D], F32, kind="ExternalOutput").ap()

    w_scr = {nm: nc.dram_tensor(nm.lower() + "_s", [D, D], BF16, kind="Internal").ap()
             for nm in ("Wq", "Wk", "Wv", "Wo")}
    wgT_s = nc.dram_tensor("wgT_s", [D, HID], BF16, kind="Internal").ap()
    wuT_s = nc.dram_tensor("wuT_s", [D, HID], BF16, kind="Internal").ap()
    wdT_s = nc.dram_tensor("wdT_s", [HID, D], BF16, kind="Internal").ap()
    attn_scr = nc.dram_tensor("attn_scr", [L, D], BF16, kind="Internal").ap()

    dbg = {}
    if debug:
        for nm, shp, dt in [("d_xnT", [128, 8, L], BF16), ("d_qr", [128, 8, D], BF16),
                            ("d_qi", [128, 8, D], BF16), ("d_kr", [128, 8, D], BF16),
                            ("d_vr", [128, 8, D], BF16), ("d_eT", [H, NF], F32),
                            ("d_scT", [H, NF], F32), ("d_attnT", [H, NF], BF16),
                            ("d_um", [H, NF], BF16), ("d_xre", [128, 8, D], BF16),
                            ("d_xim", [128, 8, D], BF16), ("d_outT", [128, 8, L], BF16),
                            ("d_attn", [L, D], BF16), ("d_qn", [1, D], BF16),
                            ("d_qmr", [1, D], BF16), ("d_qmi", [1, D], BF16),
                            ("d_sig", [128, 16, D], BF16), ("d_h", [128, 8, L], BF16)]:
            dbg[nm] = nc.dram_tensor(nm, shp, dt, kind="ExternalOutput").ap()

    from contextlib import ExitStack
    stack = ExitStack()
    E = stack.enter_context

    r1 = E(nc.sbuf_tensor("r1", [128, 16384], BF16))
    r2 = E(nc.sbuf_tensor("r2", [128, 16384], BF16))
    r3 = E(nc.sbuf_tensor("r3", [128, 8192], BF16))
    r4 = E(nc.sbuf_tensor("r4", [128, 8192], BF16))
    r5 = E(nc.sbuf_tensor("r5", [128, 8192], BF16))
    r6 = E(nc.sbuf_tensor("r6", [128, 8192], BF16))
    strm = E(nc.sbuf_tensor("strm", [128, 8192], BF16))
    spA = E(nc.sbuf_tensor("spA", [128, 1040], F32))
    spB = E(nc.sbuf_tensor("spB", [128, 1040], F32))
    spC = E(nc.sbuf_tensor("spC", [128, 1040], F32))
    spD = E(nc.sbuf_tensor("spD", [128, 1040], F32))
    spE = E(nc.sbuf_tensor("spE", [128, 2048], BF16))
    e_fh = E(nc.sbuf_tensor("e_fh", [128, 8, 16], F32))
    s_fh = E(nc.sbuf_tensor("s_fh", [128, 8, 16], F32))
    max8 = E(nc.sbuf_tensor("max8", [16, 8], F32))
    enrow = E(nc.sbuf_tensor("enrow", [1, 16], F32))
    snrow = E(nc.sbuf_tensor("snrow", [1, 16], F32))
    mxx = E(nc.sbuf_tensor("mxx", [16, 1], F32))
    negmx = E(nc.sbuf_tensor("negmx", [16, 1], F32))
    smr = E(nc.sbuf_tensor("smr", [16, 1], F32))
    rsm = E(nc.sbuf_tensor("rsm", [16, 1], F32))
    msum = [E(nc.sbuf_tensor(f"msum{i}", [128, 1], F32)) for i in range(4)]
    ssq = [E(nc.sbuf_tensor(f"ssq{i}", [128, 1], F32)) for i in range(4)]
    meanv = [E(nc.sbuf_tensor(f"meanv{i}", [128, 1], F32)) for i in range(4)]
    m2v = [E(nc.sbuf_tensor(f"m2v{i}", [128, 1], F32)) for i in range(4)]
    varv = [E(nc.sbuf_tensor(f"varv{i}", [128, 1], F32)) for i in range(4)]
    sdv = [E(nc.sbuf_tensor(f"sdv{i}", [128, 1], F32)) for i in range(4)]
    rstd = [E(nc.sbuf_tensor(f"rstd{i}", [128, 1], F32)) for i in range(4)]
    epsv = E(nc.sbuf_tensor("epsv", [128, 1], F32))
    tmpA = E(nc.sbuf_tensor("tmpA", [128, 512], F32))
    tmpB = E(nc.sbuf_tensor("tmpB", [128, 512], F32))
    tmpC = E(nc.sbuf_tensor("tmpC", [1, 1024], F32))
    qmr = E(nc.sbuf_tensor("qmr", [1, 1024], BF16))
    qmi = E(nc.sbuf_tensor("qmi", [1, 1024], BF16))
    qtr = E(nc.sbuf_tensor("qtr", [16, 1024], BF16))
    qti = E(nc.sbuf_tensor("qti", [16, 1024], BF16))
    umF = E(nc.sbuf_tensor("umF", [128, 8, 16], BF16))
    qtiT = E(nc.sbuf_tensor("qtiT", [128, 8, 16], BF16))
    eexp_t = E(nc.sbuf_tensor("eexp_t", [16, 1024], BF16))
    identb_t = E(nc.sbuf_tensor("identb_t", [128, 128], BF16))
    identf_t = E(nc.sbuf_tensor("identf_t", [128, 128], F32))
    altcol_t = E(nc.sbuf_tensor("altcol_t", [128, 1], BF16))
    onesrow = E(nc.sbuf_tensor("onesrow", [1, 128], BF16))
    onescol = E(nc.sbuf_tensor("onescol", [128, 1], BF16))
    bias_t = {nm: E(nc.sbuf_tensor(nm + "_t", [1, 1024], BF16))
              for nm in ("bq", "bk", "bv", "bo")}
    nyq_qr = E(nc.sbuf_tensor("nyq_qr", [1, 1024], BF16))
    nyq_kr = E(nc.sbuf_tensor("nyq_kr", [1, 1024], BF16))
    nyq_vr = E(nc.sbuf_tensor("nyq_vr", [1, 1024], BF16))
    nyq_xre = E(nc.sbuf_tensor("nyq_xre", [1, 1024], BF16))
    crtn_t = E(nc.sbuf_tensor("crtn_t", [1, 1024], BF16))
    crtn_b = E(nc.sbuf_tensor("crtn_b", [1, 1024], BF16))
    pw = [E(nc.psum_tensor(f"pw{i}", [128, 1024], F32)) for i in range(4)]

    semT = E(nc.semaphore("semT"))
    semV = E(nc.semaphore("semV"))
    semS = E(nc.semaphore("semS"))
    semQ = E(nc.semaphore("semQ"))
    semG = E(nc.semaphore("semG"))
    semSD = E(nc.semaphore("semSD"))
    SEMS = {"t": semT, "v": semV, "s": semS, "q": semQ, "g": semG, "sd": semSD}
    block = E(nc.Block())

    def rv(t, a, b):
        return t[:, 0:a * b].rearrange("p (a b) -> p a b", a=a)

    xnT = rv(r1, 8, 2048)
    sig = rv(r2, 16, 1024)
    crci = rv(r2, 16, 1024)
    qr_v = rv(r3, 8, 1024)
    qi_v = rv(r4, 8, 1024)
    kr_v = rv(r5, 8, 1024)
    vr_v = rv(r6, 8, 1024)
    xre_v = qr_v
    xim_v = qi_v
    cmS = r4[0:16, 0:2048]
    wqim = r4[0:16, 2048:3072]
    outT_lo = rv(r5, 4, 2048)
    outT_hi = rv(r6, 4, 2048)

    def outT(chi):
        return outT_lo[:, chi, :] if chi < 4 else outT_hi[:, chi - 4, :]

    def outT_tok(chi, lo=0, hi=2048):
        reg = "r5" if chi < 4 else "r6"
        base = (chi % 4) * 2048
        return gr(reg, base + lo, base + hi)

    hTv = [rv(r2, 8, 2048), rv(r3, 4, 2048), rv(r4, 4, 2048),
           rv(r5, 4, 2048), rv(r6, 4, 2048)]

    def hT(hc):
        if hc < 8:
            return hTv[0][:, hc, :]
        return hTv[1 + (hc - 8) // 4][:, (hc - 8) % 4, :]

    def hT_tok(hc, lo=0, hi=2048):
        reg = "r2" if hc < 8 else ("r3", "r4", "r5", "r6")[(hc - 8) // 4]
        base = (hc * 2048 if hc < 8 else ((hc - 8) % 4) * 2048)
        return gr(reg, base + lo, base + hi)

    wdres_a = rv(r1, 16, 1024)
    wdres_b = rv(strm, 8, 1024)

    def wdres(hc):
        return wdres_a[:, hc, :] if hc < 16 else wdres_b[:, hc - 16, :]

    def wdres_tok(hc):
        return gr("r1", hc * 1024, (hc + 1) * 1024) if hc < 16 else \
            gr("st", (hc - 16) * 1024, (hc - 15) * 1024)

    wbuf = rv(strm, 8, 1024)
    cosslot = rv(strm, 32, 128)
    sinslot4 = rv(strm, 64, 128)
    wgslot = rv(strm, 8, 1024)
    siluslot = rv(strm, 8, 1024)

    eT = spA[0:16, 0:1025]
    scT = spB[0:16, 0:1025]
    attnT = spC[:].bitcast(BF16)[0:16, 0:1025]
    um = spD[:].bitcast(BF16)[0:16, 0:1025]
    xstage = [spA[:, 0:1024], spB[:, 0:1024]]
    ostage = [spC[:, 0:1024], spD[:, 0:1024]]
    istage = [spA[:].bitcast(I32)[:, 0:1024], spB[:].bitcast(I32)[:, 0:1024]]
    bstage = [spC[:].bitcast(BF16)[:, 0:2048], spD[:].bitcast(BF16)[:, 0:2048]]
    lnx = [r5[:].bitcast(F32)[:, i * 1024:(i + 1) * 1024] for i in range(4)]
    LNX_TK = [gr("r5", i * 2048, (i + 1) * 2048) for i in range(4)]
    lnxn = [r6[:, 0:1024], r6[:, 1024:2048], r6[:, 6144:7168], r6[:, 7168:8192]]
    LNXN_TK = [gr("r6", 0, 1024), gr("r6", 1024, 2048),
               gr("r6", 6144, 7168), gr("r6", 7168, 8192)]
    sqdump = [r6[:].bitcast(F32)[:, 1024:2048], r6[:].bitcast(F32)[:, 2048:3072]]
    SQD_TK = [gr("r6", 2048, 4096), gr("r6", 4096, 6144)]
    pwb = [p[:].bitcast(BF16) for p in pw]

    em("q", lambda e: e.dma_start(out=identb_t[:], in_=c_identb), writes=["identb"], dma=True)
    em("q", lambda e: e.dma_start(out=identf_t[:], in_=c_identf), writes=["identf"], dma=True)
    em("q", lambda e: e.dma_start(out=altcol_t[:], in_=c_altcol), writes=["altcol"], dma=True)
    em("q", lambda e: e.dma_start(out=eexp_t[:], in_=c_eexp), writes=["eexp"], dma=True)
    em("v", lambda e: e.memset(onesrow[:], 1.0), writes=["onesrow"])
    em("v", lambda e: e.memset(epsv[:], 1e-5), writes=["epsv"])
    em("v", lambda e: e.memset(onescol[:], 1.0), writes=["onescol"])
    for nm in ("bq", "bk", "bv", "bo"):
        em("q", lambda e, nm=nm: e.dma_start(out=lnx[0][0:1, :], in_=b_ins[nm]),
           writes=LNX_TK[0], dma=True)
        em("v", lambda e, nm=nm: e.tensor_copy(bias_t[nm][:], lnx[0][0:1, :]),
           reads=LNX_TK[0], writes=[nm])
    sp4 = [spA[:, 0:1024], spB[:, 0:1024], spC[:, 0:1024], spD[:, 0:1024]]
    sp4i = [t.bitcast(I32) for t in sp4]
    SP4 = [["spA"], ["spB"], ["spC"], ["spD"]]
    spEh = [spE[:, 0:1024], spE[:, 1024:2048]]
    prep_n = {"i": 0}
    prep_pipe = []

    def prep_one(src_ap, dst_ap, dtk, int32):
        i = prep_n["i"]
        prep_n["i"] += 1
        par, ph = i % 4, i % 2
        it = sp4i[par] if int32 else sp4[par]
        em("q", lambda e, it=it, src_ap=src_ap:
           e.dma_start(out=it, in_=src_ap), writes=SP4[par], dma=True)

        def cast_store(i=i, it=it, par=par, ph=ph, dst_ap=dst_ap, dtk=dtk):
            em("v" if i % 2 == 0 else "s", lambda e, it=it, ph=ph, i=i:
               (e.tensor_copy if i % 2 == 0 else e.copy)(spEh[ph], it),
               reads=SP4[par], writes=[f"spE{ph}"])
            em("q", lambda e, ph=ph, dst_ap=dst_ap: e.dma_start(out=dst_ap, in_=spEh[ph]),
               reads=[f"spE{ph}"], writes=[dtk], dma=True)

        prep_pipe.append(cast_store)
        if len(prep_pipe) > 3:
            prep_pipe.pop(0)()

    def prep_flush():
        while prep_pipe:
            prep_pipe.pop(0)()

    prep_pending = []

    def prep_service(k):
        for _ in range(k):
            if prep_pending:
                prep_one(*prep_pending.pop(0))

    def prep_attn(nm):
        for dc in range(8):
            prep_one(W_ins[nm][dc * 128:(dc + 1) * 128, :],
                     w_scr[nm][dc * 128:(dc + 1) * 128, :], f"scr{nm}", False)
        prep_flush()

    def prep_attn_pend(nm):
        for dc in range(8):
            prep_pending.append((W_ins[nm][dc * 128:(dc + 1) * 128, :],
                                 w_scr[nm][dc * 128:(dc + 1) * 128, :], f"scr{nm}", False))

    def prep_chunk(idx, src_ap, dst_ap, dtk):
        prep_one(src_ap, dst_ap, dtk, True)

    ffn_chunks = []
    for hc in range(24):
        ffn_chunks.append((wd_in[hc * 128:(hc + 1) * 128, :], wdT_s[hc * 128:(hc + 1) * 128, :], "scrwd"))
    prep_state = {"i": 0}

    def prep_some(n):
        i0 = prep_state["i"]
        for i in range(i0, min(i0 + n, len(ffn_chunks))):
            prep_chunk(i, *ffn_chunks[i])
        prep_state["i"] = min(i0 + n, len(ffn_chunks))
        prep_flush()

    def phase_ln(b):
        for tau in range(16):
            par = tau % 4
            pwi = 2 + (tau % 2)
            em("q", lambda e, b=b, tau=tau, par=par:
               e.dma_start(out=lnx[par], in_=x_in[b, tau * 128:(tau + 1) * 128, :]),
               writes=LNX_TK[par], dma=True, tag=f"lnx{tau}")
            em("v", lambda e, par=par: e.reduce_sum(msum[par][:], lnx[par], axis=AX.X),
               reads=LNX_TK[par], writes=[f"msum{par}"])
            em("s", lambda e, par=par: e.activation(sqdump[par % 2], lnx[par], AF.Square,
                                                    accum_out=ssq[par][:]),
               reads=LNX_TK[par], writes=[f"ssq{par}"] + SQD_TK[par % 2])
            em("v", lambda e, par=par: e.tensor_scalar(out=meanv[par][:], in0=msum[par][:],
                                                       scalar1=1.0 / D, scalar2=None, op0=ALU.mult),
               reads=[f"msum{par}"], writes=[f"meanv{par}"])
            em("v", lambda e, par=par: e.tensor_mul(m2v[par][:], meanv[par][:], meanv[par][:]),
               reads=[f"meanv{par}"], writes=[f"m2v{par}"])
            em("v", lambda e, par=par: e.scalar_tensor_tensor(out=varv[par][:], in0=ssq[par][:],
                                                              scalar=1.0 / D, in1=m2v[par][:],
                                                              op0=ALU.mult, op1=ALU.subtract),
               reads=[f"ssq{par}", f"m2v{par}"], writes=[f"varv{par}"])
            em("s", lambda e, par=par: e.activation(sdv[par][:], varv[par][:], AF.Sqrt, bias=epsv[:]),
               reads=[f"varv{par}", "epsv"], writes=[f"sdv{par}"])
            em("v", lambda e, par=par: e.reciprocal(rstd[par][:], sdv[par][:]),
               reads=[f"sdv{par}"], writes=[f"rstd{par}"])
            em("v", lambda e, par=par: e.tensor_scalar(out=lnxn[par], in0=lnx[par],
                                                       scalar1=meanv[par][:], scalar2=rstd[par][:],
                                                       op0=ALU.subtract, op1=ALU.mult),
               reads=LNX_TK[par] + [f"meanv{par}", f"rstd{par}"], writes=LNXN_TK[par])
            for dc in range(8):
                em("t", lambda e, par=par, dc=dc, pwi=pwi:
                   e.transpose(pwb[pwi][:, dc * 128:(dc + 1) * 128],
                               lnxn[par][:, dc * 128:(dc + 1) * 128], identb_t[:]),
                   reads=LNXN_TK[par] + ["identb"], writes=[f"pw{pwi}"], inc=(dc == 7))
            em("s", lambda e, tau=tau, pwi=pwi:
               e.copy(xnT[:, :, tau * 128:(tau + 1) * 128],
                      pwb[pwi][:, 0:1024].rearrange("p (dc c) -> p dc c", dc=8)),
               reads=[f"pw{pwi}"], writes=[f"r1c{tau}"])
        if debug and b == 0:
            em("g", lambda e: e.dma_start(out=dbg["d_xnT"], in_=xnT),
               reads=[f"r1c{t}" for t in range(16)], dma=True)

    def prefetch_wbuf(b, si):
        nm = ("Wq", "Wk", "Wv")[si]
        for dc in range(8):
            em("q", lambda e, nm=nm, dc=dc:
               e.dma_start(out=wbuf[:, dc, :], in_=w_scr[nm][dc * 128:(dc + 1) * 128, :]),
               reads=[f"scr{nm}"], writes=gr("st", dc * 1024, (dc + 1) * 1024), dma=True)

    def phase_proj(b, si):
        bias = bias_t[("bq", "bk", "bv")[si]]
        btk = ("bq", "bk", "bv")[si]
        reg = "st"
        wv = wbuf
        for tau in range(16):
            pwi = tau % 4
            stok = gr("r2", tau * 1024, (tau + 1) * 1024)
            for dc in range(8):
                for eta in range(2):
                    em("t", lambda e, tau=tau, dc=dc, eta=eta, pwi=pwi, wv=wv:
                       e.matmul(pw[pwi][:, eta * 512:(eta + 1) * 512],
                                xnT[:, dc, tau * 128:(tau + 1) * 128],
                                wv[:, dc, eta * 512:(eta + 1) * 512],
                                start=(dc == 0), stop=False),
                       reads=[f"r1c{tau}"] + gr(reg, dc * 1024, (dc + 1) * 1024),
                       writes=[f"pw{pwi}"], inc=False)
            for eta in range(2):
                em("t", lambda e, eta=eta, pwi=pwi, bias=bias:
                   e.matmul(pw[pwi][:, eta * 512:(eta + 1) * 512], onesrow[0:1, :],
                            bias[0:1, eta * 512:(eta + 1) * 512], start=False, stop=True),
                   reads=["onesrow", btk], writes=[f"pw{pwi}"], inc=(eta == 1))
            if tau % 2 == 0:
                em("s", lambda e, tau=tau, pwi=pwi: e.copy(sig[:, tau, :], pw[pwi][:, :]),
                   reads=[f"pw{pwi}"], writes=stok)
            else:
                em("v", lambda e, tau=tau, pwi=pwi: e.tensor_copy(sig[:, tau, :], pw[pwi][:, :]),
                   reads=[f"pw{pwi}"], writes=stok)
            if b == 0:
                prep_service(1)
        if debug and b == 0 and si == 0:
            em("g", lambda e: e.dma_start(out=dbg["d_sig"], in_=sig),
               reads=gr("r2", 0, 16384), dma=True)

    def phase_ln_proj(b):
        def lndma(tau):
            par = tau % 4
            em("q", lambda e, b=b, tau=tau, par=par:
               e.dma_start(out=lnx[par], in_=x_in[b, tau * 128:(tau + 1) * 128, :]),
               writes=LNX_TK[par], dma=True, tag=f"lnx{tau}")

        def lnchain(tau):
            par = tau % 4
            em("v", lambda e, par=par: e.reduce_sum(msum[par][:], lnx[par], axis=AX.X),
               reads=LNX_TK[par], writes=[f"msum{par}"])
            em("s", lambda e, par=par: e.activation(sqdump[par % 2], lnx[par], AF.Square,
                                                    accum_out=ssq[par][:]),
               reads=LNX_TK[par], writes=[f"ssq{par}"] + SQD_TK[par % 2])
            em("v", lambda e, par=par: e.tensor_scalar(out=meanv[par][:], in0=msum[par][:],
                                                       scalar1=1.0 / D, scalar2=None, op0=ALU.mult),
               reads=[f"msum{par}"], writes=[f"meanv{par}"])
            em("v", lambda e, par=par: e.tensor_mul(m2v[par][:], meanv[par][:], meanv[par][:]),
               reads=[f"meanv{par}"], writes=[f"m2v{par}"])
            em("v", lambda e, par=par: e.scalar_tensor_tensor(out=varv[par][:], in0=ssq[par][:],
                                                              scalar=1.0 / D, in1=m2v[par][:],
                                                              op0=ALU.mult, op1=ALU.subtract),
               reads=[f"ssq{par}", f"m2v{par}"], writes=[f"varv{par}"])
            em("s", lambda e, par=par: e.activation(sdv[par][:], varv[par][:], AF.Sqrt, bias=epsv[:]),
               reads=[f"varv{par}", "epsv"], writes=[f"sdv{par}"])
            em("v", lambda e, par=par: e.reciprocal(rstd[par][:], sdv[par][:]),
               reads=[f"sdv{par}"], writes=[f"rstd{par}"])
            em("v", lambda e, par=par: e.tensor_scalar(out=lnxn[par], in0=lnx[par],
                                                       scalar1=meanv[par][:], scalar2=rstd[par][:],
                                                       op0=ALU.subtract, op1=ALU.mult),
               reads=LNX_TK[par] + [f"meanv{par}", f"rstd{par}"], writes=LNXN_TK[par])

        def transp(tau):
            par = tau % 4
            pwi = 2 + (tau % 2)
            for dc in range(8):
                em("t", lambda e, par=par, dc=dc, pwi=pwi:
                   e.transpose(pwb[pwi][:, dc * 128:(dc + 1) * 128],
                               lnxn[par][:, dc * 128:(dc + 1) * 128], identb_t[:]),
                   reads=LNXN_TK[par] + ["identb"], writes=[f"pw{pwi}"], inc=(dc == 7))
            em("s", lambda e, tau=tau, pwi=pwi:
               e.copy(xnT[:, :, tau * 128:(tau + 1) * 128],
                      pwb[pwi][:, 0:1024].rearrange("p (dc c) -> p dc c", dc=8)),
               reads=[f"pw{pwi}"], writes=[f"r1c{tau}"])

        def projmms(tau):
            ppw = tau % 2
            stok = gr("r2", tau * 1024, (tau + 1) * 1024)
            for dc in range(8):
                for eta in range(2):
                    em("t", lambda e, tau=tau, dc=dc, eta=eta, ppw=ppw:
                       e.matmul(pw[ppw][:, eta * 512:(eta + 1) * 512],
                                xnT[:, dc, tau * 128:(tau + 1) * 128],
                                wbuf[:, dc, eta * 512:(eta + 1) * 512],
                                start=(dc == 0), stop=False),
                       reads=[f"r1c{tau}"] + gr("st", dc * 1024, (dc + 1) * 1024),
                       writes=[f"pw{ppw}"], inc=False)
            for eta in range(2):
                em("t", lambda e, eta=eta, ppw=ppw:
                   e.matmul(pw[ppw][:, eta * 512:(eta + 1) * 512], onesrow[0:1, :],
                            bias_t["bq"][0:1, eta * 512:(eta + 1) * 512], start=False, stop=True),
                   reads=["onesrow", "bq"], writes=[f"pw{ppw}"], inc=(eta == 1))
            em("v", lambda e, tau=tau, ppw=ppw: e.tensor_copy(sig[:, tau, :], pw[ppw][:, :]),
               reads=[f"pw{ppw}"], writes=stok)

        for tau in range(4):
            lndma(tau)
        prefetch_wbuf(b, 0)
        em("q", lambda e: e.dma_start(
            out=spE[:].rearrange("p (tau c) -> p tau c", tau=16),
            in_=c_cos[:, 0:128].rearrange("(tau p) c -> p tau c", p=128)),
           writes=["spE0", "spE1"], dma=True)
        em("q", lambda e: e.dma_start(
            out=tmpA[:].bitcast(BF16).rearrange("p (tau c) -> p tau c", tau=8),
            in_=c_sinn[0:1024, 0:128].rearrange("(tau p) c -> p tau c", p=128)),
           writes=["tmpA"], dma=True)
        em("q", lambda e: e.dma_start(
            out=tmpB[:].bitcast(BF16).rearrange("p (tau c) -> p tau c", tau=8),
            in_=c_sinn[1024:2048, 0:128].rearrange("(tau p) c -> p tau c", p=128)),
           writes=["tmpB"], dma=True)
        for tau in range(16):
            lnchain(tau)
            if tau + 4 < 16:
                lndma(tau + 4)
            transp(tau)
            if tau >= 1:
                projmms(tau - 1)
        projmms(15)

    def phase_fft(b, si):
        tlist = {0: [("qr", c_cos, 0, qr_v, "r3", 0), ("qi", c_sinn, 1, qi_v, "r4", 2)],
                 1: [("kr", c_cos, 0, kr_v, "r5", 0)],
                 2: [("vr", c_cos, 0, vr_v, "r6", 0)]}[si]
        hoist = (si == 0 and b > 0)
        spEc = spE[:].rearrange("p (tau c) -> p tau c", tau=16)
        tmpAs = tmpA[:].bitcast(BF16).rearrange("p (tau c) -> p tau c", tau=8)
        tmpBs = tmpB[:].bitcast(BF16).rearrange("p (tau c) -> p tau c", tau=8)
        for phi in range(8):
            par = phi % 2
            for (nm, mat, kind, dstv, reg, pwbase) in tlist:
                use_h = hoist and phi == 0
                slotv = cosslot[:, par * 16:(par + 1) * 16, :] if kind == 0 else \
                    sinslot4[:, 32 + par * 16:32 + (par + 1) * 16, :]
                slot_tk = gr("st", par * 2048, (par + 1) * 2048) if kind == 0 else \
                    gr("st", 4096 + par * 2048, 4096 + (par + 1) * 2048)
                if use_h:
                    def slot_ap(tau, kind=kind):
                        if kind == 0:
                            return spEc[:, tau, :], ["spE0", "spE1"]
                        if tau < 8:
                            return tmpAs[:, tau, :], ["tmpA"]
                        return tmpBs[:, tau - 8, :], ["tmpB"]
                else:
                    def slot_ap(tau, slotv=slotv, slot_tk=slot_tk):
                        return slotv[:, tau, :], slot_tk
                    em("q", lambda e, mat=mat, phi=phi, slotv=slotv:
                       e.dma_start(out=slotv, in_=mat[:, phi * 128:(phi + 1) * 128]
                                   .rearrange("(tau p) c -> p tau c", p=128)),
                       writes=slot_tk, dma=True, tag=f"fftdma{nm}{phi}")
                pwi = pwbase + par
                for tau in range(16):
                    sap, stk = slot_ap(tau)
                    for eta in range(2):
                        em("t", lambda e, sap=sap, tau=tau, eta=eta, pwi=pwi:
                           e.matmul(pw[pwi][:, eta * 512:(eta + 1) * 512],
                                    sap,
                                    sig[:, tau, eta * 512:(eta + 1) * 512],
                                    start=(tau == 0), stop=(tau == 15)),
                           reads=stk + gr("r2", tau * 1024, (tau + 1) * 1024),
                           writes=[f"pw{pwi}"], inc=(tau == 15 and eta == 1))
                eng = "s" if kind == 0 else "v"
                em(eng, lambda e, dstv=dstv, phi=phi, pwi=pwi, eng=eng:
                   (e.copy if eng == "s" else e.tensor_copy)(dstv[:, phi, :], pw[pwi][:, :]),
                   reads=[f"pw{pwi}"], writes=gr(reg, phi * 1024, (phi + 1) * 1024))
            if si == 0:
                for eta in range(2):
                    sl = slice(eta * 512, (eta + 1) * 512)
                    qtk = gr("r3", phi * 1024 + eta * 512, phi * 1024 + (eta + 1) * 512)
                    itk = gr("r4", phi * 1024 + eta * 512, phi * 1024 + (eta + 1) * 512)
                    em("v", lambda e, phi=phi, sl=sl: e.tensor_mul(tmpA[:], qr_v[:, phi, sl], qr_v[:, phi, sl]),
                       reads=qtk, writes=["tmpA"])
                    em("v", lambda e, phi=phi, sl=sl: e.tensor_mul(tmpB[:], qi_v[:, phi, sl], qi_v[:, phi, sl]),
                       reads=itk, writes=["tmpB"])
                    em("v", lambda e: e.tensor_add(tmpA[:], tmpA[:], tmpB[:]),
                       reads=["tmpA", "tmpB"], writes=["tmpA"])
                    em("s", lambda e: e.activation(tmpA[:], tmpA[:], AF.Sqrt),
                       reads=["tmpA"], writes=["tmpA"])
                    em("v", lambda e, phi=phi, eta=eta:
                       e.tensor_reduce(out=e_fh[:, phi, eta * 8:(eta + 1) * 8].unsqueeze(2),
                                       in_=tmpA[:].rearrange("p (g d) -> p g d", g=8),
                                       axis=AX.X, op=ALU.add),
                       reads=["tmpA"], writes=["e_fh"])
            if si == 1:
                for eta in range(2):
                    sl = slice(eta * 512, (eta + 1) * 512)
                    qtk = gr("r3", phi * 1024 + eta * 512, phi * 1024 + (eta + 1) * 512)
                    ktk = gr("r5", phi * 1024 + eta * 512, phi * 1024 + (eta + 1) * 512)
                    em("v", lambda e, phi=phi, sl=sl: e.tensor_mul(tmpB[:], qr_v[:, phi, sl], kr_v[:, phi, sl]),
                       reads=qtk + ktk, writes=["tmpB"])
                    em("v", lambda e, phi=phi, eta=eta:
                       e.tensor_reduce(out=s_fh[:, phi, eta * 8:(eta + 1) * 8].unsqueeze(2),
                                       in_=tmpB[:].rearrange("p (g d) -> p g d", g=8),
                                       axis=AX.X, op=ALU.add),
                       reads=["tmpB"], writes=["s_fh"])
            if b == 0 and si == 1:
                prep_service(1)
        nyq_dst, nyq_tk = [(nyq_qr, "nyq_qr"), (nyq_kr, "nyq_kr"), (nyq_vr, "nyq_vr")][si]
        for tau in range(16):
            for eta in range(2):
                em("t", lambda e, tau=tau, eta=eta:
                   e.matmul(pw[3][0:1, eta * 512:(eta + 1) * 512], altcol_t[:],
                            sig[:, tau, eta * 512:(eta + 1) * 512],
                            start=(tau == 0), stop=(tau == 15)),
                   reads=["altcol"] + gr("r2", tau * 1024, (tau + 1) * 1024),
                   writes=["pw3"], inc=(tau == 15 and eta == 1))
        em("s", lambda e, nyq_dst=nyq_dst: e.copy(nyq_dst[:], pw[3][0:1, :]),
           reads=["pw3"], writes=[nyq_tk])
        if si == 0:
            em("v", lambda e: e.tensor_mul(tmpC[:], nyq_qr[:], nyq_qr[:]),
               reads=["nyq_qr"], writes=["tmpC"])
            em("s", lambda e: e.activation(tmpC[:], tmpC[:], AF.Sqrt),
               reads=["tmpC"], writes=["tmpC"])
            em("v", lambda e: e.tensor_reduce(out=enrow[:].unsqueeze(2),
                                              in_=tmpC[:].rearrange("p (g d) -> p g d", g=16),
                                              axis=AX.X, op=ALU.add),
               reads=["tmpC"], writes=["enrow"])
        if si == 1:
            em("v", lambda e: e.tensor_mul(tmpC[:], nyq_qr[:], nyq_kr[:]),
               reads=["nyq_qr", "nyq_kr"], writes=["tmpC"])
            em("v", lambda e: e.tensor_reduce(out=snrow[:].unsqueeze(2),
                                              in_=tmpC[:].rearrange("p (g d) -> p g d", g=16),
                                              axis=AX.X, op=ALU.add),
               reads=["tmpC"], writes=["snrow"])
        if debug and b == 0:
            dv, dr = {0: (qr_v, "r3"), 1: (kr_v, "r5"), 2: (vr_v, "r6")}[si]
            dn = {0: "d_qr", 1: "d_kr", 2: "d_vr"}[si]
            em("g", lambda e, dv=dv, dn=dn: e.dma_start(out=dbg[dn], in_=dv),
               reads=gr(dr, 0, 8192), dma=True)
            if si == 0:
                em("g", lambda e: e.dma_start(out=dbg["d_qi"], in_=qi_v), reads=gr("r4", 0, 8192), dma=True)
                em("g", lambda e: e.dma_start(out=dbg["d_qn"], in_=nyq_qr[:]), reads=["nyq_qr"], dma=True)

    def phase_middle_pre(b):
        for (src, srctk, dst, dsttk, nrow, nrtk, pwi) in [
                (e_fh, "e_fh", eT, "spA", enrow, "enrow", 2),
                (s_fh, "s_fh", scT, "spB", snrow, "snrow", 3)]:
            for phi in range(8):
                em("t", lambda e, src=src, phi=phi, pwi=pwi:
                   e.transpose(pw[pwi][0:16, phi * 128:(phi + 1) * 128],
                               src[:, phi, :], identf_t[:]),
                   reads=[srctk, "identf"], writes=[f"pw{pwi}"], inc=(phi == 7))
            em("s", lambda e, dst=dst, pwi=pwi: e.copy(dst[:, 0:1024], pw[pwi][0:16, :]),
               reads=[f"pw{pwi}"], writes=[dsttk])
            em("t", lambda e, nrow=nrow, pwi=pwi:
               e.transpose(pw[pwi][0:16, 0:1], nrow[:], identf_t[0:1, 0:1]),
               reads=[nrtk, "identf", dsttk], writes=[f"pw{pwi}"])
            em("s", lambda e, dst=dst, pwi=pwi: e.copy(dst[:, 1024:1025], pw[pwi][0:16, 0:1]),
               reads=[f"pw{pwi}"], writes=[dsttk])

    def phase_middle_a(b):
        for it in range(8):
            em("v", lambda e: e.max(out=max8[:], in_=eT), reads=["spA"], writes=["max8"])
            em("v", lambda e: e.match_replace(out=eT, in_to_replace=max8[:], in_values=eT,
                                              imm_value=-1.0),
               reads=["max8", "spA"], writes=["spA"])
        if debug and b == 0:
            em("g", lambda e: e.dma_start(out=dbg["d_eT"], in_=eT), reads=["spA"], dma=True)
            em("g", lambda e: e.dma_start(out=dbg["d_scT"], in_=scT), reads=["spB"], dma=True)
        em("v", lambda e: e.tensor_scalar(out=um, in0=eT, scalar1=0.0, scalar2=None,
                                          op0=ALU.is_ge), reads=["spA"], writes=["spD"])
        em("v", lambda e: e.tensor_scalar(out=scT, in0=scT, scalar1=C1, scalar2=None,
                                          op0=ALU.mult), reads=["spB"], writes=["spB"])
        em("v", lambda e: e.scalar_tensor_tensor(out=scT, in0=um, scalar=-1e9, in1=scT,
                                                 op0=ALU.mult, op1=ALU.add),
           reads=["spD", "spB"], writes=["spB"])
        em("v", lambda e: e.reduce_max(mxx[:], scT, axis=AX.X), reads=["spB"], writes=["mxx"])
        em("v", lambda e: e.tensor_scalar(out=negmx[:], in0=mxx[:], scalar1=-1.0, scalar2=None,
                                          op0=ALU.mult), reads=["mxx"], writes=["negmx"])
        em("s", lambda e: e.activation(scT, scT, AF.Exp, bias=negmx[:]),
           reads=["spB", "negmx"], writes=["spB"])
        em("v", lambda e: e.reduce_sum(smr[:], scT, axis=AX.X), reads=["spB"], writes=["smr"])
        em("v", lambda e: e.reciprocal(rsm[:], smr[:]), reads=["smr"], writes=["rsm"])
        em("v", lambda e: e.tensor_scalar(out=attnT, in0=scT, scalar1=rsm[:], scalar2=None,
                                          op0=ALU.mult), reads=["spB", "rsm"], writes=["spC"])

    def phase_middle_b(b):
        for (vsp, reg, nrow, nrtk, dst, dtk, pwi) in [
                (qr_v, "r3", nyq_qr, "nyq_qr", qmr, "qmr", 0),
                (qi_v, "r4", None, None, qmi, "qmi", 1)]:
            for phi in range(8):
                last = (phi == 7 and nrow is None)
                for eta in range(2):
                    em("t", lambda e, vsp=vsp, phi=phi, eta=eta, pwi=pwi, last=last:
                       e.matmul(pw[pwi][0:1, eta * 512:(eta + 1) * 512], onescol[:],
                                vsp[:, phi, eta * 512:(eta + 1) * 512],
                                start=(phi == 0), stop=last),
                       reads=gr(reg, phi * 1024, (phi + 1) * 1024) + ["onescol"],
                       writes=[f"pw{pwi}"], inc=(last and eta == 1))
            if nrow is not None:
                for eta in range(2):
                    em("t", lambda e, nrow=nrow, eta=eta, pwi=pwi:
                       e.matmul(pw[pwi][0:1, eta * 512:(eta + 1) * 512], onesrow[0:1, 0:1],
                                nrow[0:1, eta * 512:(eta + 1) * 512], start=False, stop=True),
                       reads=[nrtk, "onesrow"], writes=[f"pw{pwi}"], inc=(eta == 1))
            em("s", lambda e, dst=dst, pwi=pwi: e.mul(dst[:], pw[pwi][0:1, :], 1.0 / NF),
               reads=[f"pw{pwi}"], writes=[dtk])
        for (qrow, qrtk, qt, qttk, pwi) in [(qmr, "qmr", qtr, "qtr", 0), (qmi, "qmi", qti, "qti", 1)]:
            for eta in range(2):
                em("t", lambda e, qrow=qrow, eta=eta, pwi=pwi:
                   e.matmul(pw[pwi][0:16, eta * 512:(eta + 1) * 512], onesrow[0:1, 0:16],
                            qrow[0:1, eta * 512:(eta + 1) * 512], start=True, stop=True),
                   reads=[qrtk, "onesrow"], writes=[f"pw{pwi}"], inc=(eta == 1))
            em("v", lambda e, qt=qt, pwi=pwi: e.tensor_mul(qt[:], pw[pwi][0:16, 0:1024], eexp_t[:]),
               reads=[f"pw{pwi}", "eexp"], writes=[qttk])
        if debug and b == 0:
            em("g", lambda e: e.dma_start(out=dbg["d_attnT"], in_=attnT), reads=["spC"], dma=True)
            em("g", lambda e: e.dma_start(out=dbg["d_um"], in_=um), reads=["spD"], dma=True)
            em("g", lambda e: e.dma_start(out=dbg["d_qmr"], in_=qmr[:]), reads=["qmr"], dma=True)
            em("g", lambda e: e.dma_start(out=dbg["d_qmi"], in_=qmi[:]), reads=["qmi"], dma=True)
        for phi in range(8):
            for eta in range(2):
                em("t", lambda e, phi=phi, eta=eta:
                   e.matmul(pw[0][:, eta * 512:(eta + 1) * 512],
                            attnT[:, phi * 128:(phi + 1) * 128],
                            eexp_t[:, eta * 512:(eta + 1) * 512], start=True, stop=True),
                   reads=["spC", "eexp"], writes=["pw0"], inc=(eta == 1))
                em("t", lambda e, phi=phi, eta=eta:
                   e.matmul(pw[1][:, eta * 512:(eta + 1) * 512],
                            um[:, phi * 128:(phi + 1) * 128],
                            qtr[:, eta * 512:(eta + 1) * 512], start=True, stop=True),
                   reads=["spD", "qtr"], writes=["pw1"], inc=(eta == 1))
            em("t", lambda e, phi=phi:
               e.transpose(pwb[2][:, phi * 16:(phi + 1) * 16],
                           um[:, phi * 128:(phi + 1) * 128], identb_t[0:16, 0:16]),
               reads=["spD", "identb"], writes=["pw2"], inc=(phi == 7))
            for eta in range(2):
                sl = slice(eta * 512, (eta + 1) * 512)
                vtk = gr("r6", phi * 1024 + eta * 512, phi * 1024 + (eta + 1) * 512)
                xtk = gr("r3", phi * 1024 + eta * 512, phi * 1024 + (eta + 1) * 512)
                em("v", lambda e, phi=phi, sl=sl: e.tensor_mul(tmpA[:], pw[0][:, sl], vr_v[:, phi, sl]),
                   reads=["pw0"] + vtk, writes=["tmpA"])
                em("v", lambda e, phi=phi, sl=sl: e.tensor_add(xre_v[:, phi, sl], tmpA[:], pw[1][:, sl]),
                   reads=["tmpA", "pw1"], writes=xtk)
        em("s", lambda e: e.copy(umF[:], pwb[2][:, 0:128].rearrange("p (a c) -> p a c", a=8)),
           reads=["pw2"], writes=["umF"])
        for c in range(8):
            em("t", lambda e, c=c:
               e.transpose(pwb[3][:, c * 16:(c + 1) * 16],
                           qti[:, c * 128:(c + 1) * 128], identb_t[0:16, 0:16]),
               reads=["qti", "identb"], writes=["pw3"], inc=(c == 7))
        em("s", lambda e: e.copy(qtiT[:], pwb[3][:, 0:128].rearrange("p (a c) -> p a c", a=8)),
           reads=["pw3"], writes=["qtiT"])
        for eta in range(2):
            for c in range(8):
                em("t", lambda e, c=c, eta=eta:
                   e.matmul(pw[2][0:16, eta * 512:(eta + 1) * 512], qtiT[:, c, :],
                            wbuf[:, c, eta * 512:(eta + 1) * 512],
                            start=(c == 0), stop=(c == 7)),
                   reads=["qtiT"] + gr("st", c * 1024, (c + 1) * 1024),
                   writes=["pw2"], inc=(c == 7 and eta == 1))
        em("s", lambda e: e.copy(wqim[:], pw[2][0:16, :]), reads=["pw2"], writes=gr("r4", 2048, 3072))
        for eta in range(2):
            em("t", lambda e, eta=eta:
               e.matmul(pw[0][0:1, eta * 512:(eta + 1) * 512], attnT[:, 1024:1025],
                        eexp_t[:, eta * 512:(eta + 1) * 512], start=True, stop=True),
               reads=["spC", "eexp"], writes=["pw0"], inc=(eta == 1))
            em("t", lambda e, eta=eta:
               e.matmul(pw[1][0:1, eta * 512:(eta + 1) * 512], um[:, 1024:1025],
                        qtr[:, eta * 512:(eta + 1) * 512], start=True, stop=True),
               reads=["spD", "qtr"], writes=["pw1"], inc=(eta == 1))
        em("v", lambda e: e.tensor_mul(tmpC[:], pw[0][0:1, :], nyq_vr[:]),
           reads=["pw0", "nyq_vr"], writes=["tmpC"])
        em("v", lambda e: e.tensor_add(nyq_xre[:], tmpC[:], pw[1][0:1, :]),
           reads=["tmpC", "pw1"], writes=["nyq_xre"])
        if debug and b == 0:
            em("g", lambda e: e.dma_start(out=dbg["d_xre"], in_=xre_v), reads=gr("r3", 0, 8192), dma=True)

    def phase_irfft(b):
        altv = []
        for phi in range(8):
            if phi < 5:
                altv.append((r4[:, 3072 + phi * 1024:4096 + phi * 1024],
                             gr("r4", 3072 + phi * 1024, 4096 + phi * 1024)))
            elif phi < 7:
                altv.append((spEh[phi - 5], [f"spE{phi - 5}"]))
            else:
                altv.append((tmpA[:].bitcast(BF16), ["tmpA"]))
        em("q", lambda e: e.dma_start(
            out=crci[:, 8:16, :],
            in_=c_cit[:, 0:1024].rearrange("(phi p) t -> p phi t", p=128)),
           writes=gr("r2", 8192, 16384), dma=True)
        em("sd", lambda e: e.dma_start(
            out=crci[:, 0:8, :],
            in_=c_crt[:, 0:1024].rearrange("(phi p) t -> p phi t", p=128)),
           writes=gr("r2", 0, 8192), dma=True, fence=False)
        S.emit_fence("sd")
        em("q", lambda e: e.dma_start(out=crtn_t[:], in_=c_crtnyq[:, 0:1024]),
           writes=["crtn"], dma=True)
        em("q", lambda e: e.dma_start(out=crtn_b[:], in_=c_crtnyq[:, 1024:2048]),
           writes=["crtnb"], dma=True)
        for phi in range(8):
            em("sd", lambda e, phi=phi: e.dma_start(
                out=altv[phi][0], in_=c_crt[phi * 128:(phi + 1) * 128, 1024:2048]),
               writes=altv[phi][1], dma=True, fence=False)
        for th in range(2):
            if th == 1:
                S.emit_fence("sd")
            for half in range(2):
                hsl = slice(half * 512, (half + 1) * 512)
                for phi in range(8):
                    em("t", lambda e, phi=phi, hsl=hsl, half=half:
                       e.matmul(pw[3][0:16, hsl], umF[:, phi, :], crci[:, 8 + phi, hsl],
                                start=(phi == 0), stop=(phi == 7)),
                       reads=["umF"] + gr("r2", 8192 + phi * 1024, 8192 + (phi + 1) * 1024),
                       writes=["pw3"], inc=(phi == 7 and half == 1))
            em("s", lambda e, th=th: e.copy(cmS[:, th * 1024:(th + 1) * 1024], pw[3][0:16, :]),
               reads=["pw3"], writes=gr("r4", 0, 2048))
            if th == 0:
                em("q", lambda e: e.dma_start(
                    out=crci[:, 8:16, :],
                    in_=c_cit[:, 1024:2048].rearrange("(phi p) t -> p phi t", p=128)),
                   writes=gr("r2", 8192, 16384), dma=True)
            nyqv, nyqtk = (crtn_t, "crtn") if th == 0 else (crtn_b, "crtnb")
            for chi in range(8):
                pwi = chi % 2
                for half in range(2):
                    hsl = slice(half * 512, (half + 1) * 512)
                    for phi in range(8):
                        if th == 0:
                            mv, mtk = crci[:, phi, hsl], gr("r2", phi * 1024, (phi + 1) * 1024)
                        else:
                            mv, mtk = altv[phi][0][:, hsl], altv[phi][1]
                        em("t", lambda e, chi=chi, phi=phi, hsl=hsl, pwi=pwi, mv=mv:
                           e.matmul(pw[pwi][:, hsl], xre_v[:, phi, chi * 128:(chi + 1) * 128],
                                    mv, start=(phi == 0), stop=False),
                           reads=gr("r3", phi * 1024 + chi * 128, phi * 1024 + (chi + 1) * 128) +
                           mtk, writes=[f"pw{pwi}"], inc=False)
                    em("t", lambda e, chi=chi, hsl=hsl, pwi=pwi, half=half, nyqv=nyqv:
                       e.matmul(pw[pwi][:, hsl], nyq_xre[0:1, chi * 128:(chi + 1) * 128],
                                nyqv[0:1, hsl], start=False, stop=True),
                       reads=["nyq_xre", nyqtk], writes=[f"pw{pwi}"], inc=(half == 1))
                eng = "s" if chi % 2 == 0 else "v"
                em(eng, lambda e, chi=chi, th=th, pwi=pwi, eng=eng:
                   (e.copy if eng == "s" else e.tensor_copy)(
                       outT(chi)[:, th * 1024:(th + 1) * 1024], pw[pwi][:, :]),
                   reads=[f"pw{pwi}"], writes=outT_tok(chi, th * 1024, (th + 1) * 1024))
        if debug and b == 0:
            em("g", lambda e: e.dma_start(out=dbg["d_outT"][:, 0:4, :], in_=outT_lo),
               reads=gr("r5", 0, 8192), dma=True)
            em("g", lambda e: e.dma_start(out=dbg["d_outT"][:, 4:8, :], in_=outT_hi),
               reads=gr("r6", 0, 8192), dma=True)

    def prefetch_wo(b):
        for dc in range(8):
            em("q", lambda e, dc=dc:
               e.dma_start(out=wbuf[:, dc, :], in_=w_scr["Wo"][dc * 128:(dc + 1) * 128, :]),
               reads=["scrWo"], writes=gr("st", dc * 1024, (dc + 1) * 1024), dma=True)

    def phase_wo(b):
        for tau in range(16):
            pwi = tau % 4
            par = tau % 2
            for dc in range(8):
                for eta in range(2):
                    em("t", lambda e, tau=tau, dc=dc, eta=eta, pwi=pwi:
                       e.matmul(pw[pwi][:, eta * 512:(eta + 1) * 512],
                                outT(dc)[:, tau * 128:(tau + 1) * 128],
                                wbuf[:, dc, eta * 512:(eta + 1) * 512],
                                start=(dc == 0), stop=False),
                       reads=outT_tok(dc, tau * 128, (tau + 1) * 128) +
                       gr("st", dc * 1024, (dc + 1) * 1024),
                       writes=[f"pw{pwi}"], inc=False)
            for eta in range(2):
                em("t", lambda e, tau=tau, eta=eta, pwi=pwi:
                   e.matmul(pw[pwi][:, eta * 512:(eta + 1) * 512],
                            cmS[:, tau * 128:(tau + 1) * 128],
                            wqim[:, eta * 512:(eta + 1) * 512], start=False, stop=False),
                   reads=gr("r4", 0, 3072), writes=[f"pw{pwi}"], inc=False)
                em("t", lambda e, eta=eta, pwi=pwi:
                   e.matmul(pw[pwi][:, eta * 512:(eta + 1) * 512], onesrow[0:1, :],
                            bias_t["bo"][0:1, eta * 512:(eta + 1) * 512], start=False, stop=True),
                   reads=["onesrow", "bo"], writes=[f"pw{pwi}"], inc=(eta == 1))
            em("s", lambda e, pwi=pwi, par=par:
               e.copy(spE[:, par * 1024:(par + 1) * 1024], pw[pwi][:, :]),
               reads=[f"pw{pwi}"], writes=[f"spE{par}"])
            em("q", lambda e, tau=tau, par=par:
               e.dma_start(out=attn_scr[tau * 128:(tau + 1) * 128, :],
                           in_=spE[:, par * 1024:(par + 1) * 1024]),
               reads=[f"spE{par}"], writes=["attn_scr"], dma=True)
        if debug and b == 0:
            em("g", lambda e: e.dma_start(out=dbg["d_attn"], in_=attn_scr),
               reads=["attn_scr"], dma=True)

    def phase_ffn_gu(b, hclist):
        stg = [spA[:].bitcast(I32)[:, 0:1024], spB[:].bitcast(I32)[:, 0:1024],
               spC[:].bitcast(I32)[:, 0:1024], spD[:].bitcast(I32)[:, 0:1024]]
        stk = [["spA"], ["spB"], ["spC"], ["spD"]]
        for hc in hclist:
            par = hc % 2
            pg, pu = (0, 1) if par == 0 else (2, 3)
            em("q", lambda e, hc=hc, par=par:
               e.dma_start(out=stg[par].rearrange("p (dc c) -> p dc c", dc=8),
                           in_=wg_in[:, hc * 128:(hc + 1) * 128]
                           .rearrange("(dc p) c -> p dc c", p=128)),
               writes=stk[par], dma=True)
            em("s", lambda e, hc=hc, par=par:
               e.copy(wgslot[:, par, :], stg[par]),
               reads=stk[par], writes=gr("st", par * 1024, (par + 1) * 1024))
            em("q", lambda e, hc=hc, par=par:
               e.dma_start(out=stg[2 + par].rearrange("p (dc c) -> p dc c", dc=8),
                           in_=wu_in[:, hc * 128:(hc + 1) * 128]
                           .rearrange("(dc p) c -> p dc c", p=128)),
               writes=stk[2 + par], dma=True)
            em("v", lambda e, hc=hc, par=par:
               e.tensor_copy(wgslot[:, 2 + par, :], stg[2 + par]),
               reads=stk[2 + par], writes=gr("st", (2 + par) * 1024, (3 + par) * 1024))
            for th in range(2):
                wgv = wgslot[:, par, :].rearrange("p (dc c) -> p dc c", dc=8)
                wuv = wgslot[:, 2 + par, :].rearrange("p (dc c) -> p dc c", dc=8)
                for dc in range(8):
                    for tq in range(2):
                        em("t", lambda e, wgv=wgv, dc=dc, th=th, tq=tq, pg=pg:
                           e.matmul(pw[pg][:, tq * 512:(tq + 1) * 512], wgv[:, dc, :],
                                    xnT[:, dc, th * 1024 + tq * 512:th * 1024 + (tq + 1) * 512],
                                    start=(dc == 0), stop=(dc == 7)),
                           reads=gr("st", par * 1024, (par + 1) * 1024) +
                           [f"r1c{th * 8 + tq * 4 + i}" for i in range(4)],
                           writes=[f"pw{pg}"], inc=(dc == 7 and tq == 1))
                for dc in range(8):
                    for tq in range(2):
                        em("t", lambda e, wuv=wuv, dc=dc, th=th, tq=tq, pu=pu:
                           e.matmul(pw[pu][:, tq * 512:(tq + 1) * 512], wuv[:, dc, :],
                                    xnT[:, dc, th * 1024 + tq * 512:th * 1024 + (tq + 1) * 512],
                                    start=(dc == 0), stop=(dc == 7)),
                           reads=gr("st", (2 + par) * 1024, (3 + par) * 1024) +
                           [f"r1c{th * 8 + tq * 4 + i}" for i in range(4)],
                           writes=[f"pw{pu}"], inc=(dc == 7 and tq == 1))
                em("s", lambda e, par=par, pg=pg:
                   e.activation(siluslot[:, 4 + par, :], pw[pg][:, :], AF.Silu, scale=SC),
                   reads=[f"pw{pg}"], writes=gr("st", (4 + par) * 1024, (5 + par) * 1024))
                em("v", lambda e, hc=hc, th=th, par=par, pu=pu:
                   e.tensor_mul(hT(hc)[:, th * 1024:(th + 1) * 1024],
                                siluslot[:, 4 + par, :], pw[pu][:, :]),
                   reads=gr("st", (4 + par) * 1024, (5 + par) * 1024) + [f"pw{pu}"],
                   writes=hT_tok(hc, th * 1024, (th + 1) * 1024))
        if debug and b == 0:
            em("g", lambda e: e.dma_start(out=dbg["d_h"], in_=hTv[0]),
               reads=gr("r2", 0, 16384), dma=True)

    def phase_down_final(b):
        em("q", lambda e: e.dma_start(out=wdres_a,
                                      in_=wdT_s[0:2048, :].rearrange("(hc p) d -> p hc d", p=128)),
           reads=["scrwd"], writes=gr("r1", 0, 16384) + [f"r1c{t}" for t in range(16)], dma=True)
        em("q", lambda e: e.dma_start(out=wdres_b,
                                      in_=wdT_s[2048:3072, :].rearrange("(hc p) d -> p hc d", p=128)),
           reads=["scrwd"], writes=gr("st", 0, 8192), dma=True)
        for tp in range(8):
            taus = (2 * tp, 2 * tp + 1)
            for i, tau in enumerate(taus):
                pwi = 2 * (tp % 2) + i
                for hc in range(24):
                    for eta in range(2):
                        em("t", lambda e, tau=tau, hc=hc, eta=eta, pwi=pwi:
                           e.matmul(pw[pwi][:, eta * 512:(eta + 1) * 512],
                                    hT(hc)[:, tau * 128:(tau + 1) * 128],
                                    wdres(hc)[:, eta * 512:(eta + 1) * 512],
                                    start=(hc == 0), stop=(hc == 23)),
                           reads=hT_tok(hc, tau * 128, (tau + 1) * 128) + wdres_tok(hc),
                           writes=[f"pw{pwi}"], inc=(hc == 23 and eta == 1))
            for i, tau in enumerate(taus):
                pwi = 2 * (tp % 2) + i
                par = tau % 2
                sp = ("spA", "spB")[par]
                op_ = ("spC", "spD")[par]
                em("q", lambda e, b=b, tau=tau, par=par:
                   e.dma_start(out=xstage[par], in_=x_in[b, tau * 128:(tau + 1) * 128, :]),
                   writes=[sp], dma=True)
                em("q", lambda e, tau=tau, par=par:
                   e.dma_start(out=spE[:, par * 1024:(par + 1) * 1024],
                               in_=attn_scr[tau * 128:(tau + 1) * 128, :]),
                   reads=["attn_scr"], writes=[f"spE{par}"], dma=True)
                for eta in range(2):
                    sl = slice(eta * 512, (eta + 1) * 512)
                    em("v", lambda e, pwi=pwi, par=par, sl=sl:
                       e.scalar_tensor_tensor(out=ostage[par][:, sl], in0=pw[pwi][:, sl],
                                              scalar=SC * SC, in1=xstage[par][:, sl],
                                              op0=ALU.mult, op1=ALU.add),
                       reads=[f"pw{pwi}", sp], writes=[op_])
                    em("v", lambda e, par=par, sl=sl:
                       e.tensor_add(ostage[par][:, sl], ostage[par][:, sl],
                                    spE[:, par * 1024:(par + 1) * 1024][:, sl]),
                       reads=[op_, f"spE{par}"], writes=[op_])
                em("sd", lambda e, b=b, tau=tau, par=par:
                   e.dma_start(out=out_d[b, tau * 128:(tau + 1) * 128, :], in_=ostage[par]),
                   reads=[op_], dma=True)

    for b in range(NB):
        if b == 0:
            prep_attn("Wq")
            prefetch_wbuf(b, 0)
            phase_ln(b)
            prep_attn_pend("Wk")
            prep_attn_pend("Wv")
            phase_proj(b, 0)
        else:
            phase_ln_proj(b)
        phase_fft(b, 0)
        if b == 0:
            prep_flush()
        prefetch_wbuf(b, 1)
        if b == 0:
            prep_attn_pend("Wo")
            prep_pending.extend((src, dst, tk, True) for (src, dst, tk) in ffn_chunks)
        phase_proj(b, 1)
        phase_fft(b, 1)
        prefetch_wbuf(b, 2)
        phase_proj(b, 2)
        if b == 0:
            prep_service(99)
            prep_flush()
        phase_middle_pre(b)
        phase_middle_a(b)
        phase_fft(b, 2)
        prefetch_wo(b)
        phase_middle_b(b)
        phase_irfft(b)
        phase_wo(b)
        phase_ffn_gu(b, range(0, 24))
        phase_down_final(b)

    fin_q, fin_g, fin_sd = S.cnt["q"], S.cnt["g"], S.cnt["sd"]
    S.emit("q", lambda e: e.wait_ge(semG, fin_g), inc=False)
    S.emit("q", lambda e: e.wait_ge(semSD, fin_sd), inc=False)
    S.emit("g", lambda e: e.wait_ge(semQ, fin_q), inc=False)

    def run_engine(skey, eng):
        for (ekey, (fn, waits, inc, amt, tag)) in S.streams[skey]:
            for (pe, pv) in waits:
                eng.wait_ge(SEMS[pe], pv)
            if fn is None:
                continue
            inst = fn(eng)
            if inc and inst is not None:
                inst.then_inc(SEMS[ekey], amt)

    @block.sync
    def _(eng):
        run_engine("q", eng)

    @block.tensor
    def _(eng):
        run_engine("t", eng)

    @block.vector
    def _(eng):
        run_engine("v", eng)

    @block.scalar
    def _(eng):
        run_engine("s", eng)

    @block.gpsimd
    def _(eng):
        run_engine("g", eng)

    stack.close()
    return nc, S.nops


from concourse.bass_utils import run_bass_kernel_spmd

_CACHED = {}


def kernel(**inputs):
    x = np.ascontiguousarray(inputs["L0_attn_query"], dtype=np.float32)
    if "prog" not in _CACHED:
        _CACHED["prog"] = build_program(debug=False)[0]
        _CACHED["consts"] = build_consts()
    nc = _CACHED["prog"]
    consts = _CACHED["consts"]
    shard = x.reshape(8, NB, L, D)
    base = dict(
        Wq=np.ascontiguousarray(inputs["L0_attn_Wq"], np.float32),
        Wk=np.ascontiguousarray(inputs["L0_attn_Wk"], np.float32),
        Wv=np.ascontiguousarray(inputs["L0_attn_Wv"], np.float32),
        Wo=np.ascontiguousarray(inputs["L0_attn_Wo"], np.float32),
        bq=np.ascontiguousarray(inputs["L0_attn_bq"], np.float32).reshape(1, D),
        bk=np.ascontiguousarray(inputs["L0_attn_bk"], np.float32).reshape(1, D),
        bv=np.ascontiguousarray(inputs["L0_attn_bv"], np.float32).reshape(1, D),
        bo=np.ascontiguousarray(inputs["L0_attn_bo"], np.float32).reshape(1, D),
        wg=np.ascontiguousarray(np.asarray(inputs["L0_ffn_w_gate"], np.int32).T),
        wu=np.ascontiguousarray(np.asarray(inputs["L0_ffn_w_up"], np.int32).T),
        wd=np.ascontiguousarray(np.asarray(inputs["L0_ffn_w_down"], np.int32).T),
    )
    for k, v in consts.items():
        base[k] = np.ascontiguousarray(v)
    in_maps = [dict(base, x=np.ascontiguousarray(shard[i])) for i in range(8)]
    res = run_bass_kernel_spmd(nc, in_maps, core_ids=list(range(8)))
    out = np.concatenate(
        [np.asarray(res.results[i]["out"], dtype=np.float32) for i in range(8)], axis=0)
    return out.reshape(32, L, D)


# version 29
# speedup vs baseline: 1.0395x; 1.0163x over previous
import numpy as np
import ml_dtypes
import concourse.bass as bass
import concourse.mybir as mybir

F32 = mybir.dt.float32
BF16 = mybir.dt.bfloat16
I32 = mybir.dt.int32
AF = mybir.ActivationFunctionType
ALU = mybir.AluOpType
AX = mybir.AxisListType

NB = 4
L = 2048
D = 1024
H = 16
DH = 64
NF = 1025
HID = 3072
SC = float(1.0 / 127.0)
C1 = float(1.0 / (np.sqrt(DH) + 1e-8))
ENGS = ("t", "v", "s", "q", "g", "sd")
STREAM_OF = {"sd": "s"}


class Sched:

    def __init__(self):
        self.ops = {e: [] for e in ENGS}
        self.cnt = {e: 0 for e in ENGS}
        self.seen = {e: {p: 0 for p in ENGS} for e in ENGS}
        self.snap = {e: {0: {p: 0 for p in ENGS}} for e in ENGS}
        self.resw = {}
        self.resr = {}
        self.pend = {e: [] for e in ENGS}
        self.pend_f = {e: [] for e in ENGS}
        self.streams = {e: [] for e in ENGS if e not in STREAM_OF}
        self.nops = 0

    def _push(self, e, entry):
        self.ops[e].append(entry)
        self.streams[STREAM_OF.get(e, e)].append((e, entry))

    def emit_fence(self, e):
        bnd = self.cnt[e]
        self._push(e, (None, [(e, bnd)], False, 0, "dmafence"))
        self.seen[e][e] = bnd
        clk = dict(self.seen[e])
        clk[e] = bnd
        self.snap[e][bnd] = clk
        for kind, tk in self.pend_f[e]:
            if kind == "w":
                if self.resw.get(tk, (None, 0))[1] is None and self.resw[tk][0] == e:
                    self.resw[tk] = (e, bnd)
            else:
                lst = self.resr.get(tk, [])
                for i, (re_, rv) in enumerate(lst):
                    if re_ == e and rv is None:
                        lst[i] = (e, bnd)
        self.pend_f[e] = []

    def emit(self, e, fn, reads=(), writes=(), inc=True, dma=False, fence=True, tag=""):
        waits = {}

        def need(pe, pv):
            if pv is None:
                assert pe == e, f"cross-engine dep on un-inc'd op, tag={tag}"
                return
            if self.seen[e].get(pe, 0) < pv:
                waits[pe] = max(waits.get(pe, 0), pv)

        for tk in reads:
            w = self.resw.get(tk)
            if w is not None:
                need(w[0], w[1])
        for tk in writes:
            w = self.resw.get(tk)
            if w is not None:
                need(w[0], w[1])
            for (re_, rv) in self.resr.get(tk, ()):
                need(re_, rv)
        for pe in list(waits):
            pv = waits[pe]
            self.seen[e][pe] = max(self.seen[e][pe], pv)
            sn = self.snap[pe].get(pv)
            if sn:
                for k2, v2 in sn.items():
                    if self.seen[e].get(k2, 0) < v2:
                        self.seen[e][k2] = v2
        amt = 16 if dma else 1
        self._push(e, (fn, sorted(waits.items()), inc, amt, tag))
        self.nops += 1
        if inc:
            self.cnt[e] += amt
            nc_ = self.cnt[e]
            for kind, tk in self.pend[e]:
                if kind == "w":
                    if self.resw.get(tk, (None, 0))[1] is None and self.resw[tk][0] == e:
                        self.resw[tk] = (e, nc_)
                else:
                    lst = self.resr.get(tk, [])
                    for i, (re_, rv) in enumerate(lst):
                        if re_ == e and rv is None:
                            lst[i] = (e, nc_)
            self.pend[e] = []
            clk = dict(self.seen[e])
            clk[e] = nc_
            self.snap[e][nc_] = clk
        for tk in writes:
            if inc:
                self.resw[tk] = (e, self.cnt[e])
            else:
                self.resw[tk] = (e, None)
                self.pend[e].append(("w", tk))
            self.resr[tk] = []
        for tk in reads:
            lst = self.resr.setdefault(tk, [])
            if inc:
                lst.append((e, self.cnt[e]))
            else:
                lst.append((e, None))
                self.pend[e].append(("r", tk))
        if dma and inc:
            if fence:
                self._push(e, (None, [(e, self.cnt[e])], False, 0, "dmafence"))
                self.seen[e][e] = self.cnt[e]
                bnd = self.cnt[e]
                for kind, tk in self.pend_f[e]:
                    if kind == "w":
                        if self.resw.get(tk, (None, 0))[1] is None and self.resw[tk][0] == e:
                            self.resw[tk] = (e, bnd)
                    else:
                        lst = self.resr.get(tk, [])
                        for i, (re_, rv) in enumerate(lst):
                            if re_ == e and rv is None:
                                lst[i] = (e, bnd)
                self.pend_f[e] = []
            else:
                for tk in writes:
                    self.resw[tk] = (e, None)
                    self.pend_f[e].append(("w", tk))
                for tk in reads:
                    lst = self.resr.setdefault(tk, [])
                    for i in range(len(lst) - 1, -1, -1):
                        if lst[i] == (e, self.cnt[e]):
                            lst[i] = (e, None)
                            break
                    self.pend_f[e].append(("r", tk))


def gr(reg, lo, hi):
    return [f"{reg}g{i}" for i in range(lo // 1024, (hi + 1023) // 1024)]


def build_consts():
    n = np.arange(L)
    k = np.arange(D)
    ph = 2 * np.pi * np.outer(n, k) / L
    dft_cos = np.cos(ph).astype(ml_dtypes.bfloat16)
    dft_sinn = (-np.sin(ph)).astype(ml_dtypes.bfloat16)
    w = np.full(D, 2.0, np.float64)
    w[0] = 1.0
    phT = 2 * np.pi * np.outer(np.arange(D), n) / L
    crt = ((w[:, None] / L) * np.cos(phT)).astype(ml_dtypes.bfloat16)
    cit = (-(w[:, None] / L) * np.sin(phT)).astype(ml_dtypes.bfloat16)
    crt_nyq = ((1.0 / L) * np.where(n % 2 == 0, 1.0, -1.0)).astype(ml_dtypes.bfloat16)[None, :]
    altcol = np.where(np.arange(128) % 2 == 0, 1.0, -1.0).astype(ml_dtypes.bfloat16)[:, None]
    eexp = np.zeros((H, D), ml_dtypes.bfloat16)
    for h in range(H):
        eexp[h, h * DH:(h + 1) * DH] = 1.0
    identb = np.eye(128, dtype=ml_dtypes.bfloat16)
    identf = np.eye(128, dtype=np.float32)
    return dict(dft_cos=dft_cos, dft_sinn=dft_sinn, crt=crt, cit=cit,
                crt_nyq=crt_nyq, altcol=altcol, eexp=eexp,
                identb=identb, identf=identf)


def build_program(debug=False):
    nc = bass.Bass("TRN2", target_bir_lowering=False, debug=False)
    S = Sched()
    em = S.emit

    x_in = nc.dram_tensor("x", [NB, L, D], F32, kind="ExternalInput").ap()
    W_ins = {nm: nc.dram_tensor(nm, [D, D], F32, kind="ExternalInput").ap()
             for nm in ("Wq", "Wk", "Wv", "Wo")}
    b_ins = {nm: nc.dram_tensor(nm, [1, D], F32, kind="ExternalInput").ap()
             for nm in ("bq", "bk", "bv", "bo")}
    wg_in = nc.dram_tensor("wg", [D, HID], I32, kind="ExternalInput").ap()
    wu_in = nc.dram_tensor("wu", [D, HID], I32, kind="ExternalInput").ap()
    wd_in = nc.dram_tensor("wd", [HID, D], I32, kind="ExternalInput").ap()
    c_cos = nc.dram_tensor("dft_cos", [L, D], BF16, kind="ExternalInput").ap()
    c_sinn = nc.dram_tensor("dft_sinn", [L, D], BF16, kind="ExternalInput").ap()
    c_crt = nc.dram_tensor("crt", [D, L], BF16, kind="ExternalInput").ap()
    c_cit = nc.dram_tensor("cit", [D, L], BF16, kind="ExternalInput").ap()
    c_crtnyq = nc.dram_tensor("crt_nyq", [1, L], BF16, kind="ExternalInput").ap()
    c_altcol = nc.dram_tensor("altcol", [128, 1], BF16, kind="ExternalInput").ap()
    c_eexp = nc.dram_tensor("eexp", [H, D], BF16, kind="ExternalInput").ap()
    c_identb = nc.dram_tensor("identb", [128, 128], BF16, kind="ExternalInput").ap()
    c_identf = nc.dram_tensor("identf", [128, 128], F32, kind="ExternalInput").ap()
    out_d = nc.dram_tensor("out", [NB, L, D], F32, kind="ExternalOutput").ap()

    w_scr = {nm: nc.dram_tensor(nm.lower() + "_s", [D, D], BF16, kind="Internal").ap()
             for nm in ("Wq", "Wk", "Wv", "Wo")}
    wgT_s = nc.dram_tensor("wgT_s", [D, HID], BF16, kind="Internal").ap()
    wuT_s = nc.dram_tensor("wuT_s", [D, HID], BF16, kind="Internal").ap()
    wdT_s = nc.dram_tensor("wdT_s", [HID, D], BF16, kind="Internal").ap()
    attn_scr = nc.dram_tensor("attn_scr", [L, D], BF16, kind="Internal").ap()

    dbg = {}
    if debug:
        for nm, shp, dt in [("d_xnT", [128, 8, L], BF16), ("d_qr", [128, 8, D], BF16),
                            ("d_qi", [128, 8, D], BF16), ("d_kr", [128, 8, D], BF16),
                            ("d_vr", [128, 8, D], BF16), ("d_eT", [H, NF], F32),
                            ("d_scT", [H, NF], F32), ("d_attnT", [H, NF], BF16),
                            ("d_um", [H, NF], BF16), ("d_xre", [128, 8, D], BF16),
                            ("d_xim", [128, 8, D], BF16), ("d_outT", [128, 8, L], BF16),
                            ("d_attn", [L, D], BF16), ("d_qn", [1, D], BF16),
                            ("d_qmr", [1, D], BF16), ("d_qmi", [1, D], BF16),
                            ("d_sig", [128, 16, D], BF16), ("d_h", [128, 8, L], BF16)]:
            dbg[nm] = nc.dram_tensor(nm, shp, dt, kind="ExternalOutput").ap()

    from contextlib import ExitStack
    stack = ExitStack()
    E = stack.enter_context

    r1 = E(nc.sbuf_tensor("r1", [128, 16384], BF16))
    r2 = E(nc.sbuf_tensor("r2", [128, 16384], BF16))
    r3 = E(nc.sbuf_tensor("r3", [128, 8192], BF16))
    r4 = E(nc.sbuf_tensor("r4", [128, 8192], BF16))
    r5 = E(nc.sbuf_tensor("r5", [128, 8192], BF16))
    r6 = E(nc.sbuf_tensor("r6", [128, 8192], BF16))
    strm = E(nc.sbuf_tensor("strm", [128, 8192], BF16))
    spA = E(nc.sbuf_tensor("spA", [128, 1040], F32))
    spB = E(nc.sbuf_tensor("spB", [128, 1040], F32))
    spC = E(nc.sbuf_tensor("spC", [128, 1040], F32))
    spD = E(nc.sbuf_tensor("spD", [128, 1040], F32))
    spE = E(nc.sbuf_tensor("spE", [128, 2048], BF16))
    e_fh = E(nc.sbuf_tensor("e_fh", [128, 8, 16], F32))
    s_fh = E(nc.sbuf_tensor("s_fh", [128, 8, 16], F32))
    max8 = E(nc.sbuf_tensor("max8", [16, 8], F32))
    enrow = E(nc.sbuf_tensor("enrow", [1, 16], F32))
    snrow = E(nc.sbuf_tensor("snrow", [1, 16], F32))
    mxx = E(nc.sbuf_tensor("mxx", [16, 1], F32))
    negmx = E(nc.sbuf_tensor("negmx", [16, 1], F32))
    smr = E(nc.sbuf_tensor("smr", [16, 1], F32))
    rsm = E(nc.sbuf_tensor("rsm", [16, 1], F32))
    msum = [E(nc.sbuf_tensor(f"msum{i}", [128, 1], F32)) for i in range(4)]
    ssq = [E(nc.sbuf_tensor(f"ssq{i}", [128, 1], F32)) for i in range(4)]
    meanv = [E(nc.sbuf_tensor(f"meanv{i}", [128, 1], F32)) for i in range(4)]
    m2v = [E(nc.sbuf_tensor(f"m2v{i}", [128, 1], F32)) for i in range(4)]
    varv = [E(nc.sbuf_tensor(f"varv{i}", [128, 1], F32)) for i in range(4)]
    sdv = [E(nc.sbuf_tensor(f"sdv{i}", [128, 1], F32)) for i in range(4)]
    rstd = [E(nc.sbuf_tensor(f"rstd{i}", [128, 1], F32)) for i in range(4)]
    epsv = E(nc.sbuf_tensor("epsv", [128, 1], F32))
    tmpA = E(nc.sbuf_tensor("tmpA", [128, 512], F32))
    tmpB = E(nc.sbuf_tensor("tmpB", [128, 512], F32))
    tmpC = E(nc.sbuf_tensor("tmpC", [1, 1024], F32))
    qmr = E(nc.sbuf_tensor("qmr", [1, 1024], BF16))
    qmi = E(nc.sbuf_tensor("qmi", [1, 1024], BF16))
    qtr = E(nc.sbuf_tensor("qtr", [16, 1024], BF16))
    qti = E(nc.sbuf_tensor("qti", [16, 1024], BF16))
    umF = E(nc.sbuf_tensor("umF", [128, 8, 16], BF16))
    qtiT = E(nc.sbuf_tensor("qtiT", [128, 8, 16], BF16))
    eexp_t = E(nc.sbuf_tensor("eexp_t", [16, 1024], BF16))
    identb_t = E(nc.sbuf_tensor("identb_t", [128, 128], BF16))
    identf_t = E(nc.sbuf_tensor("identf_t", [128, 128], F32))
    altcol_t = E(nc.sbuf_tensor("altcol_t", [128, 1], BF16))
    onesrow = E(nc.sbuf_tensor("onesrow", [1, 128], BF16))
    onescol = E(nc.sbuf_tensor("onescol", [128, 1], BF16))
    bias_t = {nm: E(nc.sbuf_tensor(nm + "_t", [1, 1024], BF16))
              for nm in ("bq", "bk", "bv", "bo")}
    nyq_qr = E(nc.sbuf_tensor("nyq_qr", [1, 1024], BF16))
    nyq_kr = E(nc.sbuf_tensor("nyq_kr", [1, 1024], BF16))
    nyq_vr = E(nc.sbuf_tensor("nyq_vr", [1, 1024], BF16))
    nyq_xre = E(nc.sbuf_tensor("nyq_xre", [1, 1024], BF16))
    crtn_t = E(nc.sbuf_tensor("crtn_t", [1, 1024], BF16))
    crtn_b = E(nc.sbuf_tensor("crtn_b", [1, 1024], BF16))
    pw = [E(nc.psum_tensor(f"pw{i}", [128, 1024], F32)) for i in range(4)]

    semT = E(nc.semaphore("semT"))
    semV = E(nc.semaphore("semV"))
    semS = E(nc.semaphore("semS"))
    semQ = E(nc.semaphore("semQ"))
    semG = E(nc.semaphore("semG"))
    semSD = E(nc.semaphore("semSD"))
    SEMS = {"t": semT, "v": semV, "s": semS, "q": semQ, "g": semG, "sd": semSD}
    block = E(nc.Block())

    def rv(t, a, b):
        return t[:, 0:a * b].rearrange("p (a b) -> p a b", a=a)

    xnT = rv(r1, 8, 2048)
    sig = rv(r2, 16, 1024)
    crci = rv(r2, 16, 1024)
    qr_v = rv(r3, 8, 1024)
    qi_v = rv(r4, 8, 1024)
    kr_v = rv(r5, 8, 1024)
    vr_v = rv(r6, 8, 1024)
    xre_v = qr_v
    xim_v = qi_v
    cmS = r4[0:16, 0:2048]
    wqim = r4[0:16, 2048:3072]
    outT_lo = rv(r5, 4, 2048)
    outT_hi = rv(r6, 4, 2048)

    def outT(chi):
        return outT_lo[:, chi, :] if chi < 4 else outT_hi[:, chi - 4, :]

    def outT_tok(chi, lo=0, hi=2048):
        reg = "r5" if chi < 4 else "r6"
        base = (chi % 4) * 2048
        return gr(reg, base + lo, base + hi)

    hTv = [rv(r2, 8, 2048), rv(r3, 4, 2048), rv(r4, 4, 2048),
           rv(r5, 4, 2048), rv(r6, 4, 2048)]

    def hT(hc):
        if hc < 8:
            return hTv[0][:, hc, :]
        return hTv[1 + (hc - 8) // 4][:, (hc - 8) % 4, :]

    def hT_tok(hc, lo=0, hi=2048):
        reg = "r2" if hc < 8 else ("r3", "r4", "r5", "r6")[(hc - 8) // 4]
        base = (hc * 2048 if hc < 8 else ((hc - 8) % 4) * 2048)
        return gr(reg, base + lo, base + hi)

    wdres_a = rv(r1, 16, 1024)
    wdres_b = rv(strm, 8, 1024)

    def wdres(hc):
        return wdres_a[:, hc, :] if hc < 16 else wdres_b[:, hc - 16, :]

    def wdres_tok(hc):
        return gr("r1", hc * 1024, (hc + 1) * 1024) if hc < 16 else \
            gr("st", (hc - 16) * 1024, (hc - 15) * 1024)

    wbuf = rv(strm, 8, 1024)
    cosslot = rv(strm, 32, 128)
    sinslot4 = rv(strm, 64, 128)
    wgslot = rv(strm, 8, 1024)
    siluslot = rv(strm, 8, 1024)

    eT = spA[0:16, 0:1025]
    scT = spB[0:16, 0:1025]
    attnT = spC[:].bitcast(BF16)[0:16, 0:1025]
    um = spD[:].bitcast(BF16)[0:16, 0:1025]
    xstage = [spA[:, 0:1024], spB[:, 0:1024]]
    ostage = [spC[:, 0:1024], spD[:, 0:1024]]
    istage = [spA[:].bitcast(I32)[:, 0:1024], spB[:].bitcast(I32)[:, 0:1024]]
    bstage = [spC[:].bitcast(BF16)[:, 0:2048], spD[:].bitcast(BF16)[:, 0:2048]]
    lnx = [r5[:].bitcast(F32)[:, i * 1024:(i + 1) * 1024] for i in range(4)]
    LNX_TK = [gr("r5", i * 2048, (i + 1) * 2048) for i in range(4)]
    lnxn = [r6[:, 0:1024], r6[:, 1024:2048], r6[:, 6144:7168], r6[:, 7168:8192]]
    LNXN_TK = [gr("r6", 0, 1024), gr("r6", 1024, 2048),
               gr("r6", 6144, 7168), gr("r6", 7168, 8192)]
    sqdump = [r6[:].bitcast(F32)[:, 1024:2048], r6[:].bitcast(F32)[:, 2048:3072]]
    SQD_TK = [gr("r6", 2048, 4096), gr("r6", 4096, 6144)]
    pwb = [p[:].bitcast(BF16) for p in pw]

    em("q", lambda e: e.dma_start(out=identb_t[:], in_=c_identb), writes=["identb"], dma=True)
    em("q", lambda e: e.dma_start(out=identf_t[:], in_=c_identf), writes=["identf"], dma=True)
    em("q", lambda e: e.dma_start(out=altcol_t[:], in_=c_altcol), writes=["altcol"], dma=True)
    em("q", lambda e: e.dma_start(out=eexp_t[:], in_=c_eexp), writes=["eexp"], dma=True)
    em("v", lambda e: e.memset(onesrow[:], 1.0), writes=["onesrow"])
    em("v", lambda e: e.memset(epsv[:], 1e-5), writes=["epsv"])
    em("v", lambda e: e.memset(onescol[:], 1.0), writes=["onescol"])
    for nm in ("bq", "bk", "bv", "bo"):
        em("q", lambda e, nm=nm: e.dma_start(out=lnx[0][0:1, :], in_=b_ins[nm]),
           writes=LNX_TK[0], dma=True)
        em("v", lambda e, nm=nm: e.tensor_copy(bias_t[nm][:], lnx[0][0:1, :]),
           reads=LNX_TK[0], writes=[nm])
    sp4 = [spA[:, 0:1024], spB[:, 0:1024], spC[:, 0:1024], spD[:, 0:1024]]
    sp4i = [t.bitcast(I32) for t in sp4]
    SP4 = [["spA"], ["spB"], ["spC"], ["spD"]]
    spEh = [spE[:, 0:1024], spE[:, 1024:2048]]
    prep_n = {"i": 0}
    prep_pipe = []

    def prep_one(src_ap, dst_ap, dtk, int32):
        i = prep_n["i"]
        prep_n["i"] += 1
        par, ph = i % 4, i % 2
        it = sp4i[par] if int32 else sp4[par]
        em("q", lambda e, it=it, src_ap=src_ap:
           e.dma_start(out=it, in_=src_ap), writes=SP4[par], dma=True)

        def cast_store(i=i, it=it, par=par, ph=ph, dst_ap=dst_ap, dtk=dtk):
            em("v" if i % 2 == 0 else "s", lambda e, it=it, ph=ph, i=i:
               (e.tensor_copy if i % 2 == 0 else e.copy)(spEh[ph], it),
               reads=SP4[par], writes=[f"spE{ph}"])
            em("q", lambda e, ph=ph, dst_ap=dst_ap: e.dma_start(out=dst_ap, in_=spEh[ph]),
               reads=[f"spE{ph}"], writes=[dtk], dma=True)

        prep_pipe.append(cast_store)
        if len(prep_pipe) > 3:
            prep_pipe.pop(0)()

    def prep_flush():
        while prep_pipe:
            prep_pipe.pop(0)()

    prep_pending = []

    def prep_service(k):
        for _ in range(k):
            if prep_pending:
                prep_one(*prep_pending.pop(0))

    def prep_attn(nm):
        for dc in range(8):
            prep_one(W_ins[nm][dc * 128:(dc + 1) * 128, :],
                     w_scr[nm][dc * 128:(dc + 1) * 128, :], f"scr{nm}", False)
        prep_flush()

    def prep_attn_pend(nm):
        for dc in range(8):
            prep_pending.append((W_ins[nm][dc * 128:(dc + 1) * 128, :],
                                 w_scr[nm][dc * 128:(dc + 1) * 128, :], f"scr{nm}", False))

    def prep_chunk(idx, src_ap, dst_ap, dtk):
        prep_one(src_ap, dst_ap, dtk, True)

    ffn_chunks = []
    for hc in range(24):
        ffn_chunks.append((wd_in[hc * 128:(hc + 1) * 128, :], wdT_s[hc * 128:(hc + 1) * 128, :], "scrwd"))
    prep_state = {"i": 0}

    def prep_some(n):
        i0 = prep_state["i"]
        for i in range(i0, min(i0 + n, len(ffn_chunks))):
            prep_chunk(i, *ffn_chunks[i])
        prep_state["i"] = min(i0 + n, len(ffn_chunks))
        prep_flush()

    def phase_ln(b):
        for tau in range(16):
            par = tau % 4
            pwi = 2 + (tau % 2)
            em("q", lambda e, b=b, tau=tau, par=par:
               e.dma_start(out=lnx[par], in_=x_in[b, tau * 128:(tau + 1) * 128, :]),
               writes=LNX_TK[par], dma=True, tag=f"lnx{tau}")
            em("v", lambda e, par=par: e.reduce_sum(msum[par][:], lnx[par], axis=AX.X),
               reads=LNX_TK[par], writes=[f"msum{par}"])
            em("s", lambda e, par=par: e.activation(sqdump[par % 2], lnx[par], AF.Square,
                                                    accum_out=ssq[par][:]),
               reads=LNX_TK[par], writes=[f"ssq{par}"] + SQD_TK[par % 2])
            em("v", lambda e, par=par: e.tensor_scalar(out=meanv[par][:], in0=msum[par][:],
                                                       scalar1=1.0 / D, scalar2=None, op0=ALU.mult),
               reads=[f"msum{par}"], writes=[f"meanv{par}"])
            em("v", lambda e, par=par: e.tensor_mul(m2v[par][:], meanv[par][:], meanv[par][:]),
               reads=[f"meanv{par}"], writes=[f"m2v{par}"])
            em("v", lambda e, par=par: e.scalar_tensor_tensor(out=varv[par][:], in0=ssq[par][:],
                                                              scalar=1.0 / D, in1=m2v[par][:],
                                                              op0=ALU.mult, op1=ALU.subtract),
               reads=[f"ssq{par}", f"m2v{par}"], writes=[f"varv{par}"])
            em("s", lambda e, par=par: e.activation(sdv[par][:], varv[par][:], AF.Sqrt, bias=epsv[:]),
               reads=[f"varv{par}", "epsv"], writes=[f"sdv{par}"])
            em("v", lambda e, par=par: e.reciprocal(rstd[par][:], sdv[par][:]),
               reads=[f"sdv{par}"], writes=[f"rstd{par}"])
            em("v", lambda e, par=par: e.tensor_scalar(out=lnxn[par], in0=lnx[par],
                                                       scalar1=meanv[par][:], scalar2=rstd[par][:],
                                                       op0=ALU.subtract, op1=ALU.mult),
               reads=LNX_TK[par] + [f"meanv{par}", f"rstd{par}"], writes=LNXN_TK[par])
            for dc in range(8):
                em("t", lambda e, par=par, dc=dc, pwi=pwi:
                   e.transpose(pwb[pwi][:, dc * 128:(dc + 1) * 128],
                               lnxn[par][:, dc * 128:(dc + 1) * 128], identb_t[:]),
                   reads=LNXN_TK[par] + ["identb"], writes=[f"pw{pwi}"], inc=(dc == 7))
            em("s", lambda e, tau=tau, pwi=pwi:
               e.copy(xnT[:, :, tau * 128:(tau + 1) * 128],
                      pwb[pwi][:, 0:1024].rearrange("p (dc c) -> p dc c", dc=8)),
               reads=[f"pw{pwi}"], writes=[f"r1c{tau}"])
        if debug and b == 0:
            em("g", lambda e: e.dma_start(out=dbg["d_xnT"], in_=xnT),
               reads=[f"r1c{t}" for t in range(16)], dma=True)

    def prefetch_wbuf(b, si):
        nm = ("Wq", "Wk", "Wv")[si]
        for dc in range(8):
            em("q", lambda e, nm=nm, dc=dc:
               e.dma_start(out=wbuf[:, dc, :], in_=w_scr[nm][dc * 128:(dc + 1) * 128, :]),
               reads=[f"scr{nm}"], writes=gr("st", dc * 1024, (dc + 1) * 1024), dma=True)

    def phase_proj(b, si):
        bias = bias_t[("bq", "bk", "bv")[si]]
        btk = ("bq", "bk", "bv")[si]
        reg = "st"
        wv = wbuf
        for tau in range(16):
            pwi = tau % 4
            stok = gr("r2", tau * 1024, (tau + 1) * 1024)
            for dc in range(8):
                for eta in range(2):
                    em("t", lambda e, tau=tau, dc=dc, eta=eta, pwi=pwi, wv=wv:
                       e.matmul(pw[pwi][:, eta * 512:(eta + 1) * 512],
                                xnT[:, dc, tau * 128:(tau + 1) * 128],
                                wv[:, dc, eta * 512:(eta + 1) * 512],
                                start=(dc == 0), stop=False),
                       reads=[f"r1c{tau}"] + gr(reg, dc * 1024, (dc + 1) * 1024),
                       writes=[f"pw{pwi}"], inc=False)
            for eta in range(2):
                em("t", lambda e, eta=eta, pwi=pwi, bias=bias:
                   e.matmul(pw[pwi][:, eta * 512:(eta + 1) * 512], onesrow[0:1, :],
                            bias[0:1, eta * 512:(eta + 1) * 512], start=False, stop=True),
                   reads=["onesrow", btk], writes=[f"pw{pwi}"], inc=(eta == 1))
            if tau % 2 == 0:
                em("s", lambda e, tau=tau, pwi=pwi: e.copy(sig[:, tau, :], pw[pwi][:, :]),
                   reads=[f"pw{pwi}"], writes=stok)
            else:
                em("v", lambda e, tau=tau, pwi=pwi: e.tensor_copy(sig[:, tau, :], pw[pwi][:, :]),
                   reads=[f"pw{pwi}"], writes=stok)
            if b == 0:
                prep_service(1)
        if debug and b == 0 and si == 0:
            em("g", lambda e: e.dma_start(out=dbg["d_sig"], in_=sig),
               reads=gr("r2", 0, 16384), dma=True)

    def phase_ln_proj(b):
        def lndma(tau):
            par = tau % 4
            em("q", lambda e, b=b, tau=tau, par=par:
               e.dma_start(out=lnx[par], in_=x_in[b, tau * 128:(tau + 1) * 128, :]),
               writes=LNX_TK[par], dma=True, tag=f"lnx{tau}")

        def lnchain(tau):
            par = tau % 4
            em("v", lambda e, par=par: e.reduce_sum(msum[par][:], lnx[par], axis=AX.X),
               reads=LNX_TK[par], writes=[f"msum{par}"])
            em("s", lambda e, par=par: e.activation(sqdump[par % 2], lnx[par], AF.Square,
                                                    accum_out=ssq[par][:]),
               reads=LNX_TK[par], writes=[f"ssq{par}"] + SQD_TK[par % 2])
            em("v", lambda e, par=par: e.tensor_scalar(out=meanv[par][:], in0=msum[par][:],
                                                       scalar1=1.0 / D, scalar2=None, op0=ALU.mult),
               reads=[f"msum{par}"], writes=[f"meanv{par}"])
            em("v", lambda e, par=par: e.tensor_mul(m2v[par][:], meanv[par][:], meanv[par][:]),
               reads=[f"meanv{par}"], writes=[f"m2v{par}"])
            em("v", lambda e, par=par: e.scalar_tensor_tensor(out=varv[par][:], in0=ssq[par][:],
                                                              scalar=1.0 / D, in1=m2v[par][:],
                                                              op0=ALU.mult, op1=ALU.subtract),
               reads=[f"ssq{par}", f"m2v{par}"], writes=[f"varv{par}"])
            em("s", lambda e, par=par: e.activation(sdv[par][:], varv[par][:], AF.Sqrt, bias=epsv[:]),
               reads=[f"varv{par}", "epsv"], writes=[f"sdv{par}"])
            em("v", lambda e, par=par: e.reciprocal(rstd[par][:], sdv[par][:]),
               reads=[f"sdv{par}"], writes=[f"rstd{par}"])
            em("v", lambda e, par=par: e.tensor_scalar(out=lnxn[par], in0=lnx[par],
                                                       scalar1=meanv[par][:], scalar2=rstd[par][:],
                                                       op0=ALU.subtract, op1=ALU.mult),
               reads=LNX_TK[par] + [f"meanv{par}", f"rstd{par}"], writes=LNXN_TK[par])

        def transp(tau):
            par = tau % 4
            pwi = 2 + (tau % 2)
            for dc in range(8):
                em("t", lambda e, par=par, dc=dc, pwi=pwi:
                   e.transpose(pwb[pwi][:, dc * 128:(dc + 1) * 128],
                               lnxn[par][:, dc * 128:(dc + 1) * 128], identb_t[:]),
                   reads=LNXN_TK[par] + ["identb"], writes=[f"pw{pwi}"], inc=(dc == 7))
            em("s", lambda e, tau=tau, pwi=pwi:
               e.copy(xnT[:, :, tau * 128:(tau + 1) * 128],
                      pwb[pwi][:, 0:1024].rearrange("p (dc c) -> p dc c", dc=8)),
               reads=[f"pw{pwi}"], writes=[f"r1c{tau}"])

        def projmms(tau):
            ppw = tau % 2
            stok = gr("r2", tau * 1024, (tau + 1) * 1024)
            for dc in range(8):
                for eta in range(2):
                    em("t", lambda e, tau=tau, dc=dc, eta=eta, ppw=ppw:
                       e.matmul(pw[ppw][:, eta * 512:(eta + 1) * 512],
                                xnT[:, dc, tau * 128:(tau + 1) * 128],
                                wbuf[:, dc, eta * 512:(eta + 1) * 512],
                                start=(dc == 0), stop=False),
                       reads=[f"r1c{tau}"] + gr("st", dc * 1024, (dc + 1) * 1024),
                       writes=[f"pw{ppw}"], inc=False)
            for eta in range(2):
                em("t", lambda e, eta=eta, ppw=ppw:
                   e.matmul(pw[ppw][:, eta * 512:(eta + 1) * 512], onesrow[0:1, :],
                            bias_t["bq"][0:1, eta * 512:(eta + 1) * 512], start=False, stop=True),
                   reads=["onesrow", "bq"], writes=[f"pw{ppw}"], inc=(eta == 1))
            em("v", lambda e, tau=tau, ppw=ppw: e.tensor_copy(sig[:, tau, :], pw[ppw][:, :]),
               reads=[f"pw{ppw}"], writes=stok)

        for tau in range(4):
            lndma(tau)
        prefetch_wbuf(b, 0)
        em("q", lambda e: e.dma_start(
            out=spE[:].rearrange("p (tau c) -> p tau c", tau=16),
            in_=c_cos[:, 0:128].rearrange("(tau p) c -> p tau c", p=128)),
           writes=["spE0", "spE1"], dma=True)
        em("q", lambda e: e.dma_start(
            out=tmpA[:].bitcast(BF16).rearrange("p (tau c) -> p tau c", tau=8),
            in_=c_sinn[0:1024, 0:128].rearrange("(tau p) c -> p tau c", p=128)),
           writes=["tmpA"], dma=True)
        em("q", lambda e: e.dma_start(
            out=tmpB[:].bitcast(BF16).rearrange("p (tau c) -> p tau c", tau=8),
            in_=c_sinn[1024:2048, 0:128].rearrange("(tau p) c -> p tau c", p=128)),
           writes=["tmpB"], dma=True)
        em("q", lambda e: e.dma_start(
            out=spA[:].bitcast(BF16)[:, 0:2048].rearrange("p (tau c) -> p tau c", tau=16),
            in_=c_cos[:, 128:256].rearrange("(tau p) c -> p tau c", p=128)),
           writes=["spA"], dma=True)
        em("q", lambda e: e.dma_start(
            out=spB[:].bitcast(BF16)[:, 0:2048].rearrange("p (tau c) -> p tau c", tau=16),
            in_=c_sinn[:, 128:256].rearrange("(tau p) c -> p tau c", p=128)),
           writes=["spB"], dma=True)
        for tau in range(16):
            lnchain(tau)
            if tau + 4 < 16:
                lndma(tau + 4)
            transp(tau)
            if tau >= 1:
                projmms(tau - 1)
        projmms(15)

    def phase_fft(b, si):
        tlist = {0: [("qr", c_cos, 0, qr_v, "r3", 0), ("qi", c_sinn, 1, qi_v, "r4", 2)],
                 1: [("kr", c_cos, 0, kr_v, "r5", 0)],
                 2: [("vr", c_cos, 0, vr_v, "r6", 0)]}[si]
        hoist = (si == 0 and b > 0)
        spEc = spE[:].rearrange("p (tau c) -> p tau c", tau=16)
        tmpAs = tmpA[:].bitcast(BF16).rearrange("p (tau c) -> p tau c", tau=8)
        tmpBs = tmpB[:].bitcast(BF16).rearrange("p (tau c) -> p tau c", tau=8)
        spAc = spA[:].bitcast(BF16)[:, 0:2048].rearrange("p (tau c) -> p tau c", tau=16)
        spBc = spB[:].bitcast(BF16)[:, 0:2048].rearrange("p (tau c) -> p tau c", tau=16)
        for phi in range(8):
            par = phi % 2
            for (nm, mat, kind, dstv, reg, pwbase) in tlist:
                use_h = hoist and phi <= 1
                slotv = cosslot[:, par * 16:(par + 1) * 16, :] if kind == 0 else \
                    sinslot4[:, 32 + par * 16:32 + (par + 1) * 16, :]
                slot_tk = gr("st", par * 2048, (par + 1) * 2048) if kind == 0 else \
                    gr("st", 4096 + par * 2048, 4096 + (par + 1) * 2048)
                if use_h:
                    def slot_ap(tau, kind=kind, phi=phi):
                        if phi == 1:
                            if kind == 0:
                                return spAc[:, tau, :], ["spA"]
                            return spBc[:, tau, :], ["spB"]
                        if kind == 0:
                            return spEc[:, tau, :], ["spE0", "spE1"]
                        if tau < 8:
                            return tmpAs[:, tau, :], ["tmpA"]
                        return tmpBs[:, tau - 8, :], ["tmpB"]
                else:
                    def slot_ap(tau, slotv=slotv, slot_tk=slot_tk):
                        return slotv[:, tau, :], slot_tk
                    em("q", lambda e, mat=mat, phi=phi, slotv=slotv:
                       e.dma_start(out=slotv, in_=mat[:, phi * 128:(phi + 1) * 128]
                                   .rearrange("(tau p) c -> p tau c", p=128)),
                       writes=slot_tk, dma=True, tag=f"fftdma{nm}{phi}")
                pwi = pwbase + par
                for tau in range(16):
                    sap, stk = slot_ap(tau)
                    for eta in range(2):
                        em("t", lambda e, sap=sap, tau=tau, eta=eta, pwi=pwi:
                           e.matmul(pw[pwi][:, eta * 512:(eta + 1) * 512],
                                    sap,
                                    sig[:, tau, eta * 512:(eta + 1) * 512],
                                    start=(tau == 0), stop=(tau == 15)),
                           reads=stk + gr("r2", tau * 1024, (tau + 1) * 1024),
                           writes=[f"pw{pwi}"], inc=(tau == 15 and eta == 1))
                eng = "s" if kind == 0 else "v"
                em(eng, lambda e, dstv=dstv, phi=phi, pwi=pwi, eng=eng:
                   (e.copy if eng == "s" else e.tensor_copy)(dstv[:, phi, :], pw[pwi][:, :]),
                   reads=[f"pw{pwi}"], writes=gr(reg, phi * 1024, (phi + 1) * 1024))
            if si == 0:
                for eta in range(2):
                    sl = slice(eta * 512, (eta + 1) * 512)
                    qtk = gr("r3", phi * 1024 + eta * 512, phi * 1024 + (eta + 1) * 512)
                    itk = gr("r4", phi * 1024 + eta * 512, phi * 1024 + (eta + 1) * 512)
                    em("v", lambda e, phi=phi, sl=sl: e.tensor_mul(tmpA[:], qr_v[:, phi, sl], qr_v[:, phi, sl]),
                       reads=qtk, writes=["tmpA"])
                    em("v", lambda e, phi=phi, sl=sl: e.tensor_mul(tmpB[:], qi_v[:, phi, sl], qi_v[:, phi, sl]),
                       reads=itk, writes=["tmpB"])
                    em("v", lambda e: e.tensor_add(tmpA[:], tmpA[:], tmpB[:]),
                       reads=["tmpA", "tmpB"], writes=["tmpA"])
                    em("s", lambda e: e.activation(tmpA[:], tmpA[:], AF.Sqrt),
                       reads=["tmpA"], writes=["tmpA"])
                    em("v", lambda e, phi=phi, eta=eta:
                       e.tensor_reduce(out=e_fh[:, phi, eta * 8:(eta + 1) * 8].unsqueeze(2),
                                       in_=tmpA[:].rearrange("p (g d) -> p g d", g=8),
                                       axis=AX.X, op=ALU.add),
                       reads=["tmpA"], writes=["e_fh"])
            if si == 1:
                for eta in range(2):
                    sl = slice(eta * 512, (eta + 1) * 512)
                    qtk = gr("r3", phi * 1024 + eta * 512, phi * 1024 + (eta + 1) * 512)
                    ktk = gr("r5", phi * 1024 + eta * 512, phi * 1024 + (eta + 1) * 512)
                    em("v", lambda e, phi=phi, sl=sl: e.tensor_mul(tmpB[:], qr_v[:, phi, sl], kr_v[:, phi, sl]),
                       reads=qtk + ktk, writes=["tmpB"])
                    em("v", lambda e, phi=phi, eta=eta:
                       e.tensor_reduce(out=s_fh[:, phi, eta * 8:(eta + 1) * 8].unsqueeze(2),
                                       in_=tmpB[:].rearrange("p (g d) -> p g d", g=8),
                                       axis=AX.X, op=ALU.add),
                       reads=["tmpB"], writes=["s_fh"])
            if b == 0 and si == 1:
                prep_service(1)
        nyq_dst, nyq_tk = [(nyq_qr, "nyq_qr"), (nyq_kr, "nyq_kr"), (nyq_vr, "nyq_vr")][si]
        for tau in range(16):
            for eta in range(2):
                em("t", lambda e, tau=tau, eta=eta:
                   e.matmul(pw[3][0:1, eta * 512:(eta + 1) * 512], altcol_t[:],
                            sig[:, tau, eta * 512:(eta + 1) * 512],
                            start=(tau == 0), stop=(tau == 15)),
                   reads=["altcol"] + gr("r2", tau * 1024, (tau + 1) * 1024),
                   writes=["pw3"], inc=(tau == 15 and eta == 1))
        em("s", lambda e, nyq_dst=nyq_dst: e.copy(nyq_dst[:], pw[3][0:1, :]),
           reads=["pw3"], writes=[nyq_tk])
        if si == 0:
            em("v", lambda e: e.tensor_mul(tmpC[:], nyq_qr[:], nyq_qr[:]),
               reads=["nyq_qr"], writes=["tmpC"])
            em("s", lambda e: e.activation(tmpC[:], tmpC[:], AF.Sqrt),
               reads=["tmpC"], writes=["tmpC"])
            em("v", lambda e: e.tensor_reduce(out=enrow[:].unsqueeze(2),
                                              in_=tmpC[:].rearrange("p (g d) -> p g d", g=16),
                                              axis=AX.X, op=ALU.add),
               reads=["tmpC"], writes=["enrow"])
        if si == 1:
            em("v", lambda e: e.tensor_mul(tmpC[:], nyq_qr[:], nyq_kr[:]),
               reads=["nyq_qr", "nyq_kr"], writes=["tmpC"])
            em("v", lambda e: e.tensor_reduce(out=snrow[:].unsqueeze(2),
                                              in_=tmpC[:].rearrange("p (g d) -> p g d", g=16),
                                              axis=AX.X, op=ALU.add),
               reads=["tmpC"], writes=["snrow"])
        if debug and b == 0:
            dv, dr = {0: (qr_v, "r3"), 1: (kr_v, "r5"), 2: (vr_v, "r6")}[si]
            dn = {0: "d_qr", 1: "d_kr", 2: "d_vr"}[si]
            em("g", lambda e, dv=dv, dn=dn: e.dma_start(out=dbg[dn], in_=dv),
               reads=gr(dr, 0, 8192), dma=True)
            if si == 0:
                em("g", lambda e: e.dma_start(out=dbg["d_qi"], in_=qi_v), reads=gr("r4", 0, 8192), dma=True)
                em("g", lambda e: e.dma_start(out=dbg["d_qn"], in_=nyq_qr[:]), reads=["nyq_qr"], dma=True)

    def phase_middle_pre(b):
        for (src, srctk, dst, dsttk, nrow, nrtk, pwi) in [
                (e_fh, "e_fh", eT, "spA", enrow, "enrow", 2),
                (s_fh, "s_fh", scT, "spB", snrow, "snrow", 3)]:
            for phi in range(8):
                em("t", lambda e, src=src, phi=phi, pwi=pwi:
                   e.transpose(pw[pwi][0:16, phi * 128:(phi + 1) * 128],
                               src[:, phi, :], identf_t[:]),
                   reads=[srctk, "identf"], writes=[f"pw{pwi}"], inc=(phi == 7))
            em("s", lambda e, dst=dst, pwi=pwi: e.copy(dst[:, 0:1024], pw[pwi][0:16, :]),
               reads=[f"pw{pwi}"], writes=[dsttk])
            em("t", lambda e, nrow=nrow, pwi=pwi:
               e.transpose(pw[pwi][0:16, 0:1], nrow[:], identf_t[0:1, 0:1]),
               reads=[nrtk, "identf", dsttk], writes=[f"pw{pwi}"])
            em("s", lambda e, dst=dst, pwi=pwi: e.copy(dst[:, 1024:1025], pw[pwi][0:16, 0:1]),
               reads=[f"pw{pwi}"], writes=[dsttk])

    def phase_middle_a(b):
        for it in range(8):
            em("v", lambda e: e.max(out=max8[:], in_=eT), reads=["spA"], writes=["max8"])
            em("v", lambda e: e.match_replace(out=eT, in_to_replace=max8[:], in_values=eT,
                                              imm_value=-1.0),
               reads=["max8", "spA"], writes=["spA"])
        if debug and b == 0:
            em("g", lambda e: e.dma_start(out=dbg["d_eT"], in_=eT), reads=["spA"], dma=True)
            em("g", lambda e: e.dma_start(out=dbg["d_scT"], in_=scT), reads=["spB"], dma=True)
        em("v", lambda e: e.tensor_scalar(out=um, in0=eT, scalar1=0.0, scalar2=None,
                                          op0=ALU.is_ge), reads=["spA"], writes=["spD"])
        em("v", lambda e: e.tensor_scalar(out=scT, in0=scT, scalar1=C1, scalar2=None,
                                          op0=ALU.mult), reads=["spB"], writes=["spB"])
        em("v", lambda e: e.scalar_tensor_tensor(out=scT, in0=um, scalar=-1e9, in1=scT,
                                                 op0=ALU.mult, op1=ALU.add),
           reads=["spD", "spB"], writes=["spB"])
        em("v", lambda e: e.reduce_max(mxx[:], scT, axis=AX.X), reads=["spB"], writes=["mxx"])
        em("v", lambda e: e.tensor_scalar(out=negmx[:], in0=mxx[:], scalar1=-1.0, scalar2=None,
                                          op0=ALU.mult), reads=["mxx"], writes=["negmx"])
        em("s", lambda e: e.activation(scT, scT, AF.Exp, bias=negmx[:]),
           reads=["spB", "negmx"], writes=["spB"])
        em("v", lambda e: e.reduce_sum(smr[:], scT, axis=AX.X), reads=["spB"], writes=["smr"])
        em("v", lambda e: e.reciprocal(rsm[:], smr[:]), reads=["smr"], writes=["rsm"])
        em("v", lambda e: e.tensor_scalar(out=attnT, in0=scT, scalar1=rsm[:], scalar2=None,
                                          op0=ALU.mult), reads=["spB", "rsm"], writes=["spC"])

    def phase_middle_b(b):
        for (vsp, reg, nrow, nrtk, dst, dtk, pwi) in [
                (qr_v, "r3", nyq_qr, "nyq_qr", qmr, "qmr", 0),
                (qi_v, "r4", None, None, qmi, "qmi", 1)]:
            for phi in range(8):
                last = (phi == 7 and nrow is None)
                for eta in range(2):
                    em("t", lambda e, vsp=vsp, phi=phi, eta=eta, pwi=pwi, last=last:
                       e.matmul(pw[pwi][0:1, eta * 512:(eta + 1) * 512], onescol[:],
                                vsp[:, phi, eta * 512:(eta + 1) * 512],
                                start=(phi == 0), stop=last),
                       reads=gr(reg, phi * 1024, (phi + 1) * 1024) + ["onescol"],
                       writes=[f"pw{pwi}"], inc=(last and eta == 1))
            if nrow is not None:
                for eta in range(2):
                    em("t", lambda e, nrow=nrow, eta=eta, pwi=pwi:
                       e.matmul(pw[pwi][0:1, eta * 512:(eta + 1) * 512], onesrow[0:1, 0:1],
                                nrow[0:1, eta * 512:(eta + 1) * 512], start=False, stop=True),
                       reads=[nrtk, "onesrow"], writes=[f"pw{pwi}"], inc=(eta == 1))
            em("s", lambda e, dst=dst, pwi=pwi: e.mul(dst[:], pw[pwi][0:1, :], 1.0 / NF),
               reads=[f"pw{pwi}"], writes=[dtk])
        for (qrow, qrtk, qt, qttk, pwi) in [(qmr, "qmr", qtr, "qtr", 0), (qmi, "qmi", qti, "qti", 1)]:
            for eta in range(2):
                em("t", lambda e, qrow=qrow, eta=eta, pwi=pwi:
                   e.matmul(pw[pwi][0:16, eta * 512:(eta + 1) * 512], onesrow[0:1, 0:16],
                            qrow[0:1, eta * 512:(eta + 1) * 512], start=True, stop=True),
                   reads=[qrtk, "onesrow"], writes=[f"pw{pwi}"], inc=(eta == 1))
            em("v", lambda e, qt=qt, pwi=pwi: e.tensor_mul(qt[:], pw[pwi][0:16, 0:1024], eexp_t[:]),
               reads=[f"pw{pwi}", "eexp"], writes=[qttk])
        if debug and b == 0:
            em("g", lambda e: e.dma_start(out=dbg["d_attnT"], in_=attnT), reads=["spC"], dma=True)
            em("g", lambda e: e.dma_start(out=dbg["d_um"], in_=um), reads=["spD"], dma=True)
            em("g", lambda e: e.dma_start(out=dbg["d_qmr"], in_=qmr[:]), reads=["qmr"], dma=True)
            em("g", lambda e: e.dma_start(out=dbg["d_qmi"], in_=qmi[:]), reads=["qmi"], dma=True)
        for phi in range(8):
            for eta in range(2):
                em("t", lambda e, phi=phi, eta=eta:
                   e.matmul(pw[0][:, eta * 512:(eta + 1) * 512],
                            attnT[:, phi * 128:(phi + 1) * 128],
                            eexp_t[:, eta * 512:(eta + 1) * 512], start=True, stop=True),
                   reads=["spC", "eexp"], writes=["pw0"], inc=(eta == 1))
                em("t", lambda e, phi=phi, eta=eta:
                   e.matmul(pw[1][:, eta * 512:(eta + 1) * 512],
                            um[:, phi * 128:(phi + 1) * 128],
                            qtr[:, eta * 512:(eta + 1) * 512], start=True, stop=True),
                   reads=["spD", "qtr"], writes=["pw1"], inc=(eta == 1))
            em("t", lambda e, phi=phi:
               e.transpose(pwb[2][:, phi * 16:(phi + 1) * 16],
                           um[:, phi * 128:(phi + 1) * 128], identb_t[0:16, 0:16]),
               reads=["spD", "identb"], writes=["pw2"], inc=(phi == 7))
            for eta in range(2):
                sl = slice(eta * 512, (eta + 1) * 512)
                vtk = gr("r6", phi * 1024 + eta * 512, phi * 1024 + (eta + 1) * 512)
                xtk = gr("r3", phi * 1024 + eta * 512, phi * 1024 + (eta + 1) * 512)
                em("v", lambda e, phi=phi, sl=sl: e.tensor_mul(tmpA[:], pw[0][:, sl], vr_v[:, phi, sl]),
                   reads=["pw0"] + vtk, writes=["tmpA"])
                em("v", lambda e, phi=phi, sl=sl: e.tensor_add(xre_v[:, phi, sl], tmpA[:], pw[1][:, sl]),
                   reads=["tmpA", "pw1"], writes=xtk)
        em("s", lambda e: e.copy(umF[:], pwb[2][:, 0:128].rearrange("p (a c) -> p a c", a=8)),
           reads=["pw2"], writes=["umF"])
        for c in range(8):
            em("t", lambda e, c=c:
               e.transpose(pwb[3][:, c * 16:(c + 1) * 16],
                           qti[:, c * 128:(c + 1) * 128], identb_t[0:16, 0:16]),
               reads=["qti", "identb"], writes=["pw3"], inc=(c == 7))
        em("s", lambda e: e.copy(qtiT[:], pwb[3][:, 0:128].rearrange("p (a c) -> p a c", a=8)),
           reads=["pw3"], writes=["qtiT"])
        for eta in range(2):
            for c in range(8):
                em("t", lambda e, c=c, eta=eta:
                   e.matmul(pw[2][0:16, eta * 512:(eta + 1) * 512], qtiT[:, c, :],
                            wbuf[:, c, eta * 512:(eta + 1) * 512],
                            start=(c == 0), stop=(c == 7)),
                   reads=["qtiT"] + gr("st", c * 1024, (c + 1) * 1024),
                   writes=["pw2"], inc=(c == 7 and eta == 1))
        em("s", lambda e: e.copy(wqim[:], pw[2][0:16, :]), reads=["pw2"], writes=gr("r4", 2048, 3072))
        for eta in range(2):
            em("t", lambda e, eta=eta:
               e.matmul(pw[0][0:1, eta * 512:(eta + 1) * 512], attnT[:, 1024:1025],
                        eexp_t[:, eta * 512:(eta + 1) * 512], start=True, stop=True),
               reads=["spC", "eexp"], writes=["pw0"], inc=(eta == 1))
            em("t", lambda e, eta=eta:
               e.matmul(pw[1][0:1, eta * 512:(eta + 1) * 512], um[:, 1024:1025],
                        qtr[:, eta * 512:(eta + 1) * 512], start=True, stop=True),
               reads=["spD", "qtr"], writes=["pw1"], inc=(eta == 1))
        em("v", lambda e: e.tensor_mul(tmpC[:], pw[0][0:1, :], nyq_vr[:]),
           reads=["pw0", "nyq_vr"], writes=["tmpC"])
        em("v", lambda e: e.tensor_add(nyq_xre[:], tmpC[:], pw[1][0:1, :]),
           reads=["tmpC", "pw1"], writes=["nyq_xre"])
        if debug and b == 0:
            em("g", lambda e: e.dma_start(out=dbg["d_xre"], in_=xre_v), reads=gr("r3", 0, 8192), dma=True)

    def phase_irfft(b):
        altv = []
        for phi in range(8):
            if phi < 5:
                altv.append((r4[:, 3072 + phi * 1024:4096 + phi * 1024],
                             gr("r4", 3072 + phi * 1024, 4096 + phi * 1024)))
            elif phi < 7:
                altv.append((spEh[phi - 5], [f"spE{phi - 5}"]))
            else:
                altv.append((tmpA[:].bitcast(BF16), ["tmpA"]))
        em("q", lambda e: e.dma_start(
            out=crci[:, 8:16, :],
            in_=c_cit[:, 0:1024].rearrange("(phi p) t -> p phi t", p=128)),
           writes=gr("r2", 8192, 16384), dma=True)
        em("sd", lambda e: e.dma_start(
            out=crci[:, 0:8, :],
            in_=c_crt[:, 0:1024].rearrange("(phi p) t -> p phi t", p=128)),
           writes=gr("r2", 0, 8192), dma=True, fence=False)
        S.emit_fence("sd")
        em("q", lambda e: e.dma_start(out=crtn_t[:], in_=c_crtnyq[:, 0:1024]),
           writes=["crtn"], dma=True)
        em("q", lambda e: e.dma_start(out=crtn_b[:], in_=c_crtnyq[:, 1024:2048]),
           writes=["crtnb"], dma=True)
        for phi in range(8):
            em("sd", lambda e, phi=phi: e.dma_start(
                out=altv[phi][0], in_=c_crt[phi * 128:(phi + 1) * 128, 1024:2048]),
               writes=altv[phi][1], dma=True, fence=False)
        for th in range(2):
            if th == 1:
                S.emit_fence("sd")
            for half in range(2):
                hsl = slice(half * 512, (half + 1) * 512)
                for phi in range(8):
                    em("t", lambda e, phi=phi, hsl=hsl, half=half:
                       e.matmul(pw[3][0:16, hsl], umF[:, phi, :], crci[:, 8 + phi, hsl],
                                start=(phi == 0), stop=(phi == 7)),
                       reads=["umF"] + gr("r2", 8192 + phi * 1024, 8192 + (phi + 1) * 1024),
                       writes=["pw3"], inc=(phi == 7 and half == 1))
            em("s", lambda e, th=th: e.copy(cmS[:, th * 1024:(th + 1) * 1024], pw[3][0:16, :]),
               reads=["pw3"], writes=gr("r4", 0, 2048))
            if th == 0:
                em("q", lambda e: e.dma_start(
                    out=crci[:, 8:16, :],
                    in_=c_cit[:, 1024:2048].rearrange("(phi p) t -> p phi t", p=128)),
                   writes=gr("r2", 8192, 16384), dma=True)
            nyqv, nyqtk = (crtn_t, "crtn") if th == 0 else (crtn_b, "crtnb")
            for chi in range(8):
                pwi = chi % 2
                for half in range(2):
                    hsl = slice(half * 512, (half + 1) * 512)
                    for phi in range(8):
                        if th == 0:
                            mv, mtk = crci[:, phi, hsl], gr("r2", phi * 1024, (phi + 1) * 1024)
                        else:
                            mv, mtk = altv[phi][0][:, hsl], altv[phi][1]
                        em("t", lambda e, chi=chi, phi=phi, hsl=hsl, pwi=pwi, mv=mv:
                           e.matmul(pw[pwi][:, hsl], xre_v[:, phi, chi * 128:(chi + 1) * 128],
                                    mv, start=(phi == 0), stop=False),
                           reads=gr("r3", phi * 1024 + chi * 128, phi * 1024 + (chi + 1) * 128) +
                           mtk, writes=[f"pw{pwi}"], inc=False)
                    em("t", lambda e, chi=chi, hsl=hsl, pwi=pwi, half=half, nyqv=nyqv:
                       e.matmul(pw[pwi][:, hsl], nyq_xre[0:1, chi * 128:(chi + 1) * 128],
                                nyqv[0:1, hsl], start=False, stop=True),
                       reads=["nyq_xre", nyqtk], writes=[f"pw{pwi}"], inc=(half == 1))
                eng = "s" if chi % 2 == 0 else "v"
                em(eng, lambda e, chi=chi, th=th, pwi=pwi, eng=eng:
                   (e.copy if eng == "s" else e.tensor_copy)(
                       outT(chi)[:, th * 1024:(th + 1) * 1024], pw[pwi][:, :]),
                   reads=[f"pw{pwi}"], writes=outT_tok(chi, th * 1024, (th + 1) * 1024))
        if debug and b == 0:
            em("g", lambda e: e.dma_start(out=dbg["d_outT"][:, 0:4, :], in_=outT_lo),
               reads=gr("r5", 0, 8192), dma=True)
            em("g", lambda e: e.dma_start(out=dbg["d_outT"][:, 4:8, :], in_=outT_hi),
               reads=gr("r6", 0, 8192), dma=True)

    def prefetch_wo(b):
        for dc in range(8):
            em("q", lambda e, dc=dc:
               e.dma_start(out=wbuf[:, dc, :], in_=w_scr["Wo"][dc * 128:(dc + 1) * 128, :]),
               reads=["scrWo"], writes=gr("st", dc * 1024, (dc + 1) * 1024), dma=True)

    def phase_wo(b):
        for tau in range(16):
            pwi = tau % 4
            par = tau % 2
            for dc in range(8):
                for eta in range(2):
                    em("t", lambda e, tau=tau, dc=dc, eta=eta, pwi=pwi:
                       e.matmul(pw[pwi][:, eta * 512:(eta + 1) * 512],
                                outT(dc)[:, tau * 128:(tau + 1) * 128],
                                wbuf[:, dc, eta * 512:(eta + 1) * 512],
                                start=(dc == 0), stop=False),
                       reads=outT_tok(dc, tau * 128, (tau + 1) * 128) +
                       gr("st", dc * 1024, (dc + 1) * 1024),
                       writes=[f"pw{pwi}"], inc=False)
            for eta in range(2):
                em("t", lambda e, tau=tau, eta=eta, pwi=pwi:
                   e.matmul(pw[pwi][:, eta * 512:(eta + 1) * 512],
                            cmS[:, tau * 128:(tau + 1) * 128],
                            wqim[:, eta * 512:(eta + 1) * 512], start=False, stop=False),
                   reads=gr("r4", 0, 3072), writes=[f"pw{pwi}"], inc=False)
                em("t", lambda e, eta=eta, pwi=pwi:
                   e.matmul(pw[pwi][:, eta * 512:(eta + 1) * 512], onesrow[0:1, :],
                            bias_t["bo"][0:1, eta * 512:(eta + 1) * 512], start=False, stop=True),
                   reads=["onesrow", "bo"], writes=[f"pw{pwi}"], inc=(eta == 1))
            em("s", lambda e, pwi=pwi, par=par:
               e.copy(spE[:, par * 1024:(par + 1) * 1024], pw[pwi][:, :]),
               reads=[f"pw{pwi}"], writes=[f"spE{par}"])
            em("q", lambda e, tau=tau, par=par:
               e.dma_start(out=attn_scr[tau * 128:(tau + 1) * 128, :],
                           in_=spE[:, par * 1024:(par + 1) * 1024]),
               reads=[f"spE{par}"], writes=["attn_scr"], dma=True)
        if debug and b == 0:
            em("g", lambda e: e.dma_start(out=dbg["d_attn"], in_=attn_scr),
               reads=["attn_scr"], dma=True)

    def phase_ffn_gu(b, hclist):
        stg = [spA[:].bitcast(I32)[:, 0:1024], spB[:].bitcast(I32)[:, 0:1024],
               spC[:].bitcast(I32)[:, 0:1024], spD[:].bitcast(I32)[:, 0:1024]]
        stk = [["spA"], ["spB"], ["spC"], ["spD"]]
        for hc in hclist:
            par = hc % 2
            pg, pu = (0, 1) if par == 0 else (2, 3)
            em("q", lambda e, hc=hc, par=par:
               e.dma_start(out=stg[par].rearrange("p (dc c) -> p dc c", dc=8),
                           in_=wg_in[:, hc * 128:(hc + 1) * 128]
                           .rearrange("(dc p) c -> p dc c", p=128)),
               writes=stk[par], dma=True)
            em("s", lambda e, hc=hc, par=par:
               e.copy(wgslot[:, par, :], stg[par]),
               reads=stk[par], writes=gr("st", par * 1024, (par + 1) * 1024))
            em("q", lambda e, hc=hc, par=par:
               e.dma_start(out=stg[2 + par].rearrange("p (dc c) -> p dc c", dc=8),
                           in_=wu_in[:, hc * 128:(hc + 1) * 128]
                           .rearrange("(dc p) c -> p dc c", p=128)),
               writes=stk[2 + par], dma=True)
            em("v", lambda e, hc=hc, par=par:
               e.tensor_copy(wgslot[:, 2 + par, :], stg[2 + par]),
               reads=stk[2 + par], writes=gr("st", (2 + par) * 1024, (3 + par) * 1024))
            for th in range(2):
                wgv = wgslot[:, par, :].rearrange("p (dc c) -> p dc c", dc=8)
                wuv = wgslot[:, 2 + par, :].rearrange("p (dc c) -> p dc c", dc=8)
                for dc in range(8):
                    for tq in range(2):
                        em("t", lambda e, wgv=wgv, dc=dc, th=th, tq=tq, pg=pg:
                           e.matmul(pw[pg][:, tq * 512:(tq + 1) * 512], wgv[:, dc, :],
                                    xnT[:, dc, th * 1024 + tq * 512:th * 1024 + (tq + 1) * 512],
                                    start=(dc == 0), stop=(dc == 7)),
                           reads=gr("st", par * 1024, (par + 1) * 1024) +
                           [f"r1c{th * 8 + tq * 4 + i}" for i in range(4)],
                           writes=[f"pw{pg}"], inc=(dc == 7 and tq == 1))
                for dc in range(8):
                    for tq in range(2):
                        em("t", lambda e, wuv=wuv, dc=dc, th=th, tq=tq, pu=pu:
                           e.matmul(pw[pu][:, tq * 512:(tq + 1) * 512], wuv[:, dc, :],
                                    xnT[:, dc, th * 1024 + tq * 512:th * 1024 + (tq + 1) * 512],
                                    start=(dc == 0), stop=(dc == 7)),
                           reads=gr("st", (2 + par) * 1024, (3 + par) * 1024) +
                           [f"r1c{th * 8 + tq * 4 + i}" for i in range(4)],
                           writes=[f"pw{pu}"], inc=(dc == 7 and tq == 1))
                em("s", lambda e, par=par, pg=pg:
                   e.activation(siluslot[:, 4 + par, :], pw[pg][:, :], AF.Silu, scale=SC),
                   reads=[f"pw{pg}"], writes=gr("st", (4 + par) * 1024, (5 + par) * 1024))
                em("v", lambda e, hc=hc, th=th, par=par, pu=pu:
                   e.tensor_mul(hT(hc)[:, th * 1024:(th + 1) * 1024],
                                siluslot[:, 4 + par, :], pw[pu][:, :]),
                   reads=gr("st", (4 + par) * 1024, (5 + par) * 1024) + [f"pw{pu}"],
                   writes=hT_tok(hc, th * 1024, (th + 1) * 1024))
        if debug and b == 0:
            em("g", lambda e: e.dma_start(out=dbg["d_h"], in_=hTv[0]),
               reads=gr("r2", 0, 16384), dma=True)

    def phase_down_final(b):
        em("q", lambda e: e.dma_start(out=wdres_a,
                                      in_=wdT_s[0:2048, :].rearrange("(hc p) d -> p hc d", p=128)),
           reads=["scrwd"], writes=gr("r1", 0, 16384) + [f"r1c{t}" for t in range(16)], dma=True)
        em("q", lambda e: e.dma_start(out=wdres_b,
                                      in_=wdT_s[2048:3072, :].rearrange("(hc p) d -> p hc d", p=128)),
           reads=["scrwd"], writes=gr("st", 0, 8192), dma=True)
        for tp in range(8):
            taus = (2 * tp, 2 * tp + 1)
            for i, tau in enumerate(taus):
                pwi = 2 * (tp % 2) + i
                for hc in range(24):
                    for eta in range(2):
                        em("t", lambda e, tau=tau, hc=hc, eta=eta, pwi=pwi:
                           e.matmul(pw[pwi][:, eta * 512:(eta + 1) * 512],
                                    hT(hc)[:, tau * 128:(tau + 1) * 128],
                                    wdres(hc)[:, eta * 512:(eta + 1) * 512],
                                    start=(hc == 0), stop=(hc == 23)),
                           reads=hT_tok(hc, tau * 128, (tau + 1) * 128) + wdres_tok(hc),
                           writes=[f"pw{pwi}"], inc=(hc == 23 and eta == 1))
            for i, tau in enumerate(taus):
                pwi = 2 * (tp % 2) + i
                par = tau % 2
                sp = ("spA", "spB")[par]
                op_ = ("spC", "spD")[par]
                em("q", lambda e, b=b, tau=tau, par=par:
                   e.dma_start(out=xstage[par], in_=x_in[b, tau * 128:(tau + 1) * 128, :]),
                   writes=[sp], dma=True)
                em("q", lambda e, tau=tau, par=par:
                   e.dma_start(out=spE[:, par * 1024:(par + 1) * 1024],
                               in_=attn_scr[tau * 128:(tau + 1) * 128, :]),
                   reads=["attn_scr"], writes=[f"spE{par}"], dma=True)
                for eta in range(2):
                    sl = slice(eta * 512, (eta + 1) * 512)
                    em("v", lambda e, pwi=pwi, par=par, sl=sl:
                       e.scalar_tensor_tensor(out=ostage[par][:, sl], in0=pw[pwi][:, sl],
                                              scalar=SC * SC, in1=xstage[par][:, sl],
                                              op0=ALU.mult, op1=ALU.add),
                       reads=[f"pw{pwi}", sp], writes=[op_])
                    em("v", lambda e, par=par, sl=sl:
                       e.tensor_add(ostage[par][:, sl], ostage[par][:, sl],
                                    spE[:, par * 1024:(par + 1) * 1024][:, sl]),
                       reads=[op_, f"spE{par}"], writes=[op_])
                em("sd", lambda e, b=b, tau=tau, par=par:
                   e.dma_start(out=out_d[b, tau * 128:(tau + 1) * 128, :], in_=ostage[par]),
                   reads=[op_], dma=True)

    for b in range(NB):
        if b == 0:
            prep_attn("Wq")
            prefetch_wbuf(b, 0)
            phase_ln(b)
            prep_attn_pend("Wk")
            prep_attn_pend("Wv")
            phase_proj(b, 0)
        else:
            phase_ln_proj(b)
        phase_fft(b, 0)
        if b == 0:
            prep_flush()
        prefetch_wbuf(b, 1)
        if b == 0:
            prep_attn_pend("Wo")
            prep_pending.extend((src, dst, tk, True) for (src, dst, tk) in ffn_chunks)
        phase_proj(b, 1)
        phase_fft(b, 1)
        prefetch_wbuf(b, 2)
        phase_proj(b, 2)
        if b == 0:
            prep_service(99)
            prep_flush()
        phase_middle_pre(b)
        phase_middle_a(b)
        phase_fft(b, 2)
        prefetch_wo(b)
        phase_middle_b(b)
        phase_irfft(b)
        phase_wo(b)
        phase_ffn_gu(b, range(0, 24))
        phase_down_final(b)

    fin_q, fin_g, fin_sd = S.cnt["q"], S.cnt["g"], S.cnt["sd"]
    S.emit("q", lambda e: e.wait_ge(semG, fin_g), inc=False)
    S.emit("q", lambda e: e.wait_ge(semSD, fin_sd), inc=False)
    S.emit("g", lambda e: e.wait_ge(semQ, fin_q), inc=False)

    def run_engine(skey, eng):
        for (ekey, (fn, waits, inc, amt, tag)) in S.streams[skey]:
            for (pe, pv) in waits:
                eng.wait_ge(SEMS[pe], pv)
            if fn is None:
                continue
            inst = fn(eng)
            if inc and inst is not None:
                inst.then_inc(SEMS[ekey], amt)

    @block.sync
    def _(eng):
        run_engine("q", eng)

    @block.tensor
    def _(eng):
        run_engine("t", eng)

    @block.vector
    def _(eng):
        run_engine("v", eng)

    @block.scalar
    def _(eng):
        run_engine("s", eng)

    @block.gpsimd
    def _(eng):
        run_engine("g", eng)

    stack.close()
    return nc, S.nops


from concourse.bass_utils import run_bass_kernel_spmd

_CACHED = {}


def kernel(**inputs):
    x = np.ascontiguousarray(inputs["L0_attn_query"], dtype=np.float32)
    if "prog" not in _CACHED:
        _CACHED["prog"] = build_program(debug=False)[0]
        _CACHED["consts"] = build_consts()
    nc = _CACHED["prog"]
    consts = _CACHED["consts"]
    shard = x.reshape(8, NB, L, D)
    base = dict(
        Wq=np.ascontiguousarray(inputs["L0_attn_Wq"], np.float32),
        Wk=np.ascontiguousarray(inputs["L0_attn_Wk"], np.float32),
        Wv=np.ascontiguousarray(inputs["L0_attn_Wv"], np.float32),
        Wo=np.ascontiguousarray(inputs["L0_attn_Wo"], np.float32),
        bq=np.ascontiguousarray(inputs["L0_attn_bq"], np.float32).reshape(1, D),
        bk=np.ascontiguousarray(inputs["L0_attn_bk"], np.float32).reshape(1, D),
        bv=np.ascontiguousarray(inputs["L0_attn_bv"], np.float32).reshape(1, D),
        bo=np.ascontiguousarray(inputs["L0_attn_bo"], np.float32).reshape(1, D),
        wg=np.ascontiguousarray(np.asarray(inputs["L0_ffn_w_gate"], np.int32).T),
        wu=np.ascontiguousarray(np.asarray(inputs["L0_ffn_w_up"], np.int32).T),
        wd=np.ascontiguousarray(np.asarray(inputs["L0_ffn_w_down"], np.int32).T),
    )
    for k, v in consts.items():
        base[k] = np.ascontiguousarray(v)
    in_maps = [dict(base, x=np.ascontiguousarray(shard[i])) for i in range(8)]
    res = run_bass_kernel_spmd(nc, in_maps, core_ids=list(range(8)))
    out = np.concatenate(
        [np.asarray(res.results[i]["out"], dtype=np.float32) for i in range(8)], axis=0)
    return out.reshape(32, L, D)
